# version 23 (final)
# baseline (speedup 1.0000x reference)
import sys
sys.path.insert(0, "/opt/trn_rl_repo")
from contextlib import ExitStack
import numpy as np
import ml_dtypes
import concourse.bass as bass
import concourse.mybir as mybir
from concourse.bass_utils import run_bass_kernel_spmd

F32 = mybir.dt.float32
BF16 = mybir.dt.bfloat16
I32 = mybir.dt.int32
AF = mybir.ActivationFunctionType
ALU = mybir.AluOpType

NCORES = 8
T = 4096
D = 2048
I_DIM = 8192
NCH = 8
CH = 512
NSUB = 4
KD = 16
KI = 64
EPS = 1e-5


class Sem:
    def __init__(self, handle):
        self.h = handle
        self.n = 0

    def inc(self, amt=1):
        self.n += amt
        return self.n


class Prog:
    ENGINES = ("sync", "tensor", "vector", "scalar", "gpsimd")

    def __init__(self):
        self.streams = {e: [] for e in self.ENGINES}
        self.psum_guard = {}

    def on(self, eng, fn):
        self.streams[eng].append(fn)

    def wait(self, eng, sem, val):
        if val > 0:
            self.on(eng, lambda e, s=sem.h, v=val: e.wait_ge(s, v))

    def group(self, eng, fn, sem, amt=1):
        v = sem.inc(amt)

        def wrapped(e, fn=fn, h=sem.h, amt=amt):
            last = fn(e)
            last.then_inc(h, amt)
        self.on(eng, wrapped)
        return v

    def pe_wait_psum(self, ps):
        g = self.psum_guard.get(id(ps))
        if g is not None:
            sem, v = g
            self.wait("tensor", sem, v)

    def psum_freed(self, ps, sem, val):
        self.psum_guard[id(ps)] = (sem, val)


def build_nc():
    nc = bass.Bass("TRN2", target_bir_lowering=False)

    x_in = nc.dram_tensor("x", [T, D], F32, kind="ExternalInput")
    wbr_in = nc.dram_tensor("wbr", [128, KD * 64], F32, kind="ExternalInput")
    wbs_in = nc.dram_tensor("wbs", [64, 256], F32, kind="ExternalInput")
    wcr_in = nc.dram_tensor("wcr", [128, 2 * 64], F32, kind="ExternalInput")
    wcm_in = nc.dram_tensor("wcm", [64, D], F32, kind="ExternalInput")
    dvec_in = nc.dram_tensor("dvec", [KD, 128], F32, kind="ExternalInput")
    afac_in = nc.dram_tensor("afac", [1, 32], F32, kind="ExternalInput")
    gw_in = nc.dram_tensor("gw", [I_DIM, D], I32, kind="ExternalInput")
    uw_in = nc.dram_tensor("uw", [I_DIM, D], I32, kind="ExternalInput")
    dw_in = nc.dram_tensor("dw", [D, I_DIM], I32, kind="ExternalInput")
    gsc_in = nc.dram_tensor("gsc", [KI, 128], F32, kind="ExternalInput")
    usc_in = nc.dram_tensor("usc", [KI, 128], F32, kind="ExternalInput")
    dsc_in = nc.dram_tensor("dsc", [KD, 128], F32, kind="ExternalInput")
    idb_in = nc.dram_tensor("idb", [128, 128], BF16, kind="ExternalInput")
    idf_in = nc.dram_tensor("idf", [128, 128], F32, kind="ExternalInput")
    out_t = nc.dram_tensor("out", [T, D], F32, kind="ExternalOutput")

    xs_dram = nc.dram_tensor("xs_dram", [T, D], BF16)
    h_dram = nc.dram_tensor("h_dram", [KI, NCH, 128, CH], BF16)
    wdt_dram = nc.dram_tensor("wdt_dram", [KD, 8, 128, 8, 128], BF16)

    P = Prog()

    with ExitStack() as ctx:
        ecm = ctx.enter_context
        sm = {}
        for name in ("d_setup", "d_wd", "d_wdtw",
                     "cs_pe", "cs_se", "cs_dve",
                     "a_pe", "a_se", "a_dve", "a_gp",
                     "p_pe", "p_se", "p_dve",
                     "w_pe", "w_se", "w_dve",
                     "f_pe", "f_se", "f_dve",
                     "ph"):
            sm[name] = Sem(ecm(nc.semaphore(name)))
        smp = {}
        for name in ("d_x", "d_w", "d_wdtr", "d_hr", "d_xsw", "d_xsr",
                     "d_hw", "d_out"):
            smp[name] = [Sem(ecm(nc.semaphore(name + "0"))),
                         Sem(ecm(nc.semaphore(name + "1")))]
        smp["d_wdl"] = [Sem(ecm(nc.semaphore(f"d_wdl{i}"))) for i in range(3)]
        smp["d_wdtr3"] = [Sem(ecm(nc.semaphore(f"d_wdtr3_{i}"))) for i in range(3)]
        smp["d_xsr4"] = [Sem(ecm(nc.semaphore(f"d_xsr4_{i}"))) for i in range(4)]

        eps_t = ecm(nc.sbuf_tensor("eps_t", [128, 1], F32))
        idb = ecm(nc.sbuf_tensor("idb_s", [128, 128], BF16))
        idf = ecm(nc.sbuf_tensor("idf_s", [128, 128], F32))
        scg = ecm(nc.sbuf_tensor("scg", [128, KI], F32))
        scu = ecm(nc.sbuf_tensor("scu", [128, KI], F32))
        scd = ecm(nc.sbuf_tensor("scd", [128, KD], F32))
        bcu = ecm(nc.sbuf_tensor("bcu", [128, KI], F32))
        bcd = ecm(nc.sbuf_tensor("bcd", [128, KD], F32))
        ones_row = ecm(nc.sbuf_tensor("ones_row", [1, 128], F32))
        dummy1 = ecm(nc.sbuf_tensor("dummy1", [128, 1], F32))

        def f(eng):
            last = None
            for dst, src in ((idb, idb_in), (idf, idf_in)):
                last = eng.dma_start(out=dst[:], in_=src[:])
                last.then_inc(sm["d_setup"].h, 16)
            return last
        P.on("sync", lambda e: f(e) and None)
        sm["d_setup"].inc(32)
        v_id_loaded = sm["d_setup"].n

        v_eps = P.group("vector", lambda e: e.memset(eps_t[:], EPS), sm["cs_dve"])
        v_ones = P.group("vector", lambda e: e.memset(ones_row[:], 1.0), sm["cs_dve"])

        setup_scope = ExitStack()
        if True:
            ssc = setup_scope.enter_context
            ps_set = ssc(nc.psum_tensor("ps_set", [128, 512], F32))
            gsc_s = ssc(nc.sbuf_tensor("gsc_s", [KI, 128], F32))
            usc_s = ssc(nc.sbuf_tensor("usc_s", [KI, 128], F32))
            dsc_s = ssc(nc.sbuf_tensor("dsc_s", [KD, 128], F32))
            wbr_f = ssc(nc.sbuf_tensor("wbr_f", [128, KD * 64], F32))
            wbs_f = ssc(nc.sbuf_tensor("wbs_f", [64, 256], F32))
            wcr_f = ssc(nc.sbuf_tensor("wcr_f", [128, 2 * 64], F32))
            wcm_f = ssc(nc.sbuf_tensor("wcm_f", [64, D], F32))
            dvec_s = ssc(nc.sbuf_tensor("dvec_s", [KD, 128], F32))
            afac_s = ssc(nc.sbuf_tensor("afac_s", [1, 32], F32))


            def setup2(eng):
                last = None
                for dst, src in ((gsc_s, gsc_in), (usc_s, usc_in), (dsc_s, dsc_in),
                                 (wbr_f, wbr_in), (wbs_f, wbs_in), (wcr_f, wcr_in),
                                 (wcm_f, wcm_in), (dvec_s, dvec_in), (afac_s, afac_in)):
                    last = eng.dma_start(out=dst[:], in_=src[:])
                    last.then_inc(sm["d_setup"].h, 16)
            P.on("sync", setup2)
            sm["d_setup"].inc(16 * 9)
            v_setup_loaded = sm["d_setup"].n

            wbr_b = ssc(nc.sbuf_tensor("wbr_b", [128, KD, 64], BF16))
            wbs_b = ssc(nc.sbuf_tensor("wbs_b", [64, 256], BF16))
            wcr_b = ssc(nc.sbuf_tensor("wcr_b", [128, 2, 64], BF16))
            wcm_b = ssc(nc.sbuf_tensor("wcm_b", [64, D], BF16))
            dcols = ssc(nc.sbuf_tensor("dcols", [128, KD], F32))
            acols = ssc(nc.sbuf_tensor("acols", [128, 32], F32))
            diag_d = ssc(nc.sbuf_tensor("diag_d", [128, KD, 128], BF16))
            g_cur = ssc(nc.sbuf_tensor("g_cur", [128, 2, 256], F32))
            g_nxt = ssc(nc.sbuf_tensor("g_nxt", [128, 2, 256], F32))
            gtmp = ssc(nc.sbuf_tensor("gtmp", [128, 2, 2, 128], F32))

            P.wait("tensor", sm["d_setup"], v_setup_loaded)
            P.wait("tensor", sm["cs_dve"], v_ones)

            def scale_T(eng):
                eng.matmul(ps_set[:, 0:KI], gsc_s[:], idf[0:KI, 0:KI],
                           start=True, stop=True)
                eng.matmul(ps_set[:, 64:64 + KI], usc_s[:], idf[0:KI, 0:KI],
                           start=False, stop=True, skip_group_check=True)
                eng.matmul(ps_set[:, 128:128 + KD], dsc_s[:], idf[0:KD, 0:KD],
                           start=False, stop=True, skip_group_check=True)
                eng.matmul(ps_set[:, 160:160 + KD], dvec_s[:], idf[0:KD, 0:KD],
                           start=False, stop=True, skip_group_check=True)
                return eng.matmul(ps_set[:, 192:192 + 32], ones_row[:], afac_s[:],
                                  start=False, stop=True, skip_group_check=True)
            v_scT = P.group("tensor", scale_T, sm["cs_pe"])

            P.wait("scalar", sm["cs_pe"], v_scT)

            def scale_ev(eng):
                eng.copy(out=scg[:], in_=ps_set[:, 0:KI])
                eng.copy(out=scu[:], in_=ps_set[:, 64:64 + KI])
                eng.copy(out=scd[:], in_=ps_set[:, 128:128 + KD])
                eng.copy(out=dcols[:], in_=ps_set[:, 160:160 + KD])
                return eng.copy(out=acols[:], in_=ps_set[:, 192:192 + 32])
            v_scev = P.group("scalar", scale_ev, sm["cs_se"])

            P.wait("vector", sm["cs_se"], v_scev)

            def bias_cols(eng):
                eng.tensor_scalar_mul(out=bcu[:], in0=scu[:], scalar1=-128.0)
                return eng.tensor_scalar_mul(out=bcd[:], in0=scd[:], scalar1=-128.0)
            v_bc = P.group("vector", bias_cols, sm["cs_dve"])

            def w_cast(eng):
                eng.tensor_copy(out=wbr_b[:].rearrange("p k r -> p (k r)"), in_=wbr_f[:])
                eng.tensor_copy(out=wbs_b[:], in_=wbs_f[:])
                eng.tensor_copy(out=wcr_b[:].rearrange("p k r -> p (k r)"), in_=wcr_f[:])
                return eng.tensor_copy(out=wcm_b[:], in_=wcm_f[:])
            v_wcast = P.group("vector", w_cast, sm["cs_dve"])

            def mk_diag(eng):
                last = None
                for kk in range(KD):
                    last = eng.tensor_scalar(out=diag_d[:, kk, :], in0=idb[:],
                                             scalar1=dcols[:, kk:kk + 1],
                                             scalar2=None, op0=ALU.mult)
                return last
            v_diag = P.group("vector", mk_diag, sm["cs_dve"])

            def g_init(eng):
                eng.memset(g_cur[:], 0.0)
                eng.memset(g_nxt[:], 0.0)
                eng.tensor_copy(out=g_cur[:, 0, 0:128], in_=idf[:])
                return eng.tensor_copy(out=g_cur[:, 1, 128:256], in_=idf[:])
            v_gi = P.group("vector", g_init, sm["cs_dve"])
            P.wait("vector", sm["cs_dve"], v_gi)

            bufs = [g_cur, g_nxt]
            for lvl in range(8):
                delta = 128 >> lvl
                src, dst = bufs[lvl % 2], bufs[(lvl + 1) % 2]

                def g_mul(eng, src=src, lvl=lvl, delta=delta):
                    last = None
                    for r in range(2):
                        v = src[:, r, :].rearrange("p (h two d) -> p h two d",
                                                   two=2, d=delta)
                        for j in range(2):
                            a1 = acols[:, lvl * 4 + 2 + j: lvl * 4 + 3 + j]
                            last = eng.tensor_scalar(
                                out=gtmp[:, r, j, :].rearrange(
                                    "p (h d) -> p h d", d=delta),
                                in0=v[:, :, 1, :],
                                scalar1=a1, scalar2=None, op0=ALU.mult)
                    return last
                vv = P.group("vector", g_mul, sm["cs_dve"])
                P.wait("vector", sm["cs_dve"], vv)

                def g_add(eng, src=src, dst=dst, lvl=lvl, delta=delta):
                    last = None
                    for r in range(2):
                        v = src[:, r, :].rearrange("p (h two d) -> p h two d",
                                                   two=2, d=delta)
                        o = dst[:, r, :].rearrange("p (h two d) -> p h two d",
                                                   two=2, d=delta)
                        for j in range(2):
                            a0 = acols[:, lvl * 4 + j: lvl * 4 + 1 + j]
                            last = eng.scalar_tensor_tensor(
                                out=o[:, :, j, :],
                                in0=v[:, :, 0, :],
                                scalar=a0,
                                in1=gtmp[:, r, j, :].rearrange(
                                    "p (h d) -> p h d", d=delta),
                                op0=ALU.mult, op1=ALU.add)
                    return last
                vv = P.group("vector", g_add, sm["cs_dve"])
                P.wait("vector", sm["cs_dve"], vv)
            g_fin = bufs[0]
            v_g_done = sm["cs_dve"].n

            with ExitStack() as attn_scope:
                asb = attn_scope.enter_context
                x_buf = [asb(nc.sbuf_tensor(f"x_buf{i}", [128, NSUB, D], BF16))
                         for i in range(2)]
                xn = asb(nc.sbuf_tensor("xn", [128, NSUB, D], BF16))
                xnT = asb(nc.sbuf_tensor("xnT", [128, KD, CH], BF16))
                s1_sb = asb(nc.sbuf_tensor("s1_sb", [64, CH], BF16))
                s2_sb = asb(nc.sbuf_tensor("s2_sb", [128, 2, CH], F32))
                s3_sb = asb(nc.sbuf_tensor("s3_sb", [128, 2, CH], BF16))
                s4_sb = asb(nc.sbuf_tensor("s4_sb", [64, CH], BF16))
                yf = asb(nc.sbuf_tensor("yf", [128, KD, CH], BF16))
                y_tok = [asb(nc.sbuf_tensor(f"y_tok{i}", [128, NSUB, D], BF16))
                         for i in range(2)]
                ln2_t = [asb(nc.sbuf_tensor(f"ln2_t{i}", [128, D], BF16))
                         for i in range(4)]
                xp_t = [asb(nc.sbuf_tensor(f"xp_t{i}", [128, D], BF16))
                        for i in range(2)]
                yr_t = [asb(nc.sbuf_tensor(f"yr_t{i}", [128, D], BF16))
                        for i in range(2)]
                stA = asb(nc.sbuf_tensor("stA", [128, NSUB, 4, 6], F32))
                mvA = [asb(nc.sbuf_tensor(f"mvA{i}", [128, NSUB, 2], F32)) for i in range(2)]
                sqA = [asb(nc.sbuf_tensor(f"sqA{i}", [128, NSUB], F32)) for i in range(2)]
                rsA = [asb(nc.sbuf_tensor(f"rsA{i}", [128, NSUB], F32)) for i in range(2)]
                nmA = [asb(nc.sbuf_tensor(f"nmA{i}", [128, NSUB], F32)) for i in range(2)]
                nmA2 = [asb(nc.sbuf_tensor(f"nmA2_{i}", [128, NSUB], F32)) for i in range(2)]
                stB = asb(nc.sbuf_tensor("stB", [128, NSUB, 4, 6], F32))
                mvB = [asb(nc.sbuf_tensor(f"mvB{i}", [128, NSUB, 2], F32)) for i in range(2)]
                sqB = [asb(nc.sbuf_tensor(f"sqB{i}", [128, NSUB], F32)) for i in range(2)]
                rsB = [asb(nc.sbuf_tensor(f"rsB{i}", [128, NSUB], F32)) for i in range(2)]
                nmB = [asb(nc.sbuf_tensor(f"nmB{i}", [128, NSUB], F32)) for i in range(2)]
                ngB = [asb(nc.sbuf_tensor(f"ngB{i}", [128, NSUB], F32)) for i in range(2)]

                ps_t0 = asb(nc.psum_tensor("ps_t0", [128, 512], F32))
                ps_t1 = asb(nc.psum_tensor("ps_t1", [128, 512], F32))
                ps_s0 = asb(nc.psum_tensor("ps_s0", [128, 512], F32))
                ps_s1 = asb(nc.psum_tensor("ps_s1", [128, 512], F32))
                ps_y0 = asb(nc.psum_tensor("ps_y0", [128, 512], F32))
                ps_y1 = asb(nc.psum_tensor("ps_y1", [128, 512], F32))

                sq1_v = {}
                sq2_v = {}

                def ln_A(in_subs, st, mv, sd):
                    def stats(eng):
                        last = None
                        for s in range(NSUB):
                            for a in range(4):
                                last = eng.bn_stats(st[:, s, a, :],
                                                    in_subs[s][:, a * 512:(a + 1) * 512])
                        return last
                    vv = P.group("vector", stats, sd)
                    P.wait("vector", sd, vv)

                    def aggr(eng):
                        last = None
                        for s in range(NSUB):
                            last = eng.bn_aggr(mv[:, s, :],
                                               st[:, s, :, :].rearrange("p a b -> p (a b)"))
                        return last
                    return P.group("vector", aggr, sd)

                def ln_AB(in_subs, st, mv, sq, sd, ss):
                    def stats(eng):
                        last = None
                        for s in range(NSUB):
                            for a in range(4):
                                last = eng.bn_stats(st[:, s, a, :],
                                                    in_subs[s][:, a * 512:(a + 1) * 512])
                        return last
                    vv = P.group("vector", stats, sd)
                    P.wait("vector", sd, vv)

                    def aggr(eng):
                        last = None
                        for s in range(NSUB):
                            last = eng.bn_aggr(mv[:, s, :],
                                               st[:, s, :, :].rearrange("p a b -> p (a b)"))
                        return last
                    v_ag = P.group("vector", aggr, sd)
                    P.wait("scalar", sd, v_ag)
                    return P.group("scalar", lambda e: e.activation(
                        out=sq[:, 0:NSUB], in_=mv[:, :, 1], func=AF.Sqrt,
                        bias=eps_t[:]), ss)

                def ln_CD(mv, sq, rs, nm, sd, ss, v_sq):
                    P.wait("vector", ss, v_sq)
                    vv = P.group("vector", lambda e: e.reciprocal(rs[:, 0:NSUB], sq[:, 0:NSUB]), sd)
                    P.wait("vector", sd, vv)
                    return P.group("vector", lambda e: e.scalar_tensor_tensor(
                        out=nm[:, 0:NSUB], in0=mv[:, :, 0], scalar=-1.0,
                        in1=rs[:, 0:NSUB], op0=ALU.mult, op1=ALU.mult), sd)

                xload_v = {}
                xcons = {}

                def emit_xload(c):
                    b = c % 2
                    if c >= 2:
                        P.wait("gpsimd", sm["a_dve"], xcons[c - 2])
                    xload_v[c] = P.group("gpsimd", lambda e, c=c, b=b: e.dma_start(
                        out=x_buf[b][:],
                        in_=x_in[c * CH:(c + 1) * CH, :].rearrange(
                            "(a p) d -> p a d", p=128)), smp["d_x"][b], 16)

                emit_xload(0)
                emit_xload(1)

                P.wait("tensor", sm["cs_dve"], v_g_done)
                P.wait("vector", sm["cs_se"], v_scev)
                P.wait("scalar", sm["cs_dve"], v_eps)

                txn_done = {}
                xsw_snap = {}
                mm_done = {}
                ty_done = {}
                ytok_rd = {}
                xpadd_v = {}
                xsw_v = {}
                apply1_v = {}
                apply2_v = {}

                def emit_ln1_AB(c):
                    b = c % 2
                    P.wait("vector", smp["d_x"][b], xload_v[c])
                    if c >= 2:
                        P.wait("vector", sm["a_se"], sq1_v[c - 2])
                    xb = x_buf[b]
                    sq1_v[c] = ln_AB([xb[:, s, :] for s in range(NSUB)],
                                     stA, mvA[b], sqA[b], sm["a_dve"], sm["a_se"])

                def emit_epilogue(c):
                    b = c % 2
                    xb = x_buf[b]
                    P.wait("scalar", sm["a_dve"], aggr2_v[c])
                    sq2_v[c] = P.group("scalar", lambda e, b=b: e.activation(
                        out=sqB[b][:, 0:NSUB], in_=mvB[b][:, :, 1], func=AF.Sqrt,
                        bias=eps_t[:]), sm["a_se"])
                    if c >= 2:
                        P.wait("vector", sm["a_gp"], apply2_v.get(c - 2, 0))
                    v_nm2 = ln_CD(mvB[b], sqB[b], rsB[b], nmB[b],
                                  sm["a_dve"], sm["a_se"], sq2_v[c])
                    P.wait("vector", sm["a_dve"], v_nm2)
                    for s in range(NSUB):
                        oi = c * NSUB + s
                        if oi >= 4:
                            P.wait("vector", sm["a_gp"], xpadd_v[oi - 4])
                        P.group("vector", lambda e, s=s, b=b, xb=xb: e.tensor_scalar(
                            out=ln2_t[s][:], in0=xb[:, s, :],
                            scalar1=nmB[b][:, s:s + 1], scalar2=None,
                            op0=ALU.add), sm["a_dve"])
                    xcons[c] = sm["a_dve"].n
                    v_t_all = sm["a_dve"].n
                    if c + 2 < NCH:
                        emit_xload(c + 2)

                    P.wait("gpsimd", sm["a_dve"], v_t_all)
                    for s in range(NSUB):
                        oi = c * NSUB + s
                        v_yr = P.group("gpsimd", lambda e, s=s, b=b: e.tensor_scalar(
                            out=yr_t[s % 2][:], in0=y_tok[b][:, s, :],
                            scalar1=0.0, scalar2=rsB[b][:, s:s + 1],
                            op0=ALU.add, op1=ALU.mult), sm["a_gp"])
                        P.wait("gpsimd", sm["a_gp"], v_yr)
                        if oi >= 2:
                            P.wait("gpsimd", smp["d_xsw"][oi % 2], xsw_v[oi - 2])
                        v_xp = P.group("gpsimd", lambda e, s=s: e.tensor_tensor(
                            out=xp_t[s % 2][:], in0=yr_t[s % 2][:],
                            in1=ln2_t[s][:], op=ALU.add), sm["a_gp"])
                        xpadd_v[oi] = v_xp
                        P.wait("sync", sm["a_gp"], v_xp)
                        xsw_v[oi] = P.group("sync", lambda e, c=c, s=s: e.dma_start(
                            out=xs_dram[c * CH + s * 128: c * CH + (s + 1) * 128, :],
                            in_=xp_t[s % 2][:]), smp["d_xsw"][oi % 2], 16)
                        xsw_snap[oi] = (smp["d_xsw"][0].n, smp["d_xsw"][1].n)
                    apply2_v[c] = sm["a_gp"].n

                aggr2_v = {}

                def emit_cd1_apply1(c):
                    b = c % 2
                    if c >= 2:
                        P.wait("vector", sm["a_se"], apply1_v.get(c - 2, 0))
                    P.wait("vector", sm["a_se"], sq1_v[c])
                    vv = P.group("vector", lambda e, b=b: e.reciprocal(
                        rsA[b][:, 0:NSUB], sqA[b][:, 0:NSUB]), sm["a_dve"])
                    P.wait("vector", sm["a_dve"], vv)
                    v_ng1 = P.group("vector", lambda e, b=b: e.scalar_tensor_tensor(
                        out=nmA2[b][:, 0:NSUB], in0=mvA[b][:, :, 0], scalar=-1.0,
                        in1=rsA[b][:, 0:NSUB], op0=ALU.mult, op1=ALU.mult),
                        sm["a_dve"])
                    P.wait("scalar", sm["a_dve"], v_ng1)
                    if c > 0:
                        P.wait("scalar", sm["a_pe"], txn_done[c - 1])
                    xb = x_buf[b]

                    def ln1_apply(eng, xb=xb, b=b):
                        last = None
                        for s in range(NSUB):
                            last = eng.activation(out=xn[:, s, :], in_=xb[:, s, :],
                                                  func=AF.Identity,
                                                  scale=rsA[b][:, s:s + 1],
                                                  bias=nmA2[b][:, s:s + 1])
                        return last
                    apply1_v[c] = P.group("scalar", ln1_apply, sm["a_se"])

                emit_ln1_AB(0)
                emit_cd1_apply1(0)

                for c in range(NCH):
                    b = c % 2
                    xb = x_buf[b]

                    P.wait("tensor", sm["a_se"], apply1_v[c])
                    ps_t4 = (ps_t0, ps_t1, ps_y0, ps_y1)
                    for kk in range(KD):
                        ps = ps_t4[kk % 4]
                        P.pe_wait_psum(ps)

                        def t_xn(eng, kk=kk, ps=ps):
                            last = None
                            for s in range(NSUB):
                                last = eng.matmul(ps[:, s * 128:(s + 1) * 128],
                                                  xn[:, s, kk * 128:(kk + 1) * 128],
                                                  idb[:], start=(s == 0),
                                                  stop=(s == NSUB - 1),
                                                  skip_group_check=True)
                            return last
                        v_pe = P.group("tensor", t_xn, sm["a_pe"])
                        ev_eng, ev_sem = (("vector", sm["a_dve"]) if kk % 2 == 0
                                          else ("scalar", sm["a_se"]))
                        P.wait(ev_eng, sm["a_pe"], v_pe)
                        if c > 0:
                            P.wait(ev_eng, sm["a_pe"], mm_done[c - 1])

                        def ev_xn(eng, kk=kk, ps=ps):
                            if hasattr(eng, "tensor_copy"):
                                return eng.tensor_copy(out=xnT[:, kk, :], in_=ps[:])
                            return eng.copy(out=xnT[:, kk, :], in_=ps[:])
                        v_ev = P.group(ev_eng, ev_xn, ev_sem)
                        P.psum_freed(ps, ev_sem, v_ev)
                        if kk % 2 == 0:
                            v_xnT_d = v_ev
                        else:
                            v_xnT_s = v_ev
                    txn_done[c] = sm["a_pe"].n

                    P.wait("tensor", sm["a_dve"], v_xnT_d)
                    P.wait("tensor", sm["a_se"], v_xnT_s)
                    P.pe_wait_psum(ps_s0)

                    def mm_s1(eng):
                        last = None
                        for kk in range(KD):
                            last = eng.matmul(ps_s0[0:64, :], wbr_b[:, kk, :],
                                              xnT[:, kk, :], start=(kk == 0),
                                              stop=(kk == KD - 1))
                        return last
                    v1 = P.group("tensor", mm_s1, sm["a_pe"])
                    P.wait("scalar", sm["a_pe"], v1)
                    v_s1 = P.group("scalar", lambda e: e.copy(out=s1_sb[:], in_=ps_s0[0:64, :]),
                                   sm["a_se"])
                    P.psum_freed(ps_s0, sm["a_se"], v_s1)

                    P.wait("tensor", sm["a_se"], v_s1)
                    P.pe_wait_psum(ps_s1)
                    P.pe_wait_psum(ps_s0)

                    def mm_s2(eng):
                        eng.matmul(ps_s1[:, 0:CH], wbs_b[:, 0:128], s1_sb[:],
                                   start=True, stop=True)
                        return eng.matmul(ps_s0[:, 0:CH], wbs_b[:, 128:256], s1_sb[:],
                                          start=True, stop=True)
                    v2 = P.group("tensor", mm_s2, sm["a_pe"])
                    P.wait("scalar", sm["a_pe"], v2)

                    def ev_s2(eng):
                        eng.copy(out=s2_sb[:, 0, :], in_=ps_s1[:])
                        return eng.copy(out=s2_sb[:, 1, :], in_=ps_s0[:])
                    v_s2 = P.group("scalar", ev_s2, sm["a_se"])
                    P.psum_freed(ps_s1, sm["a_se"], v_s2)
                    P.psum_freed(ps_s0, sm["a_se"], v_s2)

                    P.wait("tensor", sm["a_se"], v_s2)
                    P.pe_wait_psum(ps_s1)
                    P.pe_wait_psum(ps_s0)

                    def mm_s3(eng):
                        last = None
                        for m2 in range(2):
                            ps = (ps_s1, ps_s0)[m2]
                            for k2 in range(2):
                                last = eng.matmul(ps[:, 0:CH],
                                                  g_fin[:, k2, m2 * 128:(m2 + 1) * 128],
                                                  s2_sb[:, k2, :],
                                                  start=(k2 == 0), stop=(k2 == 1))
                        return last
                    v3 = P.group("tensor", mm_s3, sm["a_pe"])
                    P.wait("scalar", sm["a_pe"], v3)

                    def ev_s3(eng):
                        eng.copy(out=s3_sb[:, 0, :], in_=ps_s1[:])
                        return eng.copy(out=s3_sb[:, 1, :], in_=ps_s0[:])
                    v_s3 = P.group("scalar", ev_s3, sm["a_se"])
                    P.psum_freed(ps_s1, sm["a_se"], v_s3)
                    P.psum_freed(ps_s0, sm["a_se"], v_s3)

                    P.wait("tensor", sm["a_se"], v_s3)
                    P.pe_wait_psum(ps_s1)

                    def mm_s4(eng):
                        last = None
                        for k2 in range(2):
                            last = eng.matmul(ps_s1[0:64, :], wcr_b[:, k2, :],
                                              s3_sb[:, k2, :],
                                              start=(k2 == 0), stop=(k2 == 1))
                        return last
                    v4 = P.group("tensor", mm_s4, sm["a_pe"])
                    P.wait("scalar", sm["a_pe"], v4)
                    v_s4 = P.group("scalar", lambda e: e.copy(out=s4_sb[:], in_=ps_s1[0:64, :]),
                                   sm["a_se"])
                    P.psum_freed(ps_s1, sm["a_se"], v_s4)

                    P.wait("tensor", sm["a_se"], v_s4)
                    for kk in range(KD):
                        ps = (ps_y0, ps_y1)[kk % 2]
                        P.pe_wait_psum(ps)

                        def mm_y(eng, kk=kk, ps=ps):
                            eng.matmul(ps[:, 0:CH], wcm_b[:, kk * 128:(kk + 1) * 128],
                                       s4_sb[:], start=True, stop=False)
                            return eng.matmul(ps[:, 0:CH], diag_d[:, kk, :],
                                              xnT[:, kk, :], start=False, stop=True)
                        v_pe = P.group("tensor", mm_y, sm["a_pe"])
                        ev_eng, ev_sem = (("vector", sm["a_dve"]) if kk % 2 == 0
                                          else ("scalar", sm["a_se"]))
                        P.wait(ev_eng, sm["a_pe"], v_pe)
                        if c > 0:
                            P.wait(ev_eng, sm["a_pe"], ty_done[c - 1])

                        def ev_y(eng, kk=kk, ps=ps):
                            if hasattr(eng, "tensor_copy"):
                                return eng.tensor_copy(out=yf[:, kk, :], in_=ps[:, 0:CH])
                            return eng.copy(out=yf[:, kk, :], in_=ps[:, 0:CH])
                        v_ev = P.group(ev_eng, ev_y, ev_sem)
                        P.psum_freed(ps, ev_sem, v_ev)
                        if kk % 2 == 0:
                            v_yf_d = v_ev
                        else:
                            v_yf_s = v_ev
                    mm_done[c] = sm["a_pe"].n



                    if c >= 1:
                        emit_epilogue(c - 1)
                    if c + 1 < NCH:
                        emit_ln1_AB(c + 1)

                    P.wait("tensor", sm["a_dve"], v_yf_d)
                    P.wait("tensor", sm["a_se"], v_yf_s)
                    for s in range(NSUB):
                        for kg in range(4):
                            ps = ps_t4[(s * 4 + kg) % 4]
                            P.pe_wait_psum(ps)

                            def t_y(eng, s=s, kg=kg, ps=ps):
                                last = None
                                for k2 in range(4):
                                    kk = kg * 4 + k2
                                    last = eng.matmul(ps[:, k2 * 128:(k2 + 1) * 128],
                                                      yf[:, kk, s * 128:(s + 1) * 128],
                                                      idb[:], start=(k2 == 0),
                                                      stop=(k2 == 3),
                                                      skip_group_check=True)
                                return last
                            v_pe = P.group("tensor", t_y, sm["a_pe"])
                            ev_eng, ev_sem = (("vector", sm["a_dve"]) if kg % 2 == 0
                                              else ("scalar", sm["a_se"]))
                            P.wait(ev_eng, sm["a_pe"], v_pe)
                            if c >= 2:
                                P.wait(ev_eng, sm["a_dve"], ytok_rd[c - 2])
                                P.wait(ev_eng, sm["a_gp"], apply2_v.get(c - 2, 0))

                            def ev_ty(eng, s=s, kg=kg, ps=ps, b=b):
                                o = y_tok[b][:, s, kg * 512:(kg + 1) * 512]
                                if hasattr(eng, "tensor_copy"):
                                    return eng.tensor_copy(out=o, in_=ps[:])
                                return eng.copy(out=o, in_=ps[:])
                            v_ev = P.group(ev_eng, ev_ty, ev_sem)
                            P.psum_freed(ps, ev_sem, v_ev)
                            if kg % 2 == 0:
                                v_ytok_d = v_ev
                            else:
                                v_ytok_s = v_ev
                    v_ytok = (v_ytok_d, v_ytok_s)
                    ty_done[c] = sm["a_pe"].n



                    if c + 1 < NCH:
                        emit_cd1_apply1(c + 1)

                    P.wait("vector", sm["a_dve"], v_ytok[0])
                    P.wait("vector", sm["a_se"], v_ytok[1])
                    if c >= 2:
                        P.wait("vector", sm["a_se"], sq2_v[c - 2])
                    aggr2_v[c] = ln_A([y_tok[b][:, s, :] for s in range(NSUB)],
                                      stB, mvB[b], sm["a_dve"])
                    ytok_rd[c] = sm["a_dve"].n

                emit_epilogue(NCH - 1)

                P.group("tensor", lambda e: e.nop(), sm["ph"])
                P.group("vector", lambda e: e.engine_nop(), sm["ph"])
                P.group("gpsimd", lambda e: e.tensor_copy(out=dummy1[:], in_=eps_t[:]), sm["ph"])
                v_attn_done = P.group("scalar", lambda e: e.copy(out=dummy1[:], in_=eps_t[:]), sm["ph"])

            setup_scope.close()

            with ExitStack() as s1_scope:
                sb1 = s1_scope.enter_context
                ffn_inT = sb1(nc.sbuf_tensor("ffn_inT", [128, KD, T], BF16))

                ps_w0 = sb1(nc.psum_tensor("ps_w0", [128, 512], F32))
                ps_w1 = sb1(nc.psum_tensor("ps_w1", [128, 512], F32))
                ps_w2 = sb1(nc.psum_tensor("ps_w2", [128, 512], F32))
                ps_w3 = sb1(nc.psum_tensor("ps_w3", [128, 512], F32))
                ps_w = (ps_w0, ps_w1, ps_w2, ps_w3)

                wgb = [sb1(nc.sbuf_tensor(f"wgb{i}", [128, D], BF16)) for i in range(2)]
                wub = [sb1(nc.sbuf_tensor(f"wub{i}", [128, D], BF16)) for i in range(2)]
                wdb = [sb1(nc.sbuf_tensor(f"wdb{i}", [128, 1024], BF16)) for i in range(3)]

                wload_v = {}
                wfree = {}

                def emit_wload(cc):
                    b = cc % 2
                    if cc < 2:
                        P.wait("gpsimd", sm["ph"], v_attn_done)
                    else:
                        P.wait("gpsimd", sm["w_pe"], wgT_made[cc - 2])
                    P.group("gpsimd", lambda e, cc=cc, b=b: e.dma_start(
                        out=wgb[b][:], in_=gw_in[cc * 128:(cc + 1) * 128, :]),
                        smp["d_w"][b], 16)
                    wload_v[cc] = P.group("gpsimd", lambda e, cc=cc, b=b: e.dma_start(
                        out=wub[b][:], in_=uw_in[cc * 128:(cc + 1) * 128, :]),
                        smp["d_w"][b], 16)

                wd_seq = [(m, g) for m in range(KD) for g in range(8)]
                wdload_v = {}
                wd_free = {}

                def emit_wdload(idx):
                    bw = idx % 3
                    if idx < 3:
                        P.wait("gpsimd", sm["ph"], v_attn_done)
                    else:
                        P.wait("gpsimd", sm["w_pe"], wd_T_done[idx - 3])
                    m, g = wd_seq[idx]
                    wdload_v[idx] = P.group("gpsimd", lambda e, m=m, g=g, bw=bw: e.dma_start(
                        out=wdb[bw][:], in_=dw_in[m * 128:(m + 1) * 128,
                                                  g * 1024:(g + 1) * 1024]),
                        smp["d_wdl"][bw], 16)

                emit_wload(0)
                emit_wload(1)
                emit_wdload(0)
                emit_wdload(1)
                emit_wdload(2)

                wgT = [sb1(nc.sbuf_tensor(f"wgT{i}", [128, KD, 128], BF16)) for i in range(2)]
                wuT = [sb1(nc.sbuf_tensor(f"wuT{i}", [128, KD, 128], BF16)) for i in range(2)]
                wdstg = sb1(nc.sbuf_tensor("wdstg", [128, 8, 128], BF16))
                sg_t = sb1(nc.sbuf_tensor("sg_t", [128, CH], BF16))
                h_sb = [sb1(nc.sbuf_tensor(f"h_sb{i}", [128, CH], BF16)) for i in range(2)]

                ps_g0 = sb1(nc.psum_tensor("ps_g0", [128, 512], F32))
                ps_g1 = sb1(nc.psum_tensor("ps_g1", [128, 512], F32))
                ps_u0 = sb1(nc.psum_tensor("ps_u0", [128, 512], F32))
                ps_u1 = sb1(nc.psum_tensor("ps_u1", [128, 512], F32))


                P.wait("vector", sm["cs_se"], v_scev)
                P.wait("scalar", sm["cs_dve"], v_bc)

                wgT_made = {}
                gup_done = {}
                wd_T_done = {}
                wdtw_v = {}
                hw_v = {}
                sg_rd = 0
                deq_v = {}
                wd_deq_v = {}

                def emit_wd_deq(idx):
                    if idx in wd_deq_v:
                        return
                    m, _g = wd_seq[idx]
                    bw = idx % 3
                    P.wait("scalar", smp["d_wdl"][bw], wdload_v[idx])
                    wd_deq_v[idx] = P.group("scalar", lambda e, m=m, bw=bw: e.activation(
                        out=wdb[bw][:], in_=wdb[bw][:], func=AF.Identity,
                        scale=scd[:, m:m + 1], bias=bcd[:, m:m + 1]),
                        sm["w_se"])
                    wd_free[idx] = wd_deq_v[idx]

                def emit_deq(cc):
                    b = cc % 2
                    P.wait("vector", smp["d_w"][b], wload_v[cc])
                    if cc >= 2:
                        P.wait("vector", sm["w_pe"], wgT_made[cc - 2])
                    v_dg = P.group("vector", lambda e, cc=cc, b=b: e.tensor_scalar(
                        out=wgb[b][:], in0=wgb[b][:], scalar1=-128.0,
                        scalar2=scg[:, cc:cc + 1], op0=ALU.add, op1=ALU.mult),
                        sm["w_dve"])

                    P.wait("scalar", smp["d_w"][b], wload_v[cc])
                    if cc >= 2:
                        P.wait("scalar", sm["w_pe"], wgT_made[cc - 2])
                    v_du = P.group("scalar", lambda e, cc=cc, b=b: e.activation(
                        out=wub[b][:], in_=wub[b][:], func=AF.Identity,
                        scale=scu[:, cc:cc + 1], bias=bcu[:, cc:cc + 1]),
                        sm["w_se"])
                    wfree[cc] = (v_dg, v_du)
                    deq_v[cc] = (v_dg, v_du)

                emit_deq(0)

                wT_ev_last = {}
                wTv = {}

                def emit_cc_prep(cc):
                    b = cc % 2
                    v_dg, v_du = deq_v[cc]

                    P.wait("tensor", sm["w_dve"], v_dg)
                    P.wait("tensor", sm["w_se"], v_du)
                    for w_idx, (wsrc, wdst) in enumerate(
                            ((wgb[b], wgT[b]), (wub[b], wuT[b]))):
                        for kg in range(4):
                            gi = w_idx * 4 + kg
                            ps = ps_w[gi % 4]
                            g = P.psum_guard.get(id(ps))
                            if g is not None:
                                P.wait("tensor", g[0], g[1])

                            def t_w(eng, wsrc=wsrc, kg=kg, ps=ps):
                                last = None
                                for k2 in range(4):
                                    kk = kg * 4 + k2
                                    last = eng.matmul(ps[:, k2 * 128:(k2 + 1) * 128],
                                                      wsrc[:, kk * 128:(kk + 1) * 128],
                                                      idb[:], start=(k2 == 0),
                                                      stop=(k2 == 3),
                                                      skip_group_check=True)
                                return last
                            v_pe = P.group("tensor", t_w, sm["w_pe"])
                            ev_eng, ev_sem = (("vector", sm["w_dve"]) if gi % 2 == 0
                                              else ("scalar", sm["w_se"]))
                            P.wait(ev_eng, sm["w_pe"], v_pe)
                            if cc >= 2:
                                P.wait(ev_eng, sm["w_pe"], gup_done.get(cc - 2, 0))

                            def ev_w(eng, wdst=wdst, kg=kg, ps=ps):
                                o = wdst[:].rearrange("p k q -> p (k q)")[
                                    :, kg * 512:(kg + 1) * 512]
                                if hasattr(eng, "tensor_copy"):
                                    return eng.tensor_copy(out=o, in_=ps[:])
                                return eng.copy(out=o, in_=ps[:])
                            v_ev = P.group(ev_eng, ev_w, ev_sem)
                            P.psum_freed(ps, ev_sem, v_ev)
                            wT_ev_last[ev_eng] = v_ev
                    wgT_made[cc] = sm["w_pe"].n

                    for rep in range(2):
                        idx = cc * 2 + rep
                        m, g_ = wd_seq[idx]
                        bw = idx % 3
                        if idx == 0:
                            emit_wd_deq(0)
                            emit_wd_deq(1)
                        if idx + 2 < 128:
                            emit_wd_deq(idx + 2)
                        v_dd = wd_deq_v[idx]

                        P.wait("tensor", sm["w_se"], v_dd)
                        for kg in range(2):
                            ps = ps_w[2 * rep + kg]
                            g2 = P.psum_guard.get(id(ps))
                            if g2 is not None:
                                P.wait("tensor", g2[0], g2[1])

                            def t_d(eng, kg=kg, ps=ps, bw=bw):
                                last = None
                                for k2 in range(4):
                                    kc = kg * 4 + k2
                                    last = eng.matmul(ps[:, k2 * 128:(k2 + 1) * 128],
                                                      wdb[bw][:, kc * 128:(kc + 1) * 128],
                                                      idb[:], start=(k2 == 0),
                                                      stop=(k2 == 3),
                                                      skip_group_check=True)
                                return last
                            v_pe = P.group("tensor", t_d, sm["w_pe"])
                            P.wait("vector", sm["w_pe"], v_pe)
                            if idx >= 1:
                                P.wait("vector", sm["d_wdtw"], wdtw_v[idx - 1])

                            def ev_d(eng, kg=kg, ps=ps):
                                last = None
                                for k2 in range(4):
                                    kc = kg * 4 + k2
                                    last = eng.tensor_copy(out=wdstg[:, kc, :],
                                                           in_=ps[:, k2 * 128:(k2 + 1) * 128])
                                return last
                            v_ev = P.group("vector", ev_d, sm["w_dve"])
                            P.psum_freed(ps, sm["w_dve"], v_ev)
                        wd_T_done[idx] = sm["w_pe"].n
                        v_evd = sm["w_dve"].n

                        P.wait("gpsimd", sm["w_dve"], v_evd)
                        wdtw_v[idx] = P.group("gpsimd", lambda e, m=m, g_=g_: e.dma_start(
                            out=wdt_dram[m, g_, :, :, :], in_=wdstg[:]),
                            sm["d_wdtw"], 16)
                        if idx + 3 < 128:
                            emit_wdload(idx + 3)

                    if cc + 1 < KI:
                        emit_deq(cc + 1)

                    wTv[cc] = (wT_ev_last.get("vector", 0),
                               wT_ev_last.get("scalar", 0))

                sg_rd_box = [0]

                def emit_gu_nb(cc, nb):
                    b = cc % 2
                    if nb == 0:
                        P.wait("tensor", sm["w_dve"], wTv[cc][0])
                        P.wait("tensor", sm["w_se"], wTv[cc][1])
                    if cc == 0:
                        jv = pfx_evac_at[min(4 * nb + 3, NPCH - 1)]
                        P.wait("tensor", sm["p_dve"], jv[0])
                        P.wait("tensor", sm["p_se"], jv[1])
                    psg = (ps_g0, ps_g1)[nb % 2]
                    psu = (ps_u0, ps_u1)[nb % 2]
                    for pp in (psg, psu):
                        g3 = P.psum_guard.get(id(pp))
                        if g3 is not None:
                            P.wait("tensor", g3[0], g3[1])

                    def mm_gu(eng, b=b, nb=nb, psg=psg, psu=psu):
                        for kk in range(KD):
                            eng.matmul(psg[:, 0:CH], wgT[b][:, kk, :],
                                       ffn_inT[:, kk, nb * CH:(nb + 1) * CH],
                                       start=(kk == 0), stop=(kk == KD - 1))
                        last = None
                        for kk in range(KD):
                            last = eng.matmul(psu[:, 0:CH], wuT[b][:, kk, :],
                                              ffn_inT[:, kk, nb * CH:(nb + 1) * CH],
                                              start=(kk == 0), stop=(kk == KD - 1))
                        return last
                    v_gu = P.group("tensor", mm_gu, sm["w_pe"])

                    P.wait("scalar", sm["w_pe"], v_gu)
                    if sg_rd_box[0]:
                        P.wait("scalar", sm["f_dve"], sg_rd_box[0])
                    v_sg = P.group("scalar", lambda e, psg=psg: e.activation(
                        out=sg_t[:], in_=psg[:, 0:CH], func=AF.Silu),
                        sm["f_se"])
                    P.psum_freed(psg, sm["f_se"], v_sg)

                    hidx = cc * NCH + nb
                    hb = hidx % 2
                    P.wait("vector", sm["f_se"], v_sg)
                    if hidx >= 2:
                        P.wait("vector", smp["d_hw"][hidx % 2], hw_v[hidx - 2])
                    v_h = P.group("vector", lambda e, psu=psu, hb=hb: e.tensor_tensor(
                        out=h_sb[hb][:], in0=sg_t[:], in1=psu[:, 0:CH],
                        op=ALU.mult), sm["f_dve"])
                    P.psum_freed(psu, sm["f_dve"], v_h)
                    sg_rd_box[0] = v_h

                    P.wait("gpsimd", sm["f_dve"], v_h)
                    hw_v[hidx] = P.group("gpsimd", lambda e, cc=cc, nb=nb, hb=hb: e.dma_start(
                        out=h_dram[cc, nb, :, :], in_=h_sb[hb][:]),
                        smp["d_hw"][hidx % 2], 16)

                def emit_cc_finish(cc):
                    gup_done[cc] = sm["w_pe"].n
                    if cc + 2 < KI:
                        emit_wload(cc + 2)

                with ExitStack() as pfx_scope:
                    pfx = pfx_scope.enter_context
                    pxp = [pfx(nc.sbuf_tensor(f"pxp{i}", [128, D], BF16))
                           for i in range(4)]
                    pfin = [pfx(nc.sbuf_tensor(f"pfin{i}", [128, D], BF16))
                            for i in range(2)]
                    pst = [pfx(nc.sbuf_tensor(f"pst{i}", [128, 4, 6], F32))
                           for i in range(2)]
                    pmv = [pfx(nc.sbuf_tensor(f"pmv{i}", [128, 2], F32))
                           for i in range(2)]
                    psq = [pfx(nc.sbuf_tensor(f"psq{i}", [128, 1], F32))
                           for i in range(2)]
                    prs = [pfx(nc.sbuf_tensor(f"prs{i}", [128, 1], F32))
                           for i in range(2)]
                    pnm = [pfx(nc.sbuf_tensor(f"pnm{i}", [128, 1], F32))
                           for i in range(2)]

                    NPCH = 32
                    pxl_v = {}
                    pcons = {}
                    pfin_rd = {}
                    a_v = {}
                    b_v = {}
                    c_v = {}
                    d_v = {}

                    def p_load(j):
                        bx = j % 4
                        if j < 4:
                            P.wait("sync", sm["ph"], v_attn_done)
                        else:
                            P.wait("sync", sm["p_se"], pcons[j - 4])
                        s0, s1 = xsw_snap[j]
                        P.wait("sync", smp["d_xsw"][0], s0)
                        P.wait("sync", smp["d_xsw"][1], s1)
                        pxl_v[j] = P.group("sync", lambda e, j=j, bx=bx: e.dma_start(
                            out=pxp[bx][:],
                            in_=xs_dram[j * 128:(j + 1) * 128, :]), smp["d_xsr4"][bx], 16)

                    def p_A(j):
                        b = j % 2
                        bx = j % 4
                        P.wait("vector", smp["d_xsr4"][bx], pxl_v[j])
                        if j == 0:
                            P.wait("vector", sm["ph"], v_attn_done)

                        def stats(eng, b=b, bx=bx):
                            last = None
                            for a in range(4):
                                last = eng.bn_stats(pst[b][:, a, :],
                                                    pxp[bx][:, a * 512:(a + 1) * 512])
                            return last
                        vv = P.group("vector", stats, sm["p_dve"])
                        P.wait("vector", sm["p_dve"], vv)
                        a_v[j] = P.group("vector", lambda e, b=b: e.bn_aggr(
                            pmv[b][:], pst[b][:].rearrange("p a b -> p (a b)")),
                            sm["p_dve"])

                    def p_B(j):
                        b = j % 2
                        P.wait("scalar", sm["p_dve"], a_v[j])
                        if j == 0:
                            P.wait("scalar", sm["ph"], v_attn_done)
                        b_v[j] = P.group("scalar", lambda e, b=b: e.activation(
                            out=psq[b][:], in_=pmv[b][:, 1:2], func=AF.Sqrt,
                            bias=eps_t[:]), sm["p_se"])

                    def p_C(j):
                        b = j % 2
                        P.wait("vector", sm["p_se"], b_v[j])
                        if j > 1:
                            P.wait("vector", sm["p_se"], d_v[j - 2])
                        vv = P.group("vector", lambda e, b=b: e.reciprocal(
                            prs[b][:], psq[b][:]), sm["p_dve"])
                        P.wait("vector", sm["p_dve"], vv)
                        c_v[j] = P.group("vector", lambda e, b=b: e.scalar_tensor_tensor(
                            out=pnm[b][:], in0=pmv[b][:, 0:1], scalar=-1.0,
                            in1=prs[b][:], op0=ALU.mult, op1=ALU.mult),
                            sm["p_dve"])

                    def p_D(j):
                        b = j % 2
                        bx = j % 4
                        P.wait("scalar", sm["p_dve"], c_v[j])
                        if j > 1:
                            P.wait("scalar", sm["p_pe"], pfin_rd[j - 2])
                        d_v[j] = P.group("scalar", lambda e, b=b, bx=bx: e.activation(
                            out=pfin[b][:], in_=pxp[bx][:], func=AF.Identity,
                            scale=prs[b][:], bias=pnm[b][:]), sm["p_se"])
                        pcons[j] = d_v[j]

                    def p_E(j):
                        b = j % 2
                        P.wait("tensor", sm["p_se"], d_v[j])
                        if j == 0:
                            P.wait("tensor", sm["ph"], v_attn_done)
                        for kq in range(4):
                            ps = ps_w[kq % 4]
                            g = P.psum_guard.get(id(ps))
                            if g is not None:
                                P.wait("tensor", g[0], g[1])

                            def t_fin(eng, kq=kq, ps=ps, b=b):
                                last = None
                                for k2 in range(4):
                                    kk = kq * 4 + k2
                                    last = eng.matmul(ps[:, k2 * 128:(k2 + 1) * 128],
                                                      pfin[b][:, kk * 128:(kk + 1) * 128],
                                                      idb[:], start=(k2 == 0),
                                                      stop=(k2 == 3),
                                                      skip_group_check=True)
                                return last
                            v_pe = P.group("tensor", t_fin, sm["p_pe"])
                            ev_eng, ev_sem = (("vector", sm["p_dve"]) if kq % 2 == 0
                                              else ("scalar", sm["p_se"]))
                            P.wait(ev_eng, sm["p_pe"], v_pe)

                            def ev_fin(eng, kq=kq, ps=ps, j=j):
                                o = ffn_inT[:, kq * 4:(kq + 1) * 4,
                                            j * 128:(j + 1) * 128]
                                i_ = ps[:, 0:512].rearrange("p (k q) -> p k q", k=4)
                                if hasattr(eng, "tensor_copy"):
                                    return eng.tensor_copy(out=o, in_=i_)
                                return eng.copy(out=o, in_=i_)
                            v_ev = P.group(ev_eng, ev_fin, ev_sem)
                            P.psum_freed(ps, ev_sem, v_ev)
                        pfin_rd[j] = sm["p_pe"].n

                    pfx_evac_at = {}
                    for jj in range(4):
                        p_load(jj)
                    for j in range(NPCH):
                        p_A(j)
                        p_B(j)
                        if j >= 1:
                            p_C(j - 1)
                            p_D(j - 1)
                            if j + 3 < NPCH:
                                p_load(j + 3)
                        if j >= 2:
                            p_E(j - 2)
                            pfx_evac_at[j - 2] = (sm["p_dve"].n, sm["p_se"].n)
                        if j == 1:
                            emit_cc_prep(0)
                        if j >= 5 and (j - 5) % 4 == 0 and (j - 5) // 4 <= 6:
                            emit_gu_nb(0, (j - 5) // 4)
                    p_E(NPCH - 2)
                    pfx_evac_at[NPCH - 2] = (sm["p_dve"].n, sm["p_se"].n)
                    p_C(NPCH - 1)
                    p_D(NPCH - 1)
                    p_E(NPCH - 1)
                    pfx_evac_at[NPCH - 1] = (sm["p_dve"].n, sm["p_se"].n)
                    v_pfx_dve = sm["p_dve"].n
                    v_pfx_se = sm["p_se"].n
                    v_pfx_pe = sm["p_pe"].n

                emit_gu_nb(0, 7)
                emit_cc_finish(0)
                for cc in range(1, KI):
                    emit_cc_prep(cc)
                    for nb in range(NCH):
                        emit_gu_nb(cc, nb)
                    emit_cc_finish(cc)

                v_s1_hw = (smp["d_hw"][0].n, smp["d_hw"][1].n)
                v_s1_wdtw = sm["d_wdtw"].n
                P.group("tensor", lambda e: e.nop(), sm["ph"])
                P.group("vector", lambda e: e.engine_nop(), sm["ph"])
                v_s1_done = P.group("scalar", lambda e: e.copy(out=dummy1[:], in_=eps_t[:]), sm["ph"])

            with ExitStack() as s2_scope:
                sb2 = s2_scope.enter_context
                h_ts = [sb2(nc.sbuf_tensor(f"h_ts{i}", [128, KI, CH], BF16))
                        for i in range(2)]
                wdt_st = [sb2(nc.sbuf_tensor(f"wdt_st{i}", [128, 4, 8, 128], BF16))
                          for i in range(3)]
                xr_t = [sb2(nc.sbuf_tensor(f"xr_t{i}", [128, D], BF16))
                        for i in range(2)]
                fo_f = [sb2(nc.sbuf_tensor(f"fo_f{i}", [128, CH], BF16))
                        for i in range(2)]
                fo_tok = sb2(nc.sbuf_tensor("fo_tok", [128, NSUB, D], BF16))
                ln4_t = [sb2(nc.sbuf_tensor(f"ln4_t{i}", [128, D], BF16))
                         for i in range(2)]
                o_t = [sb2(nc.sbuf_tensor(f"o_t{i}", [128, D], F32)) for i in range(2)]
                st2 = sb2(nc.sbuf_tensor("st2", [128, NSUB, 4, 6], F32))
                mv2 = sb2(nc.sbuf_tensor("mv2", [128, NSUB, 2], F32))
                sq2 = sb2(nc.sbuf_tensor("sq2", [128, NSUB], F32))
                rs2 = sb2(nc.sbuf_tensor("rs2", [128, NSUB], F32))
                nm2 = sb2(nc.sbuf_tensor("nm2", [128, NSUB], F32))

                ps_d0 = sb2(nc.psum_tensor("ps_d0", [128, 512], F32))
                ps_d1 = sb2(nc.psum_tensor("ps_d1", [128, 512], F32))
                ps_T0 = sb2(nc.psum_tensor("ps_T0", [128, 512], F32))
                ps_T1 = sb2(nc.psum_tensor("ps_T1", [128, 512], F32))

                hload_v = {}
                hts_cons = {}

                def emit_hload(ts):
                    bt = ts % 2
                    if ts < 2:
                        P.wait("gpsimd", smp["d_hw"][0], v_s1_hw[0])
                        P.wait("gpsimd", smp["d_hw"][1], v_s1_hw[1])
                        P.wait("gpsimd", sm["ph"], v_s1_done)
                    else:
                        P.wait("gpsimd", sm["f_pe"], hts_cons[ts - 2])

                    def f(eng, ts=ts, bt=bt):
                        last = None
                        for gg in range(8):
                            last = eng.dma_start(
                                out=h_ts[bt][:, gg * 8:(gg + 1) * 8, :],
                                in_=h_dram[gg * 8:(gg + 1) * 8, ts, :, :].rearrange(
                                    "c p t -> p c t"))
                            last.then_inc(smp["d_hr"][bt].h, 16)
                    P.on("gpsimd", f)
                    hload_v[ts] = smp["d_hr"][bt].inc(16 * 8)

                wdt_v = {}
                wdt_cons = {}

                def emit_wdtload(widx):
                    bm = widx % 3
                    qeng = "sync"
                    ts, rem = divmod(widx, KD * 2)
                    m, half = divmod(rem, 2)
                    if widx < 3:
                        P.wait(qeng, sm["d_wdtw"], v_s1_wdtw)
                        P.wait(qeng, sm["ph"], v_s1_done)
                    else:
                        P.wait(qeng, sm["f_pe"], wdt_cons[widx - 3])
                    wdt_v[widx] = P.group(qeng, lambda e, m=m, half=half, bm=bm: e.dma_start(
                        out=wdt_st[bm][:],
                        in_=wdt_dram[m, half * 4:(half + 1) * 4, :, :, :].rearrange(
                            "g p k h -> p g k h")), smp["d_wdtr3"][bm], 16)

                xr_v = {}
                xr_cons = {}

                def emit_xr(ts, s):
                    oi = ts * NSUB + s
                    if oi == 0:
                        P.wait("scalar", sm["ph"], v_s1_done)
                    if oi >= 2:
                        P.wait("scalar", sm["f_dve"], xr_cons[oi - 2])
                    xr_v[oi] = P.group("scalar", lambda e, ts=ts, s=s: e.dma_start(
                        out=xr_t[s % 2][:],
                        in_=xs_dram[ts * CH + s * 128: ts * CH + (s + 1) * 128, :]),
                        smp["d_xsr"][s % 2], 16)

                emit_hload(0)
                emit_hload(1)
                emit_wdtload(0)
                emit_wdtload(1)
                emit_wdtload(2)

                out_v = {}
                oadd_v = {}
                fo_rd = {}
                fotok_rd = {}
                pending_tdn = [None]

                def emit_tdn(ts, m, psd):
                    idxm = ts * KD + m
                    bf = m % 2
                    P.wait("scalar", sm["f_pe"], fo_rd[idxm])
                    if idxm > 1:
                        P.wait("scalar", sm["f_pe"], tdn_pe_done[idxm - 2])
                    v_fo = P.group("scalar", lambda e, psd=psd, bf=bf: e.copy(
                        out=fo_f[bf][:], in_=psd[:, 0:CH]), sm["f_se"])
                    P.psum_freed(psd, sm["f_se"], v_fo)

                    P.wait("tensor", sm["f_se"], v_fo)
                    for sp in range(2):
                        ps = (ps_T0, ps_T1)[sp]
                        g5 = P.psum_guard.get(id(ps))
                        if g5 is not None:
                            P.wait("tensor", g5[0], g5[1])

                        def t_dn(eng, sp=sp, ps=ps, bf=bf):
                            last = None
                            for s2_ in range(2):
                                s = sp * 2 + s2_
                                last = eng.matmul(ps[:, s2_ * 128:(s2_ + 1) * 128],
                                                  fo_f[bf][:, s * 128:(s + 1) * 128],
                                                  idb[:], start=(s2_ == 0),
                                                  stop=(s2_ == 1),
                                                  skip_group_check=True)
                            return last
                        v_pe = P.group("tensor", t_dn, sm["f_pe"])
                        P.wait("vector", sm["f_pe"], v_pe)
                        if ts > 0:
                            P.wait("vector", sm["f_se"], fotok_rd[ts - 1])

                        def ev_tdn(eng, sp=sp, ps=ps, m=m):
                            last = None
                            for s2_ in range(2):
                                s = sp * 2 + s2_
                                last = eng.tensor_copy(
                                    out=fo_tok[:, s, m * 128:(m + 1) * 128],
                                    in_=ps[:, s2_ * 128:(s2_ + 1) * 128])
                            return last
                        v_ev = P.group("vector", ev_tdn, sm["f_dve"])
                        P.psum_freed(ps, sm["f_dve"], v_ev)
                    tdn_pe_done[idxm] = sm["f_pe"].n

                tdn_pe_done = {}

                for ts in range(NCH):
                    bt = ts % 2
                    P.wait("tensor", smp["d_hr"][ts % 2], hload_v[ts])
                    for m in range(KD):
                        psd = (ps_d0, ps_d1)[m % 2]
                        g4 = P.psum_guard.get(id(psd))
                        if g4 is not None:
                            P.wait("tensor", g4[0], g4[1])
                        for half in range(2):
                            widx = ts * KD * 2 + m * 2 + half
                            bm = widx % 3
                            P.wait("tensor", smp["d_wdtr3"][bm], wdt_v[widx])

                            def mm_dn(eng, bt=bt, bm=bm, psd=psd, half=half):
                                last = None
                                for kcl in range(32):
                                    kc = half * 32 + kcl
                                    last = eng.matmul(psd[:, 0:CH],
                                                      wdt_st[bm][:, kcl // 8, kcl % 8, :],
                                                      h_ts[bt][:, kc, :],
                                                      start=(kc == 0), stop=(kc == KI - 1))
                                return last
                            v_dn = P.group("tensor", mm_dn, sm["f_pe"])
                            wdt_cons[widx] = v_dn
                            if widx + 3 < NCH * KD * 2:
                                emit_wdtload(widx + 3)
                        if m == KD - 1:
                            hts_cons[ts] = v_dn
                        fo_rd[ts * KD + m] = v_dn

                        if pending_tdn[0] is not None:
                            emit_tdn(*pending_tdn[0])
                        pending_tdn[0] = (ts, m, psd)
                    emit_tdn(*pending_tdn[0])
                    pending_tdn[0] = None
                    v_fotok = sm["f_dve"].n

                    fo_subs = [fo_tok[:, s, :] for s in range(NSUB)]
                    P.wait("vector", sm["f_dve"], v_fotok)
                    v_l4 = emit_ln_generic(P, fo_subs, st2, mv2, sq2, rs2, nm2,
                                           sm["f_dve"], sm["f_se"], eps_t)

                    P.wait("scalar", sm["f_dve"], v_l4)
                    for s in range(NSUB):
                        oi = ts * NSUB + s
                        emit_xr(ts, s)
                        if oi >= 2:
                            P.wait("scalar", sm["f_dve"], oadd_v[oi - 2])
                        v_a4 = P.group("scalar", lambda e, s=s: e.activation(
                            out=ln4_t[s % 2][:], in_=fo_tok[:, s, :], func=AF.Identity,
                            scale=rs2[:, s:s + 1], bias=nm2[:, s:s + 1]),
                            sm["f_se"])

                        P.wait("vector", sm["f_se"], v_a4)
                        P.wait("vector", smp["d_xsr"][s % 2], xr_v[oi])
                        if oi >= 2:
                            P.wait("vector", smp["d_out"][oi % 2], out_v[oi - 2])
                        v_oa = P.group("vector", lambda e, s=s: e.tensor_tensor(
                            out=o_t[s % 2][:], in0=xr_t[s % 2][:], in1=ln4_t[s % 2][:],
                            op=ALU.add), sm["f_dve"])
                        oadd_v[oi] = v_oa
                        xr_cons[oi] = v_oa

                        P.wait("gpsimd", sm["f_dve"], v_oa)
                        out_v[oi] = P.group("gpsimd", lambda e, ts=ts, s=s: e.dma_start(
                            out=out_t[ts * CH + s * 128: ts * CH + (s + 1) * 128, :],
                            in_=o_t[s % 2][:]), smp["d_out"][oi % 2], 16)
                    fotok_rd[ts] = sm["f_se"].n
                    if ts + 2 < NCH:
                        emit_hload(ts + 2)

                P.wait("gpsimd", smp["d_out"][0], smp["d_out"][0].n)
                P.wait("gpsimd", smp["d_out"][1], smp["d_out"][1].n)

        with nc.Block() as block:
            @block.sync
            def _(sync):
                for f in P.streams["sync"]:
                    f(sync)

            @block.tensor
            def _(tensor):
                for f in P.streams["tensor"]:
                    f(tensor)

            @block.vector
            def _(vector):
                for f in P.streams["vector"]:
                    f(vector)

            @block.scalar
            def _(scalar):
                for f in P.streams["scalar"]:
                    f(scalar)

            @block.gpsimd
            def _(gpsimd):
                for f in P.streams["gpsimd"]:
                    f(gpsimd)

    return nc


def emit_ln_generic(P, in_subs, st, mv, sq, rs, nm, sd, ss, eps_t):
    nsub = len(in_subs)

    def stats(eng):
        last = None
        for s in range(nsub):
            for a in range(4):
                last = eng.bn_stats(st[:, s, a, :],
                                    in_subs[s][:, a * 512:(a + 1) * 512])
        return last
    vv = P.group("vector", stats, sd)
    P.wait("vector", sd, vv)

    def aggr(eng):
        last = None
        for s in range(nsub):
            last = eng.bn_aggr(mv[:, s, :],
                               st[:, s, :, :].rearrange("p a b -> p (a b)"))
        return last
    v_ag = P.group("vector", aggr, sd)

    P.wait("scalar", sd, v_ag)
    v_sq = P.group("scalar", lambda e: e.activation(
        out=sq[:, 0:nsub], in_=mv[:, :, 1], func=AF.Sqrt, bias=eps_t[:]), ss)

    P.wait("vector", ss, v_sq)
    vv = P.group("vector", lambda e: e.reciprocal(rs[:, 0:nsub], sq[:, 0:nsub]), sd)
    P.wait("vector", sd, vv)
    v_nm = P.group("vector", lambda e: e.scalar_tensor_tensor(
        out=nm[:, 0:nsub], in0=mv[:, :, 0], scalar=-1.0, in1=rs[:, 0:nsub],
        op0=ALU.mult, op1=ALU.mult), sd)
    return v_nm


def kernel(L0_attn_u, L0_attn_A_factors, L0_attn_W_b_rank, L0_attn_W_b_state,
           L0_attn_W_c_rank, L0_attn_W_c_model, L0_attn_D_vec,
           L0_ffn_gate_w, L0_ffn_gate_scale, L0_ffn_up_w, L0_ffn_up_scale,
           L0_ffn_down_w, L0_ffn_down_scale):
    x = np.ascontiguousarray(np.asarray(L0_attn_u), dtype=np.float32).reshape(NCORES * T, D)
    wbr = np.ascontiguousarray(
        np.asarray(L0_attn_W_b_rank, dtype=np.float32).reshape(KD, 128, 64)
        .transpose(1, 0, 2).reshape(128, KD * 64))
    wbs = np.ascontiguousarray(np.asarray(L0_attn_W_b_state, dtype=np.float32))
    wcr = np.ascontiguousarray(
        np.asarray(L0_attn_W_c_rank, dtype=np.float32).reshape(2, 128, 64)
        .transpose(1, 0, 2).reshape(128, 2 * 64))
    wcm = np.ascontiguousarray(np.asarray(L0_attn_W_c_model, dtype=np.float32))
    dvec = np.ascontiguousarray(np.asarray(L0_attn_D_vec, dtype=np.float32).reshape(KD, 128))
    afac = np.ascontiguousarray(np.asarray(L0_attn_A_factors, dtype=np.float32).reshape(1, 32))
    gw = np.ascontiguousarray(np.asarray(L0_ffn_gate_w, dtype=np.int32))
    uw = np.ascontiguousarray(np.asarray(L0_ffn_up_w, dtype=np.int32))
    dw = np.ascontiguousarray(np.asarray(L0_ffn_down_w, dtype=np.int32))
    gsc = np.ascontiguousarray(np.asarray(L0_ffn_gate_scale, dtype=np.float32).reshape(KI, 128))
    usc = np.ascontiguousarray(np.asarray(L0_ffn_up_scale, dtype=np.float32).reshape(KI, 128))
    dsc = np.ascontiguousarray(np.asarray(L0_ffn_down_scale, dtype=np.float32).reshape(KD, 128))
    idb = np.eye(128, dtype=ml_dtypes.bfloat16)
    idf = np.eye(128, dtype=np.float32)

    nc = build_nc()
    in_maps = []
    for i in range(NCORES):
        in_maps.append(dict(
            x=np.ascontiguousarray(x[i * T:(i + 1) * T]),
            wbr=wbr, wbs=wbs, wcr=wcr, wcm=wcm, dvec=dvec, afac=afac,
            gw=gw, uw=uw, dw=dw, gsc=gsc, usc=usc, dsc=dsc,
            idb=idb, idf=idf,
        ))
    res = run_bass_kernel_spmd(nc, in_maps, core_ids=list(range(NCORES)))
    out = np.concatenate([np.asarray(r["out"]) for r in res.results], axis=0)
    return out.reshape(2, 16384, D).astype(np.float32)


# version 24
# speedup vs baseline: 1.0360x; 1.0360x over previous
import sys
sys.path.insert(0, "/opt/trn_rl_repo")
from contextlib import ExitStack
import numpy as np
import ml_dtypes
import concourse.bass as bass
import concourse.mybir as mybir
from concourse.bass_utils import run_bass_kernel_spmd

F32 = mybir.dt.float32
BF16 = mybir.dt.bfloat16
I32 = mybir.dt.int32
AF = mybir.ActivationFunctionType
ALU = mybir.AluOpType

NCORES = 8
T = 4096
D = 2048
I_DIM = 8192
NCH = 8
CH = 512
NSUB = 4
KD = 16
KI = 64
EPS = 1e-5


class Sem:
    def __init__(self, handle):
        self.h = handle
        self.n = 0

    def inc(self, amt=1):
        self.n += amt
        return self.n


class Prog:
    ENGINES = ("sync", "tensor", "vector", "scalar", "gpsimd")

    def __init__(self):
        self.streams = {e: [] for e in self.ENGINES}
        self.psum_guard = {}

    def on(self, eng, fn):
        self.streams[eng].append(fn)

    def wait(self, eng, sem, val):
        if val > 0:
            self.on(eng, lambda e, s=sem.h, v=val: e.wait_ge(s, v))

    def group(self, eng, fn, sem, amt=1):
        v = sem.inc(amt)

        def wrapped(e, fn=fn, h=sem.h, amt=amt):
            last = fn(e)
            last.then_inc(h, amt)
        self.on(eng, wrapped)
        return v

    def pe_wait_psum(self, ps):
        g = self.psum_guard.get(id(ps))
        if g is not None:
            sem, v = g
            self.wait("tensor", sem, v)

    def psum_freed(self, ps, sem, val):
        self.psum_guard[id(ps)] = (sem, val)


def build_nc():
    nc = bass.Bass("TRN2", target_bir_lowering=False)

    x_in = nc.dram_tensor("x", [T, D], F32, kind="ExternalInput")
    wbr_in = nc.dram_tensor("wbr", [128, KD * 64], F32, kind="ExternalInput")
    wbs_in = nc.dram_tensor("wbs", [64, 256], F32, kind="ExternalInput")
    wcr_in = nc.dram_tensor("wcr", [128, 2 * 64], F32, kind="ExternalInput")
    wcm_in = nc.dram_tensor("wcm", [64, D], F32, kind="ExternalInput")
    dvec_in = nc.dram_tensor("dvec", [KD, 128], F32, kind="ExternalInput")
    afac_in = nc.dram_tensor("afac", [1, 32], F32, kind="ExternalInput")
    gw_in = nc.dram_tensor("gw", [I_DIM, D], I32, kind="ExternalInput")
    uw_in = nc.dram_tensor("uw", [I_DIM, D], I32, kind="ExternalInput")
    dw_in = nc.dram_tensor("dw", [D, I_DIM], I32, kind="ExternalInput")
    gsc_in = nc.dram_tensor("gsc", [KI, 128], F32, kind="ExternalInput")
    usc_in = nc.dram_tensor("usc", [KI, 128], F32, kind="ExternalInput")
    dsc_in = nc.dram_tensor("dsc", [KD, 128], F32, kind="ExternalInput")
    idb_in = nc.dram_tensor("idb", [128, 128], BF16, kind="ExternalInput")
    idf_in = nc.dram_tensor("idf", [128, 128], F32, kind="ExternalInput")
    out_t = nc.dram_tensor("out", [T, D], F32, kind="ExternalOutput")

    xs_dram = nc.dram_tensor("xs_dram", [T, D], BF16)
    h_dram = nc.dram_tensor("h_dram", [KI, NCH, 128, CH], BF16)
    wdt_dram = nc.dram_tensor("wdt_dram", [KD, 8, 128, 8, 128], BF16)

    P = Prog()

    with ExitStack() as ctx:
        ecm = ctx.enter_context
        sm = {}
        for name in ("d_setup", "d_wd", "d_wdtw",
                     "cs_pe", "cs_se", "cs_dve",
                     "a_pe", "a_se", "a_dve", "a_gp",
                     "p_pe", "p_se", "p_dve",
                     "w_pe", "w_se", "w_dve",
                     "f_pe", "f_se", "f_dve",
                     "ph"):
            sm[name] = Sem(ecm(nc.semaphore(name)))
        smp = {}
        for name in ("d_x", "d_w", "d_wdtr", "d_hr", "d_xsw", "d_xsr",
                     "d_hw", "d_out"):
            smp[name] = [Sem(ecm(nc.semaphore(name + "0"))),
                         Sem(ecm(nc.semaphore(name + "1")))]
        smp["d_wdl"] = [Sem(ecm(nc.semaphore(f"d_wdl{i}"))) for i in range(3)]
        smp["d_wdtr3"] = [Sem(ecm(nc.semaphore(f"d_wdtr3_{i}"))) for i in range(3)]
        smp["d_xsr4"] = [Sem(ecm(nc.semaphore(f"d_xsr4_{i}"))) for i in range(4)]

        eps_t = ecm(nc.sbuf_tensor("eps_t", [128, 1], F32))
        idb = ecm(nc.sbuf_tensor("idb_s", [128, 128], BF16))
        idf = ecm(nc.sbuf_tensor("idf_s", [128, 128], F32))
        scg = ecm(nc.sbuf_tensor("scg", [128, KI], F32))
        scu = ecm(nc.sbuf_tensor("scu", [128, KI], F32))
        scd = ecm(nc.sbuf_tensor("scd", [128, KD], F32))
        bcu = ecm(nc.sbuf_tensor("bcu", [128, KI], F32))
        bcd = ecm(nc.sbuf_tensor("bcd", [128, KD], F32))
        ones_row = ecm(nc.sbuf_tensor("ones_row", [1, 128], F32))
        dummy1 = ecm(nc.sbuf_tensor("dummy1", [128, 1], F32))

        def f(eng):
            last = None
            for dst, src in ((idb, idb_in), (idf, idf_in)):
                last = eng.dma_start(out=dst[:], in_=src[:])
                last.then_inc(sm["d_setup"].h, 16)
            return last
        P.on("sync", lambda e: f(e) and None)
        sm["d_setup"].inc(32)
        v_id_loaded = sm["d_setup"].n

        v_eps = P.group("vector", lambda e: e.memset(eps_t[:], EPS), sm["cs_dve"])
        v_ones = P.group("vector", lambda e: e.memset(ones_row[:], 1.0), sm["cs_dve"])

        setup_scope = ExitStack()
        if True:
            ssc = setup_scope.enter_context
            ps_set = ssc(nc.psum_tensor("ps_set", [128, 512], F32))
            gsc_s = ssc(nc.sbuf_tensor("gsc_s", [KI, 128], F32))
            usc_s = ssc(nc.sbuf_tensor("usc_s", [KI, 128], F32))
            dsc_s = ssc(nc.sbuf_tensor("dsc_s", [KD, 128], F32))
            wbr_f = ssc(nc.sbuf_tensor("wbr_f", [128, KD * 64], F32))
            wbs_f = ssc(nc.sbuf_tensor("wbs_f", [64, 256], F32))
            wcr_f = ssc(nc.sbuf_tensor("wcr_f", [128, 2 * 64], F32))
            wcm_f = ssc(nc.sbuf_tensor("wcm_f", [64, D], F32))
            dvec_s = ssc(nc.sbuf_tensor("dvec_s", [KD, 128], F32))
            afac_s = ssc(nc.sbuf_tensor("afac_s", [1, 32], F32))


            def setup2(eng):
                last = None
                for dst, src in ((gsc_s, gsc_in), (usc_s, usc_in), (dsc_s, dsc_in),
                                 (wbr_f, wbr_in), (wbs_f, wbs_in), (wcr_f, wcr_in),
                                 (wcm_f, wcm_in), (dvec_s, dvec_in), (afac_s, afac_in)):
                    last = eng.dma_start(out=dst[:], in_=src[:])
                    last.then_inc(sm["d_setup"].h, 16)
            P.on("sync", setup2)
            sm["d_setup"].inc(16 * 9)
            v_setup_loaded = sm["d_setup"].n

            wbr_b = ssc(nc.sbuf_tensor("wbr_b", [128, KD, 64], BF16))
            wbs_b = ssc(nc.sbuf_tensor("wbs_b", [64, 256], BF16))
            wcr_b = ssc(nc.sbuf_tensor("wcr_b", [128, 2, 64], BF16))
            wcm_b = ssc(nc.sbuf_tensor("wcm_b", [64, D], BF16))
            dcols = ssc(nc.sbuf_tensor("dcols", [128, KD], F32))
            acols = ssc(nc.sbuf_tensor("acols", [128, 32], F32))
            diag_d = ssc(nc.sbuf_tensor("diag_d", [128, KD, 128], BF16))
            g_cur = ssc(nc.sbuf_tensor("g_cur", [128, 2, 256], F32))
            g_nxt = ssc(nc.sbuf_tensor("g_nxt", [128, 2, 256], F32))
            gtmp = ssc(nc.sbuf_tensor("gtmp", [128, 2, 2, 128], F32))

            P.wait("tensor", sm["d_setup"], v_setup_loaded)
            P.wait("tensor", sm["cs_dve"], v_ones)

            def scale_T(eng):
                eng.matmul(ps_set[:, 0:KI], gsc_s[:], idf[0:KI, 0:KI],
                           start=True, stop=True)
                eng.matmul(ps_set[:, 64:64 + KI], usc_s[:], idf[0:KI, 0:KI],
                           start=False, stop=True, skip_group_check=True)
                eng.matmul(ps_set[:, 128:128 + KD], dsc_s[:], idf[0:KD, 0:KD],
                           start=False, stop=True, skip_group_check=True)
                eng.matmul(ps_set[:, 160:160 + KD], dvec_s[:], idf[0:KD, 0:KD],
                           start=False, stop=True, skip_group_check=True)
                return eng.matmul(ps_set[:, 192:192 + 32], ones_row[:], afac_s[:],
                                  start=False, stop=True, skip_group_check=True)
            v_scT = P.group("tensor", scale_T, sm["cs_pe"])

            P.wait("scalar", sm["cs_pe"], v_scT)

            def scale_ev(eng):
                eng.copy(out=scg[:], in_=ps_set[:, 0:KI])
                eng.copy(out=scu[:], in_=ps_set[:, 64:64 + KI])
                eng.copy(out=scd[:], in_=ps_set[:, 128:128 + KD])
                eng.copy(out=dcols[:], in_=ps_set[:, 160:160 + KD])
                return eng.copy(out=acols[:], in_=ps_set[:, 192:192 + 32])
            v_scev = P.group("scalar", scale_ev, sm["cs_se"])

            P.wait("vector", sm["cs_se"], v_scev)

            def bias_cols(eng):
                eng.tensor_scalar_mul(out=bcu[:], in0=scu[:], scalar1=-128.0)
                return eng.tensor_scalar_mul(out=bcd[:], in0=scd[:], scalar1=-128.0)
            v_bc = P.group("vector", bias_cols, sm["cs_dve"])

            def w_cast(eng):
                eng.tensor_copy(out=wbr_b[:].rearrange("p k r -> p (k r)"), in_=wbr_f[:])
                eng.tensor_copy(out=wbs_b[:], in_=wbs_f[:])
                eng.tensor_copy(out=wcr_b[:].rearrange("p k r -> p (k r)"), in_=wcr_f[:])
                return eng.tensor_copy(out=wcm_b[:], in_=wcm_f[:])
            v_wcast = P.group("vector", w_cast, sm["cs_dve"])

            def mk_diag(eng):
                last = None
                for kk in range(KD):
                    last = eng.tensor_scalar(out=diag_d[:, kk, :], in0=idb[:],
                                             scalar1=dcols[:, kk:kk + 1],
                                             scalar2=None, op0=ALU.mult)
                return last
            v_diag = P.group("vector", mk_diag, sm["cs_dve"])

            def g_init(eng):
                eng.memset(g_cur[:], 0.0)
                eng.memset(g_nxt[:], 0.0)
                eng.tensor_copy(out=g_cur[:, 0, 0:128], in_=idf[:])
                return eng.tensor_copy(out=g_cur[:, 1, 128:256], in_=idf[:])
            v_gi = P.group("vector", g_init, sm["cs_dve"])
            P.wait("vector", sm["cs_dve"], v_gi)

            bufs = [g_cur, g_nxt]
            for lvl in range(8):
                delta = 128 >> lvl
                src, dst = bufs[lvl % 2], bufs[(lvl + 1) % 2]

                def g_mul(eng, src=src, lvl=lvl, delta=delta):
                    last = None
                    for r in range(2):
                        v = src[:, r, :].rearrange("p (h two d) -> p h two d",
                                                   two=2, d=delta)
                        for j in range(2):
                            a1 = acols[:, lvl * 4 + 2 + j: lvl * 4 + 3 + j]
                            last = eng.tensor_scalar(
                                out=gtmp[:, r, j, :].rearrange(
                                    "p (h d) -> p h d", d=delta),
                                in0=v[:, :, 1, :],
                                scalar1=a1, scalar2=None, op0=ALU.mult)
                    return last
                vv = P.group("vector", g_mul, sm["cs_dve"])
                P.wait("vector", sm["cs_dve"], vv)

                def g_add(eng, src=src, dst=dst, lvl=lvl, delta=delta):
                    last = None
                    for r in range(2):
                        v = src[:, r, :].rearrange("p (h two d) -> p h two d",
                                                   two=2, d=delta)
                        o = dst[:, r, :].rearrange("p (h two d) -> p h two d",
                                                   two=2, d=delta)
                        for j in range(2):
                            a0 = acols[:, lvl * 4 + j: lvl * 4 + 1 + j]
                            last = eng.scalar_tensor_tensor(
                                out=o[:, :, j, :],
                                in0=v[:, :, 0, :],
                                scalar=a0,
                                in1=gtmp[:, r, j, :].rearrange(
                                    "p (h d) -> p h d", d=delta),
                                op0=ALU.mult, op1=ALU.add)
                    return last
                vv = P.group("vector", g_add, sm["cs_dve"])
                P.wait("vector", sm["cs_dve"], vv)
            g_fin = bufs[0]
            v_g_done = sm["cs_dve"].n

            with ExitStack() as attn_scope:
                asb = attn_scope.enter_context
                x_buf = [asb(nc.sbuf_tensor(f"x_buf{i}", [128, NSUB, D], BF16))
                         for i in range(2)]
                xn = asb(nc.sbuf_tensor("xn", [128, NSUB, D], BF16))
                xnT = asb(nc.sbuf_tensor("xnT", [128, KD, CH], BF16))
                s1_sb = asb(nc.sbuf_tensor("s1_sb", [64, CH], BF16))
                s2_sb = asb(nc.sbuf_tensor("s2_sb", [128, 2, CH], F32))
                s3_sb = asb(nc.sbuf_tensor("s3_sb", [128, 2, CH], BF16))
                s4_sb = asb(nc.sbuf_tensor("s4_sb", [64, CH], BF16))
                yf = asb(nc.sbuf_tensor("yf", [128, KD, CH], BF16))
                y_tok = [asb(nc.sbuf_tensor(f"y_tok{i}", [128, NSUB, D], BF16))
                         for i in range(2)]
                ln2_t = [asb(nc.sbuf_tensor(f"ln2_t{i}", [128, D], BF16))
                         for i in range(4)]
                xp_t = [asb(nc.sbuf_tensor(f"xp_t{i}", [128, D], BF16))
                        for i in range(2)]
                yr_t = [asb(nc.sbuf_tensor(f"yr_t{i}", [128, D], BF16))
                        for i in range(2)]
                stA = asb(nc.sbuf_tensor("stA", [128, NSUB, 4, 6], F32))
                mvA = [asb(nc.sbuf_tensor(f"mvA{i}", [128, NSUB, 2], F32)) for i in range(2)]
                sqA = [asb(nc.sbuf_tensor(f"sqA{i}", [128, NSUB], F32)) for i in range(2)]
                rsA = [asb(nc.sbuf_tensor(f"rsA{i}", [128, NSUB], F32)) for i in range(2)]
                nmA = [asb(nc.sbuf_tensor(f"nmA{i}", [128, NSUB], F32)) for i in range(2)]
                nmA2 = [asb(nc.sbuf_tensor(f"nmA2_{i}", [128, NSUB], F32)) for i in range(2)]
                stB = asb(nc.sbuf_tensor("stB", [128, NSUB, 4, 6], F32))
                mvB = [asb(nc.sbuf_tensor(f"mvB{i}", [128, NSUB, 2], F32)) for i in range(2)]
                sqB = [asb(nc.sbuf_tensor(f"sqB{i}", [128, NSUB], F32)) for i in range(2)]
                rsB = [asb(nc.sbuf_tensor(f"rsB{i}", [128, NSUB], F32)) for i in range(2)]
                nmB = [asb(nc.sbuf_tensor(f"nmB{i}", [128, NSUB], F32)) for i in range(2)]
                ngB = [asb(nc.sbuf_tensor(f"ngB{i}", [128, NSUB], F32)) for i in range(2)]

                ps_t0 = asb(nc.psum_tensor("ps_t0", [128, 512], F32))
                ps_t1 = asb(nc.psum_tensor("ps_t1", [128, 512], F32))
                ps_s0 = asb(nc.psum_tensor("ps_s0", [128, 512], F32))
                ps_s1 = asb(nc.psum_tensor("ps_s1", [128, 512], F32))
                ps_y0 = asb(nc.psum_tensor("ps_y0", [128, 512], F32))
                ps_y1 = asb(nc.psum_tensor("ps_y1", [128, 512], F32))

                sq1_v = {}
                sq2_v = {}

                def ln_A(in_subs, st, mv, sd):
                    def stats(eng):
                        last = None
                        for s in range(NSUB):
                            for a in range(4):
                                last = eng.bn_stats(st[:, s, a, :],
                                                    in_subs[s][:, a * 512:(a + 1) * 512])
                        return last
                    vv = P.group("vector", stats, sd)
                    P.wait("vector", sd, vv)

                    def aggr(eng):
                        last = None
                        for s in range(NSUB):
                            last = eng.bn_aggr(mv[:, s, :],
                                               st[:, s, :, :].rearrange("p a b -> p (a b)"))
                        return last
                    return P.group("vector", aggr, sd)

                def ln_AB(in_subs, st, mv, sq, sd, ss):
                    def stats(eng):
                        last = None
                        for s in range(NSUB):
                            for a in range(4):
                                last = eng.bn_stats(st[:, s, a, :],
                                                    in_subs[s][:, a * 512:(a + 1) * 512])
                        return last
                    vv = P.group("vector", stats, sd)
                    P.wait("vector", sd, vv)

                    def aggr(eng):
                        last = None
                        for s in range(NSUB):
                            last = eng.bn_aggr(mv[:, s, :],
                                               st[:, s, :, :].rearrange("p a b -> p (a b)"))
                        return last
                    v_ag = P.group("vector", aggr, sd)
                    P.wait("scalar", sd, v_ag)
                    return P.group("scalar", lambda e: e.activation(
                        out=sq[:, 0:NSUB], in_=mv[:, :, 1], func=AF.Sqrt,
                        bias=eps_t[:]), ss)

                def ln_CD(mv, sq, rs, nm, sd, ss, v_sq):
                    P.wait("vector", ss, v_sq)
                    vv = P.group("vector", lambda e: e.reciprocal(rs[:, 0:NSUB], sq[:, 0:NSUB]), sd)
                    P.wait("vector", sd, vv)
                    return P.group("vector", lambda e: e.scalar_tensor_tensor(
                        out=nm[:, 0:NSUB], in0=mv[:, :, 0], scalar=-1.0,
                        in1=rs[:, 0:NSUB], op0=ALU.mult, op1=ALU.mult), sd)

                xload_v = {}
                xcons = {}

                def emit_xload(c):
                    b = c % 2
                    if c >= 2:
                        P.wait("gpsimd", sm["a_dve"], xcons[c - 2])
                    xload_v[c] = P.group("gpsimd", lambda e, c=c, b=b: e.dma_start(
                        out=x_buf[b][:],
                        in_=x_in[c * CH:(c + 1) * CH, :].rearrange(
                            "(a p) d -> p a d", p=128)), smp["d_x"][b], 16)

                emit_xload(0)
                emit_xload(1)

                P.wait("tensor", sm["cs_dve"], v_g_done)
                P.wait("vector", sm["cs_se"], v_scev)
                P.wait("scalar", sm["cs_dve"], v_eps)

                txn_done = {}
                xsw_snap = {}
                mm_done = {}
                ty_done = {}
                ytok_rd = {}
                xpadd_v = {}
                xsw_v = {}
                apply1_v = {}
                apply2_v = {}

                def emit_ln1_AB(c):
                    b = c % 2
                    P.wait("vector", smp["d_x"][b], xload_v[c])
                    if c >= 2:
                        P.wait("vector", sm["a_se"], sq1_v[c - 2])
                    xb = x_buf[b]
                    sq1_v[c] = ln_AB([xb[:, s, :] for s in range(NSUB)],
                                     stA, mvA[b], sqA[b], sm["a_dve"], sm["a_se"])

                def emit_epilogue(c):
                    b = c % 2
                    xb = x_buf[b]
                    P.wait("scalar", sm["a_dve"], aggr2_v[c])
                    sq2_v[c] = P.group("scalar", lambda e, b=b: e.activation(
                        out=sqB[b][:, 0:NSUB], in_=mvB[b][:, :, 1], func=AF.Sqrt,
                        bias=eps_t[:]), sm["a_se"])
                    if c >= 2:
                        P.wait("vector", sm["a_gp"], apply2_v.get(c - 2, 0))
                    v_nm2 = ln_CD(mvB[b], sqB[b], rsB[b], nmB[b],
                                  sm["a_dve"], sm["a_se"], sq2_v[c])
                    P.wait("vector", sm["a_dve"], v_nm2)
                    for s in range(NSUB):
                        oi = c * NSUB + s
                        if oi >= 4:
                            P.wait("vector", sm["a_gp"], xpadd_v[oi - 4])
                        P.group("vector", lambda e, s=s, b=b, xb=xb: e.tensor_scalar(
                            out=ln2_t[s][:], in0=xb[:, s, :],
                            scalar1=nmB[b][:, s:s + 1], scalar2=None,
                            op0=ALU.add), sm["a_dve"])
                    xcons[c] = sm["a_dve"].n
                    v_t_all = sm["a_dve"].n
                    if c + 2 < NCH:
                        emit_xload(c + 2)

                    P.wait("gpsimd", sm["a_dve"], v_t_all)
                    for s in range(NSUB):
                        oi = c * NSUB + s
                        v_yr = P.group("gpsimd", lambda e, s=s, b=b: e.tensor_scalar(
                            out=yr_t[s % 2][:], in0=y_tok[b][:, s, :],
                            scalar1=0.0, scalar2=rsB[b][:, s:s + 1],
                            op0=ALU.add, op1=ALU.mult), sm["a_gp"])
                        P.wait("gpsimd", sm["a_gp"], v_yr)
                        if oi >= 2:
                            P.wait("gpsimd", smp["d_xsw"][oi % 2], xsw_v[oi - 2])
                        v_xp = P.group("gpsimd", lambda e, s=s: e.tensor_tensor(
                            out=xp_t[s % 2][:], in0=yr_t[s % 2][:],
                            in1=ln2_t[s][:], op=ALU.add), sm["a_gp"])
                        xpadd_v[oi] = v_xp
                        P.wait("sync", sm["a_gp"], v_xp)
                        xsw_v[oi] = P.group("sync", lambda e, c=c, s=s: e.dma_start(
                            out=xs_dram[c * CH + s * 128: c * CH + (s + 1) * 128, :],
                            in_=xp_t[s % 2][:]), smp["d_xsw"][oi % 2], 16)
                        xsw_snap[oi] = (smp["d_xsw"][0].n, smp["d_xsw"][1].n)
                    apply2_v[c] = sm["a_gp"].n

                aggr2_v = {}

                def emit_cd1_apply1(c):
                    b = c % 2
                    if c >= 2:
                        P.wait("vector", sm["a_se"], apply1_v.get(c - 2, 0))
                    P.wait("vector", sm["a_se"], sq1_v[c])
                    vv = P.group("vector", lambda e, b=b: e.reciprocal(
                        rsA[b][:, 0:NSUB], sqA[b][:, 0:NSUB]), sm["a_dve"])
                    P.wait("vector", sm["a_dve"], vv)
                    v_ng1 = P.group("vector", lambda e, b=b: e.scalar_tensor_tensor(
                        out=nmA2[b][:, 0:NSUB], in0=mvA[b][:, :, 0], scalar=-1.0,
                        in1=rsA[b][:, 0:NSUB], op0=ALU.mult, op1=ALU.mult),
                        sm["a_dve"])
                    P.wait("scalar", sm["a_dve"], v_ng1)
                    if c > 0:
                        P.wait("scalar", sm["a_pe"], txn_done[c - 1])
                    xb = x_buf[b]

                    def ln1_apply(eng, xb=xb, b=b):
                        last = None
                        for s in range(NSUB):
                            last = eng.activation(out=xn[:, s, :], in_=xb[:, s, :],
                                                  func=AF.Identity,
                                                  scale=rsA[b][:, s:s + 1],
                                                  bias=nmA2[b][:, s:s + 1])
                        return last
                    apply1_v[c] = P.group("scalar", ln1_apply, sm["a_se"])

                emit_ln1_AB(0)
                emit_cd1_apply1(0)

                for c in range(NCH):
                    b = c % 2
                    xb = x_buf[b]

                    P.wait("tensor", sm["a_se"], apply1_v[c])
                    ps_t4 = (ps_t0, ps_t1, ps_y0, ps_y1)
                    for kk in range(KD):
                        ps = ps_t4[kk % 4]
                        P.pe_wait_psum(ps)

                        def t_xn(eng, kk=kk, ps=ps):
                            last = None
                            for s in range(NSUB):
                                last = eng.matmul(ps[:, s * 128:(s + 1) * 128],
                                                  xn[:, s, kk * 128:(kk + 1) * 128],
                                                  idb[:], start=(s == 0),
                                                  stop=(s == NSUB - 1),
                                                  skip_group_check=True)
                            return last
                        v_pe = P.group("tensor", t_xn, sm["a_pe"])
                        ev_eng, ev_sem = (("vector", sm["a_dve"]) if kk % 2 == 0
                                          else ("scalar", sm["a_se"]))
                        P.wait(ev_eng, sm["a_pe"], v_pe)
                        if c > 0:
                            P.wait(ev_eng, sm["a_pe"], mm_done[c - 1])

                        def ev_xn(eng, kk=kk, ps=ps):
                            if hasattr(eng, "tensor_copy"):
                                return eng.tensor_copy(out=xnT[:, kk, :], in_=ps[:])
                            return eng.copy(out=xnT[:, kk, :], in_=ps[:])
                        v_ev = P.group(ev_eng, ev_xn, ev_sem)
                        P.psum_freed(ps, ev_sem, v_ev)
                        if kk % 2 == 0:
                            v_xnT_d = v_ev
                        else:
                            v_xnT_s = v_ev
                    txn_done[c] = sm["a_pe"].n

                    P.wait("tensor", sm["a_dve"], v_xnT_d)
                    P.wait("tensor", sm["a_se"], v_xnT_s)
                    P.pe_wait_psum(ps_s0)

                    def mm_s1(eng):
                        last = None
                        for kk in range(KD):
                            last = eng.matmul(ps_s0[0:64, :], wbr_b[:, kk, :],
                                              xnT[:, kk, :], start=(kk == 0),
                                              stop=(kk == KD - 1))
                        return last
                    v1 = P.group("tensor", mm_s1, sm["a_pe"])
                    P.wait("scalar", sm["a_pe"], v1)
                    v_s1 = P.group("scalar", lambda e: e.copy(out=s1_sb[:], in_=ps_s0[0:64, :]),
                                   sm["a_se"])
                    P.psum_freed(ps_s0, sm["a_se"], v_s1)

                    P.wait("tensor", sm["a_se"], v_s1)
                    P.pe_wait_psum(ps_s1)
                    P.pe_wait_psum(ps_s0)

                    def mm_s2(eng):
                        eng.matmul(ps_s1[:, 0:CH], wbs_b[:, 0:128], s1_sb[:],
                                   start=True, stop=True)
                        return eng.matmul(ps_s0[:, 0:CH], wbs_b[:, 128:256], s1_sb[:],
                                          start=True, stop=True)
                    v2 = P.group("tensor", mm_s2, sm["a_pe"])
                    P.wait("scalar", sm["a_pe"], v2)

                    def ev_s2(eng):
                        eng.copy(out=s2_sb[:, 0, :], in_=ps_s1[:])
                        return eng.copy(out=s2_sb[:, 1, :], in_=ps_s0[:])
                    v_s2 = P.group("scalar", ev_s2, sm["a_se"])
                    P.psum_freed(ps_s1, sm["a_se"], v_s2)
                    P.psum_freed(ps_s0, sm["a_se"], v_s2)

                    P.wait("tensor", sm["a_se"], v_s2)
                    P.pe_wait_psum(ps_s1)
                    P.pe_wait_psum(ps_s0)

                    def mm_s3(eng):
                        last = None
                        for m2 in range(2):
                            ps = (ps_s1, ps_s0)[m2]
                            for k2 in range(2):
                                last = eng.matmul(ps[:, 0:CH],
                                                  g_fin[:, k2, m2 * 128:(m2 + 1) * 128],
                                                  s2_sb[:, k2, :],
                                                  start=(k2 == 0), stop=(k2 == 1))
                        return last
                    v3 = P.group("tensor", mm_s3, sm["a_pe"])
                    P.wait("scalar", sm["a_pe"], v3)

                    def ev_s3(eng):
                        eng.copy(out=s3_sb[:, 0, :], in_=ps_s1[:])
                        return eng.copy(out=s3_sb[:, 1, :], in_=ps_s0[:])
                    v_s3 = P.group("scalar", ev_s3, sm["a_se"])
                    P.psum_freed(ps_s1, sm["a_se"], v_s3)
                    P.psum_freed(ps_s0, sm["a_se"], v_s3)

                    P.wait("tensor", sm["a_se"], v_s3)
                    P.pe_wait_psum(ps_s1)

                    def mm_s4(eng):
                        last = None
                        for k2 in range(2):
                            last = eng.matmul(ps_s1[0:64, :], wcr_b[:, k2, :],
                                              s3_sb[:, k2, :],
                                              start=(k2 == 0), stop=(k2 == 1))
                        return last
                    v4 = P.group("tensor", mm_s4, sm["a_pe"])
                    P.wait("scalar", sm["a_pe"], v4)
                    v_s4 = P.group("scalar", lambda e: e.copy(out=s4_sb[:], in_=ps_s1[0:64, :]),
                                   sm["a_se"])
                    P.psum_freed(ps_s1, sm["a_se"], v_s4)

                    P.wait("tensor", sm["a_se"], v_s4)
                    for kk in range(KD):
                        ps = (ps_y0, ps_y1)[kk % 2]
                        P.pe_wait_psum(ps)

                        def mm_y(eng, kk=kk, ps=ps):
                            eng.matmul(ps[:, 0:CH], wcm_b[:, kk * 128:(kk + 1) * 128],
                                       s4_sb[:], start=True, stop=False)
                            return eng.matmul(ps[:, 0:CH], diag_d[:, kk, :],
                                              xnT[:, kk, :], start=False, stop=True)
                        v_pe = P.group("tensor", mm_y, sm["a_pe"])
                        ev_eng, ev_sem = (("vector", sm["a_dve"]) if kk % 2 == 0
                                          else ("scalar", sm["a_se"]))
                        P.wait(ev_eng, sm["a_pe"], v_pe)
                        if c > 0:
                            P.wait(ev_eng, sm["a_pe"], ty_done[c - 1])

                        def ev_y(eng, kk=kk, ps=ps):
                            if hasattr(eng, "tensor_copy"):
                                return eng.tensor_copy(out=yf[:, kk, :], in_=ps[:, 0:CH])
                            return eng.copy(out=yf[:, kk, :], in_=ps[:, 0:CH])
                        v_ev = P.group(ev_eng, ev_y, ev_sem)
                        P.psum_freed(ps, ev_sem, v_ev)
                        if kk % 2 == 0:
                            v_yf_d = v_ev
                        else:
                            v_yf_s = v_ev
                    mm_done[c] = sm["a_pe"].n



                    if c >= 1:
                        emit_epilogue(c - 1)
                    if c + 1 < NCH:
                        emit_ln1_AB(c + 1)

                    P.wait("tensor", sm["a_dve"], v_yf_d)
                    P.wait("tensor", sm["a_se"], v_yf_s)
                    for s in range(NSUB):
                        for kg in range(4):
                            ps = ps_t4[(s * 4 + kg) % 4]
                            P.pe_wait_psum(ps)

                            def t_y(eng, s=s, kg=kg, ps=ps):
                                last = None
                                for k2 in range(4):
                                    kk = kg * 4 + k2
                                    last = eng.matmul(ps[:, k2 * 128:(k2 + 1) * 128],
                                                      yf[:, kk, s * 128:(s + 1) * 128],
                                                      idb[:], start=(k2 == 0),
                                                      stop=(k2 == 3),
                                                      skip_group_check=True)
                                return last
                            v_pe = P.group("tensor", t_y, sm["a_pe"])
                            ev_eng, ev_sem = (("vector", sm["a_dve"]) if kg % 2 == 0
                                              else ("scalar", sm["a_se"]))
                            P.wait(ev_eng, sm["a_pe"], v_pe)
                            if c >= 2:
                                P.wait(ev_eng, sm["a_dve"], ytok_rd[c - 2])
                                P.wait(ev_eng, sm["a_gp"], apply2_v.get(c - 2, 0))

                            def ev_ty(eng, s=s, kg=kg, ps=ps, b=b):
                                o = y_tok[b][:, s, kg * 512:(kg + 1) * 512]
                                if hasattr(eng, "tensor_copy"):
                                    return eng.tensor_copy(out=o, in_=ps[:])
                                return eng.copy(out=o, in_=ps[:])
                            v_ev = P.group(ev_eng, ev_ty, ev_sem)
                            P.psum_freed(ps, ev_sem, v_ev)
                            if kg % 2 == 0:
                                v_ytok_d = v_ev
                            else:
                                v_ytok_s = v_ev
                    v_ytok = (v_ytok_d, v_ytok_s)
                    ty_done[c] = sm["a_pe"].n



                    if c + 1 < NCH:
                        emit_cd1_apply1(c + 1)

                    P.wait("vector", sm["a_dve"], v_ytok[0])
                    P.wait("vector", sm["a_se"], v_ytok[1])
                    if c >= 2:
                        P.wait("vector", sm["a_se"], sq2_v[c - 2])
                    aggr2_v[c] = ln_A([y_tok[b][:, s, :] for s in range(NSUB)],
                                      stB, mvB[b], sm["a_dve"])
                    ytok_rd[c] = sm["a_dve"].n

                emit_epilogue(NCH - 1)

                P.group("tensor", lambda e: e.nop(), sm["ph"])
                P.group("vector", lambda e: e.engine_nop(), sm["ph"])
                P.group("gpsimd", lambda e: e.tensor_copy(out=dummy1[:], in_=eps_t[:]), sm["ph"])
                v_attn_done = P.group("scalar", lambda e: e.copy(out=dummy1[:], in_=eps_t[:]), sm["ph"])

            setup_scope.close()

            with ExitStack() as s1_scope:
                sb1 = s1_scope.enter_context
                ffn_inT = sb1(nc.sbuf_tensor("ffn_inT", [128, KD, T], BF16))

                ps_w0 = sb1(nc.psum_tensor("ps_w0", [128, 512], F32))
                ps_w1 = sb1(nc.psum_tensor("ps_w1", [128, 512], F32))
                ps_w2 = sb1(nc.psum_tensor("ps_w2", [128, 512], F32))
                ps_w3 = sb1(nc.psum_tensor("ps_w3", [128, 512], F32))
                ps_w = (ps_w0, ps_w1, ps_w2, ps_w3)

                wgb = [sb1(nc.sbuf_tensor(f"wgb{i}", [128, D], BF16)) for i in range(2)]
                wub = [sb1(nc.sbuf_tensor(f"wub{i}", [128, D], BF16)) for i in range(2)]
                wdb = [sb1(nc.sbuf_tensor(f"wdb{i}", [128, 1024], BF16)) for i in range(3)]

                wload_v = {}
                wfree = {}

                def emit_wload(cc):
                    if cc in wload_v:
                        return
                    b = cc % 2
                    if cc < 2:
                        P.wait("gpsimd", sm["ph"], v_attn_done)
                    else:
                        P.wait("gpsimd", sm["w_pe"], wgT_made[cc - 2])
                    P.group("gpsimd", lambda e, cc=cc, b=b: e.dma_start(
                        out=wgb[b][:], in_=gw_in[cc * 128:(cc + 1) * 128, :]),
                        smp["d_w"][b], 16)
                    wload_v[cc] = P.group("gpsimd", lambda e, cc=cc, b=b: e.dma_start(
                        out=wub[b][:], in_=uw_in[cc * 128:(cc + 1) * 128, :]),
                        smp["d_w"][b], 16)

                wd_seq = [(m, g) for m in range(KD) for g in range(8)]
                wdload_v = {}
                wd_free = {}

                def emit_wdload(idx):
                    bw = idx % 3
                    if idx < 3:
                        P.wait("gpsimd", sm["ph"], v_attn_done)
                    else:
                        P.wait("gpsimd", sm["w_pe"], wd_T_done[idx - 3])
                    m, g = wd_seq[idx]
                    wdload_v[idx] = P.group("gpsimd", lambda e, m=m, g=g, bw=bw: e.dma_start(
                        out=wdb[bw][:], in_=dw_in[m * 128:(m + 1) * 128,
                                                  g * 1024:(g + 1) * 1024]),
                        smp["d_wdl"][bw], 16)

                emit_wload(0)
                emit_wload(1)
                emit_wdload(0)
                emit_wdload(1)
                emit_wdload(2)

                wgT = [sb1(nc.sbuf_tensor(f"wgT{i}", [128, KD, 128], BF16)) for i in range(2)]
                wuT = [sb1(nc.sbuf_tensor(f"wuT{i}", [128, KD, 128], BF16)) for i in range(2)]
                wdstg = sb1(nc.sbuf_tensor("wdstg", [128, 8, 128], BF16))
                sg_t = sb1(nc.sbuf_tensor("sg_t", [128, CH], BF16))
                h_sb = [sb1(nc.sbuf_tensor(f"h_sb{i}", [128, CH], BF16)) for i in range(2)]

                ps_g0 = sb1(nc.psum_tensor("ps_g0", [128, 512], F32))
                ps_g1 = sb1(nc.psum_tensor("ps_g1", [128, 512], F32))
                ps_u0 = sb1(nc.psum_tensor("ps_u0", [128, 512], F32))
                ps_u1 = sb1(nc.psum_tensor("ps_u1", [128, 512], F32))


                P.wait("vector", sm["cs_se"], v_scev)
                P.wait("scalar", sm["cs_dve"], v_bc)

                wgT_made = {}
                gup_done = {}
                wd_T_done = {}
                wdtw_v = {}
                hw_v = {}
                sg_rd = 0
                deq_v = {}
                wd_deq_v = {}

                def emit_wd_deq(idx):
                    if idx in wd_deq_v:
                        return
                    m, _g = wd_seq[idx]
                    bw = idx % 3
                    P.wait("scalar", smp["d_wdl"][bw], wdload_v[idx])
                    wd_deq_v[idx] = P.group("scalar", lambda e, m=m, bw=bw: e.activation(
                        out=wdb[bw][:], in_=wdb[bw][:], func=AF.Identity,
                        scale=scd[:, m:m + 1], bias=bcd[:, m:m + 1]),
                        sm["w_se"])
                    wd_free[idx] = wd_deq_v[idx]

                def emit_deq(cc):
                    b = cc % 2
                    P.wait("vector", smp["d_w"][b], wload_v[cc])
                    if cc >= 2:
                        P.wait("vector", sm["w_pe"], wgT_made[cc - 2])
                    v_dg = P.group("vector", lambda e, cc=cc, b=b: e.tensor_scalar(
                        out=wgb[b][:], in0=wgb[b][:], scalar1=-128.0,
                        scalar2=scg[:, cc:cc + 1], op0=ALU.add, op1=ALU.mult),
                        sm["w_dve"])

                    P.wait("scalar", smp["d_w"][b], wload_v[cc])
                    if cc >= 2:
                        P.wait("scalar", sm["w_pe"], wgT_made[cc - 2])
                    v_du = P.group("scalar", lambda e, cc=cc, b=b: e.activation(
                        out=wub[b][:], in_=wub[b][:], func=AF.Identity,
                        scale=scu[:, cc:cc + 1], bias=bcu[:, cc:cc + 1]),
                        sm["w_se"])
                    wfree[cc] = (v_dg, v_du)
                    deq_v[cc] = (v_dg, v_du)

                emit_deq(0)

                wT_ev_last = {}
                wTv = {}

                def emit_cc_prep(cc):
                    b = cc % 2
                    v_dg, v_du = deq_v[cc]

                    P.wait("tensor", sm["w_dve"], v_dg)
                    P.wait("tensor", sm["w_se"], v_du)
                    for w_idx, (wsrc, wdst) in enumerate(
                            ((wgb[b], wgT[b]), (wub[b], wuT[b]))):
                        for kg in range(4):
                            gi = w_idx * 4 + kg
                            ps = ps_w[gi % 4]
                            g = P.psum_guard.get(id(ps))
                            if g is not None:
                                P.wait("tensor", g[0], g[1])

                            def t_w(eng, wsrc=wsrc, kg=kg, ps=ps):
                                last = None
                                for k2 in range(4):
                                    kk = kg * 4 + k2
                                    last = eng.matmul(ps[:, k2 * 128:(k2 + 1) * 128],
                                                      wsrc[:, kk * 128:(kk + 1) * 128],
                                                      idb[:], start=(k2 == 0),
                                                      stop=(k2 == 3),
                                                      skip_group_check=True)
                                return last
                            v_pe = P.group("tensor", t_w, sm["w_pe"])
                            ev_eng, ev_sem = (("vector", sm["w_dve"]) if gi % 2 == 0
                                              else ("scalar", sm["w_se"]))
                            P.wait(ev_eng, sm["w_pe"], v_pe)
                            if cc >= 2:
                                P.wait(ev_eng, sm["w_pe"], gup_done.get(cc - 2, 0))

                            def ev_w(eng, wdst=wdst, kg=kg, ps=ps):
                                o = wdst[:].rearrange("p k q -> p (k q)")[
                                    :, kg * 512:(kg + 1) * 512]
                                if hasattr(eng, "tensor_copy"):
                                    return eng.tensor_copy(out=o, in_=ps[:])
                                return eng.copy(out=o, in_=ps[:])
                            v_ev = P.group(ev_eng, ev_w, ev_sem)
                            P.psum_freed(ps, ev_sem, v_ev)
                            wT_ev_last[ev_eng] = v_ev
                    wgT_made[cc] = sm["w_pe"].n

                    for rep in range(2):
                        idx = cc * 2 + rep
                        m, g_ = wd_seq[idx]
                        bw = idx % 3
                        if idx == 0:
                            emit_wd_deq(0)
                            emit_wd_deq(1)
                        if idx + 2 < 128:
                            emit_wd_deq(idx + 2)
                        v_dd = wd_deq_v[idx]

                        P.wait("tensor", sm["w_se"], v_dd)
                        for kg in range(2):
                            ps = ps_w[2 * rep + kg]
                            g2 = P.psum_guard.get(id(ps))
                            if g2 is not None:
                                P.wait("tensor", g2[0], g2[1])

                            def t_d(eng, kg=kg, ps=ps, bw=bw):
                                last = None
                                for k2 in range(4):
                                    kc = kg * 4 + k2
                                    last = eng.matmul(ps[:, k2 * 128:(k2 + 1) * 128],
                                                      wdb[bw][:, kc * 128:(kc + 1) * 128],
                                                      idb[:], start=(k2 == 0),
                                                      stop=(k2 == 3),
                                                      skip_group_check=True)
                                return last
                            v_pe = P.group("tensor", t_d, sm["w_pe"])
                            P.wait("vector", sm["w_pe"], v_pe)
                            if idx >= 1:
                                P.wait("vector", sm["d_wdtw"], wdtw_v[idx - 1])

                            def ev_d(eng, kg=kg, ps=ps):
                                last = None
                                for k2 in range(4):
                                    kc = kg * 4 + k2
                                    last = eng.tensor_copy(out=wdstg[:, kc, :],
                                                           in_=ps[:, k2 * 128:(k2 + 1) * 128])
                                return last
                            v_ev = P.group("vector", ev_d, sm["w_dve"])
                            P.psum_freed(ps, sm["w_dve"], v_ev)
                        wd_T_done[idx] = sm["w_pe"].n
                        v_evd = sm["w_dve"].n

                        P.wait("gpsimd", sm["w_dve"], v_evd)
                        wdtw_v[idx] = P.group("gpsimd", lambda e, m=m, g_=g_: e.dma_start(
                            out=wdt_dram[m, g_, :, :, :], in_=wdstg[:]),
                            sm["d_wdtw"], 16)
                        if idx + 3 < 128:
                            emit_wdload(idx + 3)

                    if cc + 1 < KI:
                        emit_deq(cc + 1)

                    wTv[cc] = (wT_ev_last.get("vector", 0),
                               wT_ev_last.get("scalar", 0))

                sg_rd_box = [0]
                gu_emi_box = [0]

                def emit_gu_nb(cc, nb):
                    b = cc % 2
                    emi = gu_emi_box[0]
                    gu_emi_box[0] += 1
                    if nb == 0:
                        P.wait("tensor", sm["w_dve"], wTv[cc][0])
                        P.wait("tensor", sm["w_se"], wTv[cc][1])
                    if cc <= 1:
                        jv = pfx_evac_at[min(4 * nb + 3, NPCH - 1)]
                        P.wait("tensor", sm["p_dve"], jv[0])
                        P.wait("tensor", sm["p_se"], jv[1])
                    psg = (ps_g0, ps_g1)[emi % 2]
                    psu = (ps_u0, ps_u1)[emi % 2]
                    for pp in (psg, psu):
                        g3 = P.psum_guard.get(id(pp))
                        if g3 is not None:
                            P.wait("tensor", g3[0], g3[1])

                    def mm_gu(eng, b=b, nb=nb, psg=psg, psu=psu):
                        for kk in range(KD):
                            eng.matmul(psg[:, 0:CH], wgT[b][:, kk, :],
                                       ffn_inT[:, kk, nb * CH:(nb + 1) * CH],
                                       start=(kk == 0), stop=(kk == KD - 1))
                        last = None
                        for kk in range(KD):
                            last = eng.matmul(psu[:, 0:CH], wuT[b][:, kk, :],
                                              ffn_inT[:, kk, nb * CH:(nb + 1) * CH],
                                              start=(kk == 0), stop=(kk == KD - 1))
                        return last
                    v_gu = P.group("tensor", mm_gu, sm["w_pe"])

                    P.wait("scalar", sm["w_pe"], v_gu)
                    if sg_rd_box[0]:
                        P.wait("scalar", sm["f_dve"], sg_rd_box[0])
                    v_sg = P.group("scalar", lambda e, psg=psg: e.activation(
                        out=sg_t[:], in_=psg[:, 0:CH], func=AF.Silu),
                        sm["f_se"])
                    P.psum_freed(psg, sm["f_se"], v_sg)

                    hb = emi % 2
                    P.wait("vector", sm["f_se"], v_sg)
                    if emi >= 2:
                        P.wait("vector", smp["d_hw"][emi % 2], hw_v[emi - 2])
                    v_h = P.group("vector", lambda e, psu=psu, hb=hb: e.tensor_tensor(
                        out=h_sb[hb][:], in0=sg_t[:], in1=psu[:, 0:CH],
                        op=ALU.mult), sm["f_dve"])
                    P.psum_freed(psu, sm["f_dve"], v_h)
                    sg_rd_box[0] = v_h

                    P.wait("gpsimd", sm["f_dve"], v_h)
                    hw_v[emi] = P.group("gpsimd", lambda e, cc=cc, nb=nb, hb=hb: e.dma_start(
                        out=h_dram[cc, nb, :, :], in_=h_sb[hb][:]),
                        smp["d_hw"][emi % 2], 16)

                def emit_cc_finish(cc):
                    gup_done[cc] = sm["w_pe"].n
                    if cc + 2 < KI:
                        emit_wload(cc + 2)

                with ExitStack() as pfx_scope:
                    pfx = pfx_scope.enter_context
                    pxp = [pfx(nc.sbuf_tensor(f"pxp{i}", [128, D], BF16))
                           for i in range(4)]
                    pfin = [pfx(nc.sbuf_tensor(f"pfin{i}", [128, D], BF16))
                            for i in range(2)]
                    pst = [pfx(nc.sbuf_tensor(f"pst{i}", [128, 4, 6], F32))
                           for i in range(2)]
                    pmv = [pfx(nc.sbuf_tensor(f"pmv{i}", [128, 2], F32))
                           for i in range(2)]
                    psq = [pfx(nc.sbuf_tensor(f"psq{i}", [128, 1], F32))
                           for i in range(2)]
                    prs = [pfx(nc.sbuf_tensor(f"prs{i}", [128, 1], F32))
                           for i in range(2)]
                    pnm = [pfx(nc.sbuf_tensor(f"pnm{i}", [128, 1], F32))
                           for i in range(2)]

                    NPCH = 32
                    pxl_v = {}
                    pcons = {}
                    pfin_rd = {}
                    a_v = {}
                    b_v = {}
                    c_v = {}
                    d_v = {}

                    def p_load(j):
                        bx = j % 4
                        if j < 4:
                            P.wait("sync", sm["ph"], v_attn_done)
                        else:
                            P.wait("sync", sm["p_se"], pcons[j - 4])
                        s0, s1 = xsw_snap[j]
                        P.wait("sync", smp["d_xsw"][0], s0)
                        P.wait("sync", smp["d_xsw"][1], s1)
                        pxl_v[j] = P.group("sync", lambda e, j=j, bx=bx: e.dma_start(
                            out=pxp[bx][:],
                            in_=xs_dram[j * 128:(j + 1) * 128, :]), smp["d_xsr4"][bx], 16)

                    def p_A(j):
                        b = j % 2
                        bx = j % 4
                        P.wait("vector", smp["d_xsr4"][bx], pxl_v[j])
                        if j == 0:
                            P.wait("vector", sm["ph"], v_attn_done)

                        def stats(eng, b=b, bx=bx):
                            last = None
                            for a in range(4):
                                last = eng.bn_stats(pst[b][:, a, :],
                                                    pxp[bx][:, a * 512:(a + 1) * 512])
                            return last
                        vv = P.group("vector", stats, sm["p_dve"])
                        P.wait("vector", sm["p_dve"], vv)
                        a_v[j] = P.group("vector", lambda e, b=b: e.bn_aggr(
                            pmv[b][:], pst[b][:].rearrange("p a b -> p (a b)")),
                            sm["p_dve"])

                    def p_B(j):
                        b = j % 2
                        P.wait("scalar", sm["p_dve"], a_v[j])
                        if j == 0:
                            P.wait("scalar", sm["ph"], v_attn_done)
                        b_v[j] = P.group("scalar", lambda e, b=b: e.activation(
                            out=psq[b][:], in_=pmv[b][:, 1:2], func=AF.Sqrt,
                            bias=eps_t[:]), sm["p_se"])

                    def p_C(j):
                        b = j % 2
                        P.wait("vector", sm["p_se"], b_v[j])
                        if j > 1:
                            P.wait("vector", sm["p_se"], d_v[j - 2])
                        vv = P.group("vector", lambda e, b=b: e.reciprocal(
                            prs[b][:], psq[b][:]), sm["p_dve"])
                        P.wait("vector", sm["p_dve"], vv)
                        c_v[j] = P.group("vector", lambda e, b=b: e.scalar_tensor_tensor(
                            out=pnm[b][:], in0=pmv[b][:, 0:1], scalar=-1.0,
                            in1=prs[b][:], op0=ALU.mult, op1=ALU.mult),
                            sm["p_dve"])

                    def p_D(j):
                        b = j % 2
                        bx = j % 4
                        P.wait("scalar", sm["p_dve"], c_v[j])
                        if j > 1:
                            P.wait("scalar", sm["p_pe"], pfin_rd[j - 2])
                        d_v[j] = P.group("scalar", lambda e, b=b, bx=bx: e.activation(
                            out=pfin[b][:], in_=pxp[bx][:], func=AF.Identity,
                            scale=prs[b][:], bias=pnm[b][:]), sm["p_se"])
                        pcons[j] = d_v[j]

                    def p_E(j):
                        b = j % 2
                        P.wait("tensor", sm["p_se"], d_v[j])
                        if j == 0:
                            P.wait("tensor", sm["ph"], v_attn_done)
                        for kq in range(4):
                            ps = ps_w[kq % 4]
                            g = P.psum_guard.get(id(ps))
                            if g is not None:
                                P.wait("tensor", g[0], g[1])

                            def t_fin(eng, kq=kq, ps=ps, b=b):
                                last = None
                                for k2 in range(4):
                                    kk = kq * 4 + k2
                                    last = eng.matmul(ps[:, k2 * 128:(k2 + 1) * 128],
                                                      pfin[b][:, kk * 128:(kk + 1) * 128],
                                                      idb[:], start=(k2 == 0),
                                                      stop=(k2 == 3),
                                                      skip_group_check=True)
                                return last
                            v_pe = P.group("tensor", t_fin, sm["p_pe"])
                            ev_eng, ev_sem = (("vector", sm["p_dve"]) if kq % 2 == 0
                                              else ("scalar", sm["p_se"]))
                            P.wait(ev_eng, sm["p_pe"], v_pe)

                            def ev_fin(eng, kq=kq, ps=ps, j=j):
                                o = ffn_inT[:, kq * 4:(kq + 1) * 4,
                                            j * 128:(j + 1) * 128]
                                i_ = ps[:, 0:512].rearrange("p (k q) -> p k q", k=4)
                                if hasattr(eng, "tensor_copy"):
                                    return eng.tensor_copy(out=o, in_=i_)
                                return eng.copy(out=o, in_=i_)
                            v_ev = P.group(ev_eng, ev_fin, ev_sem)
                            P.psum_freed(ps, ev_sem, v_ev)
                        pfin_rd[j] = sm["p_pe"].n

                    pfx_evac_at = {}
                    for jj in range(4):
                        p_load(jj)
                    for j in range(NPCH):
                        p_A(j)
                        p_B(j)
                        if j >= 1:
                            p_C(j - 1)
                            p_D(j - 1)
                            if j + 3 < NPCH:
                                p_load(j + 3)
                        if j >= 2:
                            p_E(j - 2)
                            pfx_evac_at[j - 2] = (sm["p_dve"].n, sm["p_se"].n)
                        if j == 1:
                            emit_cc_prep(0)
                        if j == 3:
                            emit_wload(2)
                            emit_cc_prep(1)
                        if j >= 5 and (j - 5) % 4 == 0 and (j - 5) // 4 <= 6:
                            emit_gu_nb(0, (j - 5) // 4)
                        if j >= 7 and (j - 7) % 4 == 0 and (j - 7) // 4 <= 5:
                            emit_gu_nb(1, (j - 7) // 4)
                    p_E(NPCH - 2)
                    pfx_evac_at[NPCH - 2] = (sm["p_dve"].n, sm["p_se"].n)
                    p_C(NPCH - 1)
                    p_D(NPCH - 1)
                    p_E(NPCH - 1)
                    pfx_evac_at[NPCH - 1] = (sm["p_dve"].n, sm["p_se"].n)
                    v_pfx_dve = sm["p_dve"].n
                    v_pfx_se = sm["p_se"].n
                    v_pfx_pe = sm["p_pe"].n

                emit_gu_nb(0, 7)
                emit_cc_finish(0)
                emit_gu_nb(1, 6)
                emit_gu_nb(1, 7)
                emit_cc_finish(1)
                for cc in range(2, KI):
                    emit_cc_prep(cc)
                    for nb in range(NCH):
                        emit_gu_nb(cc, nb)
                    emit_cc_finish(cc)

                v_s1_hw = (smp["d_hw"][0].n, smp["d_hw"][1].n)
                v_s1_wdtw = sm["d_wdtw"].n
                P.group("tensor", lambda e: e.nop(), sm["ph"])
                P.group("vector", lambda e: e.engine_nop(), sm["ph"])
                v_s1_done = P.group("scalar", lambda e: e.copy(out=dummy1[:], in_=eps_t[:]), sm["ph"])

            with ExitStack() as s2_scope:
                sb2 = s2_scope.enter_context
                h_ts = [sb2(nc.sbuf_tensor(f"h_ts{i}", [128, KI, CH], BF16))
                        for i in range(2)]
                wdt_st = [sb2(nc.sbuf_tensor(f"wdt_st{i}", [128, 4, 8, 128], BF16))
                          for i in range(3)]
                xr_t = [sb2(nc.sbuf_tensor(f"xr_t{i}", [128, D], BF16))
                        for i in range(2)]
                fo_f = [sb2(nc.sbuf_tensor(f"fo_f{i}", [128, CH], BF16))
                        for i in range(2)]
                fo_tok = sb2(nc.sbuf_tensor("fo_tok", [128, NSUB, D], BF16))
                ln4_t = [sb2(nc.sbuf_tensor(f"ln4_t{i}", [128, D], BF16))
                         for i in range(2)]
                o_t = [sb2(nc.sbuf_tensor(f"o_t{i}", [128, D], F32)) for i in range(2)]
                st2 = sb2(nc.sbuf_tensor("st2", [128, NSUB, 4, 6], F32))
                mv2 = sb2(nc.sbuf_tensor("mv2", [128, NSUB, 2], F32))
                sq2 = sb2(nc.sbuf_tensor("sq2", [128, NSUB], F32))
                rs2 = sb2(nc.sbuf_tensor("rs2", [128, NSUB], F32))
                nm2 = sb2(nc.sbuf_tensor("nm2", [128, NSUB], F32))

                ps_d0 = sb2(nc.psum_tensor("ps_d0", [128, 512], F32))
                ps_d1 = sb2(nc.psum_tensor("ps_d1", [128, 512], F32))
                ps_T0 = sb2(nc.psum_tensor("ps_T0", [128, 512], F32))
                ps_T1 = sb2(nc.psum_tensor("ps_T1", [128, 512], F32))

                hload_v = {}
                hts_cons = {}

                def emit_hload(ts):
                    bt = ts % 2
                    if ts < 2:
                        P.wait("gpsimd", smp["d_hw"][0], v_s1_hw[0])
                        P.wait("gpsimd", smp["d_hw"][1], v_s1_hw[1])
                        P.wait("gpsimd", sm["ph"], v_s1_done)
                    else:
                        P.wait("gpsimd", sm["f_pe"], hts_cons[ts - 2])

                    def f(eng, ts=ts, bt=bt):
                        last = None
                        for gg in range(8):
                            last = eng.dma_start(
                                out=h_ts[bt][:, gg * 8:(gg + 1) * 8, :],
                                in_=h_dram[gg * 8:(gg + 1) * 8, ts, :, :].rearrange(
                                    "c p t -> p c t"))
                            last.then_inc(smp["d_hr"][bt].h, 16)
                    P.on("gpsimd", f)
                    hload_v[ts] = smp["d_hr"][bt].inc(16 * 8)

                wdt_v = {}
                wdt_cons = {}

                def emit_wdtload(widx):
                    bm = widx % 3
                    qeng = "sync"
                    ts, rem = divmod(widx, KD * 2)
                    m, half = divmod(rem, 2)
                    if widx < 3:
                        P.wait(qeng, sm["d_wdtw"], v_s1_wdtw)
                        P.wait(qeng, sm["ph"], v_s1_done)
                    else:
                        P.wait(qeng, sm["f_pe"], wdt_cons[widx - 3])
                    wdt_v[widx] = P.group(qeng, lambda e, m=m, half=half, bm=bm: e.dma_start(
                        out=wdt_st[bm][:],
                        in_=wdt_dram[m, half * 4:(half + 1) * 4, :, :, :].rearrange(
                            "g p k h -> p g k h")), smp["d_wdtr3"][bm], 16)

                xr_v = {}
                xr_cons = {}

                def emit_xr(ts, s):
                    oi = ts * NSUB + s
                    if oi == 0:
                        P.wait("scalar", sm["ph"], v_s1_done)
                    if oi >= 2:
                        P.wait("scalar", sm["f_dve"], xr_cons[oi - 2])
                    xr_v[oi] = P.group("scalar", lambda e, ts=ts, s=s: e.dma_start(
                        out=xr_t[s % 2][:],
                        in_=xs_dram[ts * CH + s * 128: ts * CH + (s + 1) * 128, :]),
                        smp["d_xsr"][s % 2], 16)

                emit_hload(0)
                emit_hload(1)
                emit_wdtload(0)
                emit_wdtload(1)
                emit_wdtload(2)

                out_v = {}
                oadd_v = {}
                fo_rd = {}
                fotok_rd = {}
                pending_tdn = [None]

                def emit_tdn(ts, m, psd):
                    idxm = ts * KD + m
                    bf = m % 2
                    P.wait("scalar", sm["f_pe"], fo_rd[idxm])
                    if idxm > 1:
                        P.wait("scalar", sm["f_pe"], tdn_pe_done[idxm - 2])
                    v_fo = P.group("scalar", lambda e, psd=psd, bf=bf: e.copy(
                        out=fo_f[bf][:], in_=psd[:, 0:CH]), sm["f_se"])
                    P.psum_freed(psd, sm["f_se"], v_fo)

                    P.wait("tensor", sm["f_se"], v_fo)
                    for sp in range(2):
                        ps = (ps_T0, ps_T1)[sp]
                        g5 = P.psum_guard.get(id(ps))
                        if g5 is not None:
                            P.wait("tensor", g5[0], g5[1])

                        def t_dn(eng, sp=sp, ps=ps, bf=bf):
                            last = None
                            for s2_ in range(2):
                                s = sp * 2 + s2_
                                last = eng.matmul(ps[:, s2_ * 128:(s2_ + 1) * 128],
                                                  fo_f[bf][:, s * 128:(s + 1) * 128],
                                                  idb[:], start=(s2_ == 0),
                                                  stop=(s2_ == 1),
                                                  skip_group_check=True)
                            return last
                        v_pe = P.group("tensor", t_dn, sm["f_pe"])
                        P.wait("vector", sm["f_pe"], v_pe)
                        if ts > 0:
                            P.wait("vector", sm["f_se"], fotok_rd[ts - 1])

                        def ev_tdn(eng, sp=sp, ps=ps, m=m):
                            last = None
                            for s2_ in range(2):
                                s = sp * 2 + s2_
                                last = eng.tensor_copy(
                                    out=fo_tok[:, s, m * 128:(m + 1) * 128],
                                    in_=ps[:, s2_ * 128:(s2_ + 1) * 128])
                            return last
                        v_ev = P.group("vector", ev_tdn, sm["f_dve"])
                        P.psum_freed(ps, sm["f_dve"], v_ev)
                    tdn_pe_done[idxm] = sm["f_pe"].n

                tdn_pe_done = {}

                for ts in range(NCH):
                    bt = ts % 2
                    P.wait("tensor", smp["d_hr"][ts % 2], hload_v[ts])
                    for m in range(KD):
                        psd = (ps_d0, ps_d1)[m % 2]
                        g4 = P.psum_guard.get(id(psd))
                        if g4 is not None:
                            P.wait("tensor", g4[0], g4[1])
                        for half in range(2):
                            widx = ts * KD * 2 + m * 2 + half
                            bm = widx % 3
                            P.wait("tensor", smp["d_wdtr3"][bm], wdt_v[widx])

                            def mm_dn(eng, bt=bt, bm=bm, psd=psd, half=half):
                                last = None
                                for kcl in range(32):
                                    kc = half * 32 + kcl
                                    last = eng.matmul(psd[:, 0:CH],
                                                      wdt_st[bm][:, kcl // 8, kcl % 8, :],
                                                      h_ts[bt][:, kc, :],
                                                      start=(kc == 0), stop=(kc == KI - 1))
                                return last
                            v_dn = P.group("tensor", mm_dn, sm["f_pe"])
                            wdt_cons[widx] = v_dn
                            if widx + 3 < NCH * KD * 2:
                                emit_wdtload(widx + 3)
                        if m == KD - 1:
                            hts_cons[ts] = v_dn
                        fo_rd[ts * KD + m] = v_dn

                        if pending_tdn[0] is not None:
                            emit_tdn(*pending_tdn[0])
                        pending_tdn[0] = (ts, m, psd)
                    emit_tdn(*pending_tdn[0])
                    pending_tdn[0] = None
                    v_fotok = sm["f_dve"].n

                    fo_subs = [fo_tok[:, s, :] for s in range(NSUB)]
                    P.wait("vector", sm["f_dve"], v_fotok)
                    v_l4 = emit_ln_generic(P, fo_subs, st2, mv2, sq2, rs2, nm2,
                                           sm["f_dve"], sm["f_se"], eps_t)

                    P.wait("scalar", sm["f_dve"], v_l4)
                    for s in range(NSUB):
                        oi = ts * NSUB + s
                        emit_xr(ts, s)
                        if oi >= 2:
                            P.wait("scalar", sm["f_dve"], oadd_v[oi - 2])
                        v_a4 = P.group("scalar", lambda e, s=s: e.activation(
                            out=ln4_t[s % 2][:], in_=fo_tok[:, s, :], func=AF.Identity,
                            scale=rs2[:, s:s + 1], bias=nm2[:, s:s + 1]),
                            sm["f_se"])

                        P.wait("vector", sm["f_se"], v_a4)
                        P.wait("vector", smp["d_xsr"][s % 2], xr_v[oi])
                        if oi >= 2:
                            P.wait("vector", smp["d_out"][oi % 2], out_v[oi - 2])
                        v_oa = P.group("vector", lambda e, s=s: e.tensor_tensor(
                            out=o_t[s % 2][:], in0=xr_t[s % 2][:], in1=ln4_t[s % 2][:],
                            op=ALU.add), sm["f_dve"])
                        oadd_v[oi] = v_oa
                        xr_cons[oi] = v_oa

                        P.wait("gpsimd", sm["f_dve"], v_oa)
                        out_v[oi] = P.group("gpsimd", lambda e, ts=ts, s=s: e.dma_start(
                            out=out_t[ts * CH + s * 128: ts * CH + (s + 1) * 128, :],
                            in_=o_t[s % 2][:]), smp["d_out"][oi % 2], 16)
                    fotok_rd[ts] = sm["f_se"].n
                    if ts + 2 < NCH:
                        emit_hload(ts + 2)

                P.wait("gpsimd", smp["d_out"][0], smp["d_out"][0].n)
                P.wait("gpsimd", smp["d_out"][1], smp["d_out"][1].n)

        with nc.Block() as block:
            @block.sync
            def _(sync):
                for f in P.streams["sync"]:
                    f(sync)

            @block.tensor
            def _(tensor):
                for f in P.streams["tensor"]:
                    f(tensor)

            @block.vector
            def _(vector):
                for f in P.streams["vector"]:
                    f(vector)

            @block.scalar
            def _(scalar):
                for f in P.streams["scalar"]:
                    f(scalar)

            @block.gpsimd
            def _(gpsimd):
                for f in P.streams["gpsimd"]:
                    f(gpsimd)

    return nc


def emit_ln_generic(P, in_subs, st, mv, sq, rs, nm, sd, ss, eps_t):
    nsub = len(in_subs)

    def stats(eng):
        last = None
        for s in range(nsub):
            for a in range(4):
                last = eng.bn_stats(st[:, s, a, :],
                                    in_subs[s][:, a * 512:(a + 1) * 512])
        return last
    vv = P.group("vector", stats, sd)
    P.wait("vector", sd, vv)

    def aggr(eng):
        last = None
        for s in range(nsub):
            last = eng.bn_aggr(mv[:, s, :],
                               st[:, s, :, :].rearrange("p a b -> p (a b)"))
        return last
    v_ag = P.group("vector", aggr, sd)

    P.wait("scalar", sd, v_ag)
    v_sq = P.group("scalar", lambda e: e.activation(
        out=sq[:, 0:nsub], in_=mv[:, :, 1], func=AF.Sqrt, bias=eps_t[:]), ss)

    P.wait("vector", ss, v_sq)
    vv = P.group("vector", lambda e: e.reciprocal(rs[:, 0:nsub], sq[:, 0:nsub]), sd)
    P.wait("vector", sd, vv)
    v_nm = P.group("vector", lambda e: e.scalar_tensor_tensor(
        out=nm[:, 0:nsub], in0=mv[:, :, 0], scalar=-1.0, in1=rs[:, 0:nsub],
        op0=ALU.mult, op1=ALU.mult), sd)
    return v_nm


def kernel(L0_attn_u, L0_attn_A_factors, L0_attn_W_b_rank, L0_attn_W_b_state,
           L0_attn_W_c_rank, L0_attn_W_c_model, L0_attn_D_vec,
           L0_ffn_gate_w, L0_ffn_gate_scale, L0_ffn_up_w, L0_ffn_up_scale,
           L0_ffn_down_w, L0_ffn_down_scale):
    x = np.ascontiguousarray(np.asarray(L0_attn_u), dtype=np.float32).reshape(NCORES * T, D)
    wbr = np.ascontiguousarray(
        np.asarray(L0_attn_W_b_rank, dtype=np.float32).reshape(KD, 128, 64)
        .transpose(1, 0, 2).reshape(128, KD * 64))
    wbs = np.ascontiguousarray(np.asarray(L0_attn_W_b_state, dtype=np.float32))
    wcr = np.ascontiguousarray(
        np.asarray(L0_attn_W_c_rank, dtype=np.float32).reshape(2, 128, 64)
        .transpose(1, 0, 2).reshape(128, 2 * 64))
    wcm = np.ascontiguousarray(np.asarray(L0_attn_W_c_model, dtype=np.float32))
    dvec = np.ascontiguousarray(np.asarray(L0_attn_D_vec, dtype=np.float32).reshape(KD, 128))
    afac = np.ascontiguousarray(np.asarray(L0_attn_A_factors, dtype=np.float32).reshape(1, 32))
    gw = np.ascontiguousarray(np.asarray(L0_ffn_gate_w, dtype=np.int32))
    uw = np.ascontiguousarray(np.asarray(L0_ffn_up_w, dtype=np.int32))
    dw = np.ascontiguousarray(np.asarray(L0_ffn_down_w, dtype=np.int32))
    gsc = np.ascontiguousarray(np.asarray(L0_ffn_gate_scale, dtype=np.float32).reshape(KI, 128))
    usc = np.ascontiguousarray(np.asarray(L0_ffn_up_scale, dtype=np.float32).reshape(KI, 128))
    dsc = np.ascontiguousarray(np.asarray(L0_ffn_down_scale, dtype=np.float32).reshape(KD, 128))
    idb = np.eye(128, dtype=ml_dtypes.bfloat16)
    idf = np.eye(128, dtype=np.float32)

    nc = build_nc()
    in_maps = []
    for i in range(NCORES):
        in_maps.append(dict(
            x=np.ascontiguousarray(x[i * T:(i + 1) * T]),
            wbr=wbr, wbs=wbs, wcr=wcr, wcm=wcm, dvec=dvec, afac=afac,
            gw=gw, uw=uw, dw=dw, gsc=gsc, usc=usc, dsc=dsc,
            idb=idb, idf=idf,
        ))
    res = run_bass_kernel_spmd(nc, in_maps, core_ids=list(range(NCORES)))
    out = np.concatenate([np.asarray(r["out"]) for r in res.results], axis=0)
    return out.reshape(2, 16384, D).astype(np.float32)


# version 25
# speedup vs baseline: 1.1893x; 1.1479x over previous
import sys
sys.path.insert(0, "/opt/trn_rl_repo")
from contextlib import ExitStack
import numpy as np
import ml_dtypes
import concourse.bass as bass
import concourse.mybir as mybir
from concourse.bass_utils import run_bass_kernel_spmd

F32 = mybir.dt.float32
BF16 = mybir.dt.bfloat16
I32 = mybir.dt.int32
AF = mybir.ActivationFunctionType
ALU = mybir.AluOpType

NCORES = 8
T = 4096
D = 2048
I_DIM = 8192
NCH = 8
CH = 512
NSUB = 4
KD = 16
KI = 64
EPS = 1e-5


class Sem:
    def __init__(self, handle):
        self.h = handle
        self.n = 0

    def inc(self, amt=1):
        self.n += amt
        return self.n


class Prog:
    ENGINES = ("sync", "tensor", "vector", "scalar", "gpsimd")

    def __init__(self):
        self.streams = {e: [] for e in self.ENGINES}
        self.psum_guard = {}

    def on(self, eng, fn):
        self.streams[eng].append(fn)

    def wait(self, eng, sem, val):
        if val > 0:
            self.on(eng, lambda e, s=sem.h, v=val: e.wait_ge(s, v))

    def group(self, eng, fn, sem, amt=1):
        v = sem.inc(amt)

        def wrapped(e, fn=fn, h=sem.h, amt=amt):
            last = fn(e)
            last.then_inc(h, amt)
        self.on(eng, wrapped)
        return v

    def pe_wait_psum(self, ps):
        g = self.psum_guard.get(id(ps))
        if g is not None:
            sem, v = g
            self.wait("tensor", sem, v)

    def psum_freed(self, ps, sem, val):
        self.psum_guard[id(ps)] = (sem, val)


def build_nc():
    nc = bass.Bass("TRN2", target_bir_lowering=False)

    x_in = nc.dram_tensor("x", [T, D], F32, kind="ExternalInput")
    wbr_in = nc.dram_tensor("wbr", [128, KD * 64], F32, kind="ExternalInput")
    wbs_in = nc.dram_tensor("wbs", [64, 256], F32, kind="ExternalInput")
    wcr_in = nc.dram_tensor("wcr", [128, 2 * 64], F32, kind="ExternalInput")
    wcm_in = nc.dram_tensor("wcm", [64, D], F32, kind="ExternalInput")
    dvec_in = nc.dram_tensor("dvec", [KD, 128], F32, kind="ExternalInput")
    afac_in = nc.dram_tensor("afac", [1, 32], F32, kind="ExternalInput")
    gw_in = nc.dram_tensor("gw", [I_DIM, D], I32, kind="ExternalInput")
    uw_in = nc.dram_tensor("uw", [I_DIM, D], I32, kind="ExternalInput")
    dw_in = nc.dram_tensor("dw", [D, I_DIM], I32, kind="ExternalInput")
    gsc_in = nc.dram_tensor("gsc", [KI, 128], F32, kind="ExternalInput")
    usc_in = nc.dram_tensor("usc", [KI, 128], F32, kind="ExternalInput")
    dsc_in = nc.dram_tensor("dsc", [KD, 128], F32, kind="ExternalInput")
    idb_in = nc.dram_tensor("idb", [128, 128], BF16, kind="ExternalInput")
    idf_in = nc.dram_tensor("idf", [128, 128], F32, kind="ExternalInput")
    out_t = nc.dram_tensor("out", [T, D], F32, kind="ExternalOutput")

    xs_dram = nc.dram_tensor("xs_dram", [T, D], BF16)
    h_dram = nc.dram_tensor("h_dram", [KI, NCH, 128, CH], BF16)
    wdt_dram = nc.dram_tensor("wdt_dram", [KD, 8, 128, 8, 128], BF16)

    P = Prog()

    with ExitStack() as ctx:
        ecm = ctx.enter_context
        sm = {}
        for name in ("d_setup", "d_wd", "d_wdtw",
                     "cs_pe", "cs_se", "cs_dve",
                     "a_pe", "a_se", "a_dve", "a_gp",
                     "p_pe", "p_se", "p_dve",
                     "w_pe", "w_se", "w_dve",
                     "f_pe", "f_se", "f_dve",
                     "ph"):
            sm[name] = Sem(ecm(nc.semaphore(name)))
        smp = {}
        for name in ("d_x", "d_w", "d_wdtr", "d_hr", "d_xsw", "d_xsr",
                     "d_hw", "d_out"):
            smp[name] = [Sem(ecm(nc.semaphore(name + "0"))),
                         Sem(ecm(nc.semaphore(name + "1")))]
        smp["d_wdl"] = [Sem(ecm(nc.semaphore(f"d_wdl{i}"))) for i in range(3)]
        smp["d_wdtr3"] = [Sem(ecm(nc.semaphore(f"d_wdtr3_{i}"))) for i in range(3)]
        smp["d_xsr4"] = [Sem(ecm(nc.semaphore(f"d_xsr4_{i}"))) for i in range(4)]

        eps_t = ecm(nc.sbuf_tensor("eps_t", [128, 1], F32))
        idb = ecm(nc.sbuf_tensor("idb_s", [128, 128], BF16))
        idf = ecm(nc.sbuf_tensor("idf_s", [128, 128], F32))
        scg = ecm(nc.sbuf_tensor("scg", [128, KI], F32))
        scu = ecm(nc.sbuf_tensor("scu", [128, KI], F32))
        scd = ecm(nc.sbuf_tensor("scd", [128, KD], F32))
        bcu = ecm(nc.sbuf_tensor("bcu", [128, KI], F32))
        bcd = ecm(nc.sbuf_tensor("bcd", [128, KD], F32))
        ones_row = ecm(nc.sbuf_tensor("ones_row", [1, 128], F32))
        dummy1 = ecm(nc.sbuf_tensor("dummy1", [128, 1], F32))

        def f(eng):
            last = None
            for dst, src in ((idb, idb_in), (idf, idf_in)):
                last = eng.dma_start(out=dst[:], in_=src[:])
                last.then_inc(sm["d_setup"].h, 16)
            return last
        P.on("sync", lambda e: f(e) and None)
        sm["d_setup"].inc(32)
        v_id_loaded = sm["d_setup"].n

        v_eps = P.group("vector", lambda e: e.memset(eps_t[:], EPS), sm["cs_dve"])
        v_ones = P.group("vector", lambda e: e.memset(ones_row[:], 1.0), sm["cs_dve"])

        setup_scope = ExitStack()
        if True:
            ssc = setup_scope.enter_context
            ps_set = ssc(nc.psum_tensor("ps_set", [128, 512], F32))
            gsc_s = ssc(nc.sbuf_tensor("gsc_s", [KI, 128], F32))
            usc_s = ssc(nc.sbuf_tensor("usc_s", [KI, 128], F32))
            dsc_s = ssc(nc.sbuf_tensor("dsc_s", [KD, 128], F32))
            wbr_f = ssc(nc.sbuf_tensor("wbr_f", [128, KD * 64], F32))
            wbs_f = ssc(nc.sbuf_tensor("wbs_f", [64, 256], F32))
            wcr_f = ssc(nc.sbuf_tensor("wcr_f", [128, 2 * 64], F32))
            wcm_f = ssc(nc.sbuf_tensor("wcm_f", [64, D], F32))
            dvec_s = ssc(nc.sbuf_tensor("dvec_s", [KD, 128], F32))
            afac_s = ssc(nc.sbuf_tensor("afac_s", [1, 32], F32))


            def setup2(eng):
                last = None
                for dst, src in ((gsc_s, gsc_in), (usc_s, usc_in), (dsc_s, dsc_in),
                                 (wbr_f, wbr_in), (wbs_f, wbs_in), (wcr_f, wcr_in),
                                 (wcm_f, wcm_in), (dvec_s, dvec_in), (afac_s, afac_in)):
                    last = eng.dma_start(out=dst[:], in_=src[:])
                    last.then_inc(sm["d_setup"].h, 16)
            P.on("sync", setup2)
            sm["d_setup"].inc(16 * 9)
            v_setup_loaded = sm["d_setup"].n

            wbr_b = ssc(nc.sbuf_tensor("wbr_b", [128, KD, 64], BF16))
            wbs_b = ssc(nc.sbuf_tensor("wbs_b", [64, 256], BF16))
            wcr_b = ssc(nc.sbuf_tensor("wcr_b", [128, 2, 64], BF16))
            wcm_b = ssc(nc.sbuf_tensor("wcm_b", [64, D], BF16))
            dcols = ssc(nc.sbuf_tensor("dcols", [128, KD], F32))
            acols = ssc(nc.sbuf_tensor("acols", [128, 32], F32))
            diag_d = ssc(nc.sbuf_tensor("diag_d", [128, KD, 128], BF16))
            g_cur = ssc(nc.sbuf_tensor("g_cur", [128, 2, 256], F32))
            g_nxt = ssc(nc.sbuf_tensor("g_nxt", [128, 2, 256], F32))
            gtmp = ssc(nc.sbuf_tensor("gtmp", [128, 2, 2, 128], F32))

            P.wait("tensor", sm["d_setup"], v_setup_loaded)
            P.wait("tensor", sm["cs_dve"], v_ones)

            def scale_T(eng):
                eng.matmul(ps_set[:, 0:KI], gsc_s[:], idf[0:KI, 0:KI],
                           start=True, stop=True)
                eng.matmul(ps_set[:, 64:64 + KI], usc_s[:], idf[0:KI, 0:KI],
                           start=False, stop=True, skip_group_check=True)
                eng.matmul(ps_set[:, 128:128 + KD], dsc_s[:], idf[0:KD, 0:KD],
                           start=False, stop=True, skip_group_check=True)
                eng.matmul(ps_set[:, 160:160 + KD], dvec_s[:], idf[0:KD, 0:KD],
                           start=False, stop=True, skip_group_check=True)
                return eng.matmul(ps_set[:, 192:192 + 32], ones_row[:], afac_s[:],
                                  start=False, stop=True, skip_group_check=True)
            v_scT = P.group("tensor", scale_T, sm["cs_pe"])

            P.wait("scalar", sm["cs_pe"], v_scT)

            def scale_ev(eng):
                eng.copy(out=scg[:], in_=ps_set[:, 0:KI])
                eng.copy(out=scu[:], in_=ps_set[:, 64:64 + KI])
                eng.copy(out=scd[:], in_=ps_set[:, 128:128 + KD])
                eng.copy(out=dcols[:], in_=ps_set[:, 160:160 + KD])
                return eng.copy(out=acols[:], in_=ps_set[:, 192:192 + 32])
            v_scev = P.group("scalar", scale_ev, sm["cs_se"])

            P.wait("vector", sm["cs_se"], v_scev)

            def bias_cols(eng):
                eng.tensor_scalar_mul(out=bcu[:], in0=scu[:], scalar1=-128.0)
                return eng.tensor_scalar_mul(out=bcd[:], in0=scd[:], scalar1=-128.0)
            v_bc = P.group("vector", bias_cols, sm["cs_dve"])

            def w_cast(eng):
                eng.tensor_copy(out=wbr_b[:].rearrange("p k r -> p (k r)"), in_=wbr_f[:])
                eng.tensor_copy(out=wbs_b[:], in_=wbs_f[:])
                eng.tensor_copy(out=wcr_b[:].rearrange("p k r -> p (k r)"), in_=wcr_f[:])
                return eng.tensor_copy(out=wcm_b[:], in_=wcm_f[:])
            v_wcast = P.group("vector", w_cast, sm["cs_dve"])

            def mk_diag(eng):
                last = None
                for kk in range(KD):
                    last = eng.tensor_scalar(out=diag_d[:, kk, :], in0=idb[:],
                                             scalar1=dcols[:, kk:kk + 1],
                                             scalar2=None, op0=ALU.mult)
                return last
            v_diag = P.group("vector", mk_diag, sm["cs_dve"])

            def g_init(eng):
                eng.memset(g_cur[:], 0.0)
                eng.memset(g_nxt[:], 0.0)
                eng.tensor_copy(out=g_cur[:, 0, 0:128], in_=idf[:])
                return eng.tensor_copy(out=g_cur[:, 1, 128:256], in_=idf[:])
            v_gi = P.group("vector", g_init, sm["cs_dve"])
            P.wait("vector", sm["cs_dve"], v_gi)

            bufs = [g_cur, g_nxt]
            for lvl in range(8):
                delta = 128 >> lvl
                src, dst = bufs[lvl % 2], bufs[(lvl + 1) % 2]

                def g_mul(eng, src=src, lvl=lvl, delta=delta):
                    last = None
                    for r in range(2):
                        v = src[:, r, :].rearrange("p (h two d) -> p h two d",
                                                   two=2, d=delta)
                        for j in range(2):
                            a1 = acols[:, lvl * 4 + 2 + j: lvl * 4 + 3 + j]
                            last = eng.tensor_scalar(
                                out=gtmp[:, r, j, :].rearrange(
                                    "p (h d) -> p h d", d=delta),
                                in0=v[:, :, 1, :],
                                scalar1=a1, scalar2=None, op0=ALU.mult)
                    return last
                vv = P.group("vector", g_mul, sm["cs_dve"])
                P.wait("vector", sm["cs_dve"], vv)

                def g_add(eng, src=src, dst=dst, lvl=lvl, delta=delta):
                    last = None
                    for r in range(2):
                        v = src[:, r, :].rearrange("p (h two d) -> p h two d",
                                                   two=2, d=delta)
                        o = dst[:, r, :].rearrange("p (h two d) -> p h two d",
                                                   two=2, d=delta)
                        for j in range(2):
                            a0 = acols[:, lvl * 4 + j: lvl * 4 + 1 + j]
                            last = eng.scalar_tensor_tensor(
                                out=o[:, :, j, :],
                                in0=v[:, :, 0, :],
                                scalar=a0,
                                in1=gtmp[:, r, j, :].rearrange(
                                    "p (h d) -> p h d", d=delta),
                                op0=ALU.mult, op1=ALU.add)
                    return last
                vv = P.group("vector", g_add, sm["cs_dve"])
                P.wait("vector", sm["cs_dve"], vv)
            g_fin = bufs[0]
            v_g_done = sm["cs_dve"].n

            with ExitStack() as attn_scope:
                asb = attn_scope.enter_context
                x_buf = [asb(nc.sbuf_tensor(f"x_buf{i}", [128, NSUB, D], BF16))
                         for i in range(2)]
                xn = asb(nc.sbuf_tensor("xn", [128, NSUB, D], BF16))
                xnT = asb(nc.sbuf_tensor("xnT", [128, KD, CH], BF16))
                s1_sb = asb(nc.sbuf_tensor("s1_sb", [64, CH], BF16))
                s2_sb = asb(nc.sbuf_tensor("s2_sb", [128, 2, CH], F32))
                s3_sb = asb(nc.sbuf_tensor("s3_sb", [128, 2, CH], BF16))
                s4_sb = asb(nc.sbuf_tensor("s4_sb", [64, CH], BF16))
                yf = asb(nc.sbuf_tensor("yf", [128, KD, CH], BF16))
                y_tok = [asb(nc.sbuf_tensor(f"y_tok{i}", [128, NSUB, D], BF16))
                         for i in range(2)]
                ln2_t = [asb(nc.sbuf_tensor(f"ln2_t{i}", [128, D], BF16))
                         for i in range(4)]
                xp_t = [asb(nc.sbuf_tensor(f"xp_t{i}", [128, D], BF16))
                        for i in range(2)]
                yr_t = [asb(nc.sbuf_tensor(f"yr_t{i}", [128, D], BF16))
                        for i in range(2)]
                stA = asb(nc.sbuf_tensor("stA", [128, NSUB, 4, 6], F32))
                mvA = [asb(nc.sbuf_tensor(f"mvA{i}", [128, NSUB, 2], F32)) for i in range(2)]
                sqA = [asb(nc.sbuf_tensor(f"sqA{i}", [128, NSUB], F32)) for i in range(2)]
                rsA = [asb(nc.sbuf_tensor(f"rsA{i}", [128, NSUB], F32)) for i in range(2)]
                nmA = [asb(nc.sbuf_tensor(f"nmA{i}", [128, NSUB], F32)) for i in range(2)]
                nmA2 = [asb(nc.sbuf_tensor(f"nmA2_{i}", [128, NSUB], F32)) for i in range(2)]
                stB = asb(nc.sbuf_tensor("stB", [128, NSUB, 4, 6], F32))
                mvB = [asb(nc.sbuf_tensor(f"mvB{i}", [128, NSUB, 2], F32)) for i in range(2)]
                sqB = [asb(nc.sbuf_tensor(f"sqB{i}", [128, NSUB], F32)) for i in range(2)]
                rsB = [asb(nc.sbuf_tensor(f"rsB{i}", [128, NSUB], F32)) for i in range(2)]
                nmB = [asb(nc.sbuf_tensor(f"nmB{i}", [128, NSUB], F32)) for i in range(2)]
                ngB = [asb(nc.sbuf_tensor(f"ngB{i}", [128, NSUB], F32)) for i in range(2)]

                ps_t0 = asb(nc.psum_tensor("ps_t0", [128, 512], F32))
                ps_t1 = asb(nc.psum_tensor("ps_t1", [128, 512], F32))
                ps_s0 = asb(nc.psum_tensor("ps_s0", [128, 512], F32))
                ps_s1 = asb(nc.psum_tensor("ps_s1", [128, 512], F32))
                ps_y0 = asb(nc.psum_tensor("ps_y0", [128, 512], F32))
                ps_y1 = asb(nc.psum_tensor("ps_y1", [128, 512], F32))

                sq1_v = {}
                sq2_v = {}

                def ln_A(in_subs, st, mv, sd):
                    def stats(eng):
                        last = None
                        for s in range(NSUB):
                            for a in range(4):
                                last = eng.bn_stats(st[:, s, a, :],
                                                    in_subs[s][:, a * 512:(a + 1) * 512])
                        return last
                    vv = P.group("vector", stats, sd)
                    P.wait("vector", sd, vv)

                    def aggr(eng):
                        last = None
                        for s in range(NSUB):
                            last = eng.bn_aggr(mv[:, s, :],
                                               st[:, s, :, :].rearrange("p a b -> p (a b)"))
                        return last
                    return P.group("vector", aggr, sd)

                def ln_AB(in_subs, st, mv, sq, sd, ss):
                    def stats(eng):
                        last = None
                        for s in range(NSUB):
                            for a in range(4):
                                last = eng.bn_stats(st[:, s, a, :],
                                                    in_subs[s][:, a * 512:(a + 1) * 512])
                        return last
                    vv = P.group("vector", stats, sd)
                    P.wait("vector", sd, vv)

                    def aggr(eng):
                        last = None
                        for s in range(NSUB):
                            last = eng.bn_aggr(mv[:, s, :],
                                               st[:, s, :, :].rearrange("p a b -> p (a b)"))
                        return last
                    v_ag = P.group("vector", aggr, sd)
                    P.wait("scalar", sd, v_ag)
                    return P.group("scalar", lambda e: e.activation(
                        out=sq[:, 0:NSUB], in_=mv[:, :, 1], func=AF.Sqrt,
                        bias=eps_t[:]), ss)

                def ln_CD(mv, sq, rs, nm, sd, ss, v_sq):
                    P.wait("vector", ss, v_sq)
                    vv = P.group("vector", lambda e: e.reciprocal(rs[:, 0:NSUB], sq[:, 0:NSUB]), sd)
                    P.wait("vector", sd, vv)
                    return P.group("vector", lambda e: e.scalar_tensor_tensor(
                        out=nm[:, 0:NSUB], in0=mv[:, :, 0], scalar=-1.0,
                        in1=rs[:, 0:NSUB], op0=ALU.mult, op1=ALU.mult), sd)

                xload_v = {}
                xcons = {}

                def emit_xload(c):
                    b = c % 2
                    if c >= 2:
                        P.wait("gpsimd", sm["a_dve"], xcons[c - 2])
                    xload_v[c] = P.group("gpsimd", lambda e, c=c, b=b: e.dma_start(
                        out=x_buf[b][:],
                        in_=x_in[c * CH:(c + 1) * CH, :].rearrange(
                            "(a p) d -> p a d", p=128)), smp["d_x"][b], 16)

                emit_xload(0)
                emit_xload(1)

                P.wait("tensor", sm["cs_dve"], v_g_done)
                P.wait("vector", sm["cs_se"], v_scev)
                P.wait("scalar", sm["cs_dve"], v_eps)

                txn_done = {}
                xsw_snap = {}
                mm_done = {}
                ty_done = {}
                ytok_rd = {}
                xpadd_v = {}
                xsw_v = {}
                apply1_v = {}
                apply2_v = {}

                def emit_ln1_AB(c):
                    b = c % 2
                    P.wait("vector", smp["d_x"][b], xload_v[c])
                    if c >= 2:
                        P.wait("vector", sm["a_se"], sq1_v[c - 2])
                    xb = x_buf[b]
                    sq1_v[c] = ln_AB([xb[:, s, :] for s in range(NSUB)],
                                     stA, mvA[b], sqA[b], sm["a_dve"], sm["a_se"])

                def emit_epilogue(c):
                    b = c % 2
                    xb = x_buf[b]
                    P.wait("scalar", sm["a_dve"], aggr2_v[c])
                    sq2_v[c] = P.group("scalar", lambda e, b=b: e.activation(
                        out=sqB[b][:, 0:NSUB], in_=mvB[b][:, :, 1], func=AF.Sqrt,
                        bias=eps_t[:]), sm["a_se"])
                    if c >= 2:
                        P.wait("vector", sm["a_gp"], apply2_v.get(c - 2, 0))
                    v_nm2 = ln_CD(mvB[b], sqB[b], rsB[b], nmB[b],
                                  sm["a_dve"], sm["a_se"], sq2_v[c])
                    P.wait("vector", sm["a_dve"], v_nm2)
                    for s in range(NSUB):
                        oi = c * NSUB + s
                        if oi >= 4:
                            P.wait("vector", sm["a_gp"], xpadd_v[oi - 4])
                        P.group("vector", lambda e, s=s, b=b, xb=xb: e.tensor_scalar(
                            out=ln2_t[s][:], in0=xb[:, s, :],
                            scalar1=nmB[b][:, s:s + 1], scalar2=None,
                            op0=ALU.add), sm["a_dve"])
                    xcons[c] = sm["a_dve"].n
                    v_t_all = sm["a_dve"].n
                    if c + 2 < NCH:
                        emit_xload(c + 2)

                    P.wait("gpsimd", sm["a_dve"], v_t_all)
                    for s in range(NSUB):
                        oi = c * NSUB + s
                        v_yr = P.group("gpsimd", lambda e, s=s, b=b: e.tensor_scalar(
                            out=yr_t[s % 2][:], in0=y_tok[b][:, s, :],
                            scalar1=0.0, scalar2=rsB[b][:, s:s + 1],
                            op0=ALU.add, op1=ALU.mult), sm["a_gp"])
                        P.wait("gpsimd", sm["a_gp"], v_yr)
                        if oi >= 2:
                            P.wait("gpsimd", smp["d_xsw"][oi % 2], xsw_v[oi - 2])
                        v_xp = P.group("gpsimd", lambda e, s=s: e.tensor_tensor(
                            out=xp_t[s % 2][:], in0=yr_t[s % 2][:],
                            in1=ln2_t[s][:], op=ALU.add), sm["a_gp"])
                        xpadd_v[oi] = v_xp
                        P.wait("sync", sm["a_gp"], v_xp)
                        xsw_v[oi] = P.group("sync", lambda e, c=c, s=s: e.dma_start(
                            out=xs_dram[c * CH + s * 128: c * CH + (s + 1) * 128, :],
                            in_=xp_t[s % 2][:]), smp["d_xsw"][oi % 2], 16)
                        xsw_snap[oi] = (smp["d_xsw"][0].n, smp["d_xsw"][1].n)
                    apply2_v[c] = sm["a_gp"].n

                aggr2_v = {}

                def emit_cd1_apply1(c):
                    b = c % 2
                    if c >= 2:
                        P.wait("vector", sm["a_se"], apply1_v.get(c - 2, 0))
                    P.wait("vector", sm["a_se"], sq1_v[c])
                    vv = P.group("vector", lambda e, b=b: e.reciprocal(
                        rsA[b][:, 0:NSUB], sqA[b][:, 0:NSUB]), sm["a_dve"])
                    P.wait("vector", sm["a_dve"], vv)
                    v_ng1 = P.group("vector", lambda e, b=b: e.scalar_tensor_tensor(
                        out=nmA2[b][:, 0:NSUB], in0=mvA[b][:, :, 0], scalar=-1.0,
                        in1=rsA[b][:, 0:NSUB], op0=ALU.mult, op1=ALU.mult),
                        sm["a_dve"])
                    P.wait("scalar", sm["a_dve"], v_ng1)
                    if c > 0:
                        P.wait("scalar", sm["a_pe"], txn_done[c - 1])
                    xb = x_buf[b]

                    def ln1_apply(eng, xb=xb, b=b):
                        last = None
                        for s in range(NSUB):
                            last = eng.activation(out=xn[:, s, :], in_=xb[:, s, :],
                                                  func=AF.Identity,
                                                  scale=rsA[b][:, s:s + 1],
                                                  bias=nmA2[b][:, s:s + 1])
                        return last
                    apply1_v[c] = P.group("scalar", ln1_apply, sm["a_se"])

                emit_ln1_AB(0)
                emit_cd1_apply1(0)

                for c in range(NCH):
                    b = c % 2
                    xb = x_buf[b]

                    P.wait("tensor", sm["a_se"], apply1_v[c])
                    ps_t4 = (ps_t0, ps_t1, ps_y0, ps_y1)
                    for kk in range(KD):
                        ps = ps_t4[kk % 4]
                        P.pe_wait_psum(ps)

                        def t_xn(eng, kk=kk, ps=ps):
                            last = None
                            for s in range(NSUB):
                                last = eng.matmul(ps[:, s * 128:(s + 1) * 128],
                                                  xn[:, s, kk * 128:(kk + 1) * 128],
                                                  idb[:], start=(s == 0),
                                                  stop=(s == NSUB - 1),
                                                  skip_group_check=True)
                            return last
                        v_pe = P.group("tensor", t_xn, sm["a_pe"])
                        ev_eng, ev_sem = (("vector", sm["a_dve"]) if kk % 2 == 0
                                          else ("scalar", sm["a_se"]))
                        P.wait(ev_eng, sm["a_pe"], v_pe)
                        if c > 0:
                            P.wait(ev_eng, sm["a_pe"], mm_done[c - 1])

                        def ev_xn(eng, kk=kk, ps=ps):
                            if hasattr(eng, "tensor_copy"):
                                return eng.tensor_copy(out=xnT[:, kk, :], in_=ps[:])
                            return eng.copy(out=xnT[:, kk, :], in_=ps[:])
                        v_ev = P.group(ev_eng, ev_xn, ev_sem)
                        P.psum_freed(ps, ev_sem, v_ev)
                        if kk % 2 == 0:
                            v_xnT_d = v_ev
                        else:
                            v_xnT_s = v_ev
                    txn_done[c] = sm["a_pe"].n

                    P.wait("tensor", sm["a_dve"], v_xnT_d)
                    P.wait("tensor", sm["a_se"], v_xnT_s)
                    P.pe_wait_psum(ps_s0)

                    def mm_s1(eng):
                        last = None
                        for kk in range(KD):
                            last = eng.matmul(ps_s0[0:64, :], wbr_b[:, kk, :],
                                              xnT[:, kk, :], start=(kk == 0),
                                              stop=(kk == KD - 1))
                        return last
                    v1 = P.group("tensor", mm_s1, sm["a_pe"])
                    P.wait("scalar", sm["a_pe"], v1)
                    v_s1 = P.group("scalar", lambda e: e.copy(out=s1_sb[:], in_=ps_s0[0:64, :]),
                                   sm["a_se"])
                    P.psum_freed(ps_s0, sm["a_se"], v_s1)

                    P.wait("tensor", sm["a_se"], v_s1)
                    P.pe_wait_psum(ps_s1)
                    P.pe_wait_psum(ps_s0)

                    def mm_s2(eng):
                        eng.matmul(ps_s1[:, 0:CH], wbs_b[:, 0:128], s1_sb[:],
                                   start=True, stop=True)
                        return eng.matmul(ps_s0[:, 0:CH], wbs_b[:, 128:256], s1_sb[:],
                                          start=True, stop=True)
                    v2 = P.group("tensor", mm_s2, sm["a_pe"])
                    P.wait("scalar", sm["a_pe"], v2)

                    def ev_s2(eng):
                        eng.copy(out=s2_sb[:, 0, :], in_=ps_s1[:])
                        return eng.copy(out=s2_sb[:, 1, :], in_=ps_s0[:])
                    v_s2 = P.group("scalar", ev_s2, sm["a_se"])
                    P.psum_freed(ps_s1, sm["a_se"], v_s2)
                    P.psum_freed(ps_s0, sm["a_se"], v_s2)

                    P.wait("tensor", sm["a_se"], v_s2)
                    P.pe_wait_psum(ps_s1)
                    P.pe_wait_psum(ps_s0)

                    def mm_s3(eng):
                        last = None
                        for m2 in range(2):
                            ps = (ps_s1, ps_s0)[m2]
                            for k2 in range(2):
                                last = eng.matmul(ps[:, 0:CH],
                                                  g_fin[:, k2, m2 * 128:(m2 + 1) * 128],
                                                  s2_sb[:, k2, :],
                                                  start=(k2 == 0), stop=(k2 == 1))
                        return last
                    v3 = P.group("tensor", mm_s3, sm["a_pe"])
                    P.wait("scalar", sm["a_pe"], v3)

                    def ev_s3(eng):
                        eng.copy(out=s3_sb[:, 0, :], in_=ps_s1[:])
                        return eng.copy(out=s3_sb[:, 1, :], in_=ps_s0[:])
                    v_s3 = P.group("scalar", ev_s3, sm["a_se"])
                    P.psum_freed(ps_s1, sm["a_se"], v_s3)
                    P.psum_freed(ps_s0, sm["a_se"], v_s3)

                    P.wait("tensor", sm["a_se"], v_s3)
                    P.pe_wait_psum(ps_s1)

                    def mm_s4(eng):
                        last = None
                        for k2 in range(2):
                            last = eng.matmul(ps_s1[0:64, :], wcr_b[:, k2, :],
                                              s3_sb[:, k2, :],
                                              start=(k2 == 0), stop=(k2 == 1))
                        return last
                    v4 = P.group("tensor", mm_s4, sm["a_pe"])
                    P.wait("scalar", sm["a_pe"], v4)
                    v_s4 = P.group("scalar", lambda e: e.copy(out=s4_sb[:], in_=ps_s1[0:64, :]),
                                   sm["a_se"])
                    P.psum_freed(ps_s1, sm["a_se"], v_s4)

                    P.wait("tensor", sm["a_se"], v_s4)
                    for kk in range(KD):
                        ps = (ps_y0, ps_y1)[kk % 2]
                        P.pe_wait_psum(ps)

                        def mm_y(eng, kk=kk, ps=ps):
                            eng.matmul(ps[:, 0:CH], wcm_b[:, kk * 128:(kk + 1) * 128],
                                       s4_sb[:], start=True, stop=False)
                            return eng.matmul(ps[:, 0:CH], diag_d[:, kk, :],
                                              xnT[:, kk, :], start=False, stop=True)
                        v_pe = P.group("tensor", mm_y, sm["a_pe"])
                        ev_eng, ev_sem = (("vector", sm["a_dve"]) if kk % 2 == 0
                                          else ("scalar", sm["a_se"]))
                        P.wait(ev_eng, sm["a_pe"], v_pe)
                        if c > 0:
                            P.wait(ev_eng, sm["a_pe"], ty_done[c - 1])

                        def ev_y(eng, kk=kk, ps=ps):
                            if hasattr(eng, "tensor_copy"):
                                return eng.tensor_copy(out=yf[:, kk, :], in_=ps[:, 0:CH])
                            return eng.copy(out=yf[:, kk, :], in_=ps[:, 0:CH])
                        v_ev = P.group(ev_eng, ev_y, ev_sem)
                        P.psum_freed(ps, ev_sem, v_ev)
                        if kk % 2 == 0:
                            v_yf_d = v_ev
                        else:
                            v_yf_s = v_ev
                    mm_done[c] = sm["a_pe"].n



                    if c >= 1:
                        emit_epilogue(c - 1)
                    if c + 1 < NCH:
                        emit_ln1_AB(c + 1)

                    P.wait("tensor", sm["a_dve"], v_yf_d)
                    P.wait("tensor", sm["a_se"], v_yf_s)
                    for s in range(NSUB):
                        for kg in range(4):
                            ps = ps_t4[(s * 4 + kg) % 4]
                            P.pe_wait_psum(ps)

                            def t_y(eng, s=s, kg=kg, ps=ps):
                                last = None
                                for k2 in range(4):
                                    kk = kg * 4 + k2
                                    last = eng.matmul(ps[:, k2 * 128:(k2 + 1) * 128],
                                                      yf[:, kk, s * 128:(s + 1) * 128],
                                                      idb[:], start=(k2 == 0),
                                                      stop=(k2 == 3),
                                                      skip_group_check=True)
                                return last
                            v_pe = P.group("tensor", t_y, sm["a_pe"])
                            ev_eng, ev_sem = (("vector", sm["a_dve"]) if kg % 2 == 0
                                              else ("scalar", sm["a_se"]))
                            P.wait(ev_eng, sm["a_pe"], v_pe)
                            if c >= 2:
                                P.wait(ev_eng, sm["a_dve"], ytok_rd[c - 2])
                                P.wait(ev_eng, sm["a_gp"], apply2_v.get(c - 2, 0))

                            def ev_ty(eng, s=s, kg=kg, ps=ps, b=b):
                                o = y_tok[b][:, s, kg * 512:(kg + 1) * 512]
                                if hasattr(eng, "tensor_copy"):
                                    return eng.tensor_copy(out=o, in_=ps[:])
                                return eng.copy(out=o, in_=ps[:])
                            v_ev = P.group(ev_eng, ev_ty, ev_sem)
                            P.psum_freed(ps, ev_sem, v_ev)
                            if kg % 2 == 0:
                                v_ytok_d = v_ev
                            else:
                                v_ytok_s = v_ev
                    v_ytok = (v_ytok_d, v_ytok_s)
                    ty_done[c] = sm["a_pe"].n



                    if c + 1 < NCH:
                        emit_cd1_apply1(c + 1)

                    P.wait("vector", sm["a_dve"], v_ytok[0])
                    P.wait("vector", sm["a_se"], v_ytok[1])
                    if c >= 2:
                        P.wait("vector", sm["a_se"], sq2_v[c - 2])
                    aggr2_v[c] = ln_A([y_tok[b][:, s, :] for s in range(NSUB)],
                                      stB, mvB[b], sm["a_dve"])
                    ytok_rd[c] = sm["a_dve"].n

                emit_epilogue(NCH - 1)

                P.group("tensor", lambda e: e.nop(), sm["ph"])
                P.group("vector", lambda e: e.engine_nop(), sm["ph"])
                P.group("gpsimd", lambda e: e.tensor_copy(out=dummy1[:], in_=eps_t[:]), sm["ph"])
                v_attn_done = P.group("scalar", lambda e: e.copy(out=dummy1[:], in_=eps_t[:]), sm["ph"])

            setup_scope.close()

            with ExitStack() as s1_scope:
                sb1 = s1_scope.enter_context
                ffn_inT = sb1(nc.sbuf_tensor("ffn_inT", [128, KD, T], BF16))

                ps_w0 = sb1(nc.psum_tensor("ps_w0", [128, 512], F32))
                ps_w1 = sb1(nc.psum_tensor("ps_w1", [128, 512], F32))
                ps_w2 = sb1(nc.psum_tensor("ps_w2", [128, 512], F32))
                ps_w3 = sb1(nc.psum_tensor("ps_w3", [128, 512], F32))
                ps_w = (ps_w0, ps_w1, ps_w2, ps_w3)

                wgb = [sb1(nc.sbuf_tensor(f"wgb{i}", [128, D], BF16)) for i in range(2)]
                wub = [sb1(nc.sbuf_tensor(f"wub{i}", [128, D], BF16)) for i in range(2)]
                wdb = [sb1(nc.sbuf_tensor(f"wdb{i}", [128, 1024], BF16)) for i in range(3)]

                wload_v = {}
                wfree = {}

                def emit_wload(cc):
                    if cc in wload_v:
                        return
                    b = cc % 2
                    if cc < 2:
                        P.wait("gpsimd", sm["ph"], v_attn_done)
                    else:
                        P.wait("gpsimd", sm["w_pe"], wgT_made[cc - 2])
                    P.group("gpsimd", lambda e, cc=cc, b=b: e.dma_start(
                        out=wgb[b][:], in_=gw_in[cc * 128:(cc + 1) * 128, :]),
                        smp["d_w"][b], 16)
                    wload_v[cc] = P.group("gpsimd", lambda e, cc=cc, b=b: e.dma_start(
                        out=wub[b][:], in_=uw_in[cc * 128:(cc + 1) * 128, :]),
                        smp["d_w"][b], 16)

                wd_seq = [(m, g) for m in range(KD) for g in range(8)]
                wdload_v = {}
                wd_free = {}

                def emit_wdload(idx):
                    bw = idx % 3
                    if idx < 3:
                        P.wait("gpsimd", sm["ph"], v_attn_done)
                    else:
                        P.wait("gpsimd", sm["w_pe"], wd_T_done[idx - 3])
                    m, g = wd_seq[idx]
                    wdload_v[idx] = P.group("gpsimd", lambda e, m=m, g=g, bw=bw: e.dma_start(
                        out=wdb[bw][:], in_=dw_in[m * 128:(m + 1) * 128,
                                                  g * 1024:(g + 1) * 1024]),
                        smp["d_wdl"][bw], 16)

                emit_wload(0)
                emit_wload(1)
                emit_wdload(0)
                emit_wdload(1)
                emit_wdload(2)

                wgT = [sb1(nc.sbuf_tensor(f"wgT{i}", [128, KD, 128], BF16)) for i in range(2)]
                wuT = [sb1(nc.sbuf_tensor(f"wuT{i}", [128, KD, 128], BF16)) for i in range(2)]
                wdstg = sb1(nc.sbuf_tensor("wdstg", [128, 8, 128], BF16))
                sg_t = sb1(nc.sbuf_tensor("sg_t", [128, CH], BF16))
                h_sb = [sb1(nc.sbuf_tensor(f"h_sb{i}", [128, CH], BF16)) for i in range(2)]

                ps_g0 = sb1(nc.psum_tensor("ps_g0", [128, 512], F32))
                ps_g1 = sb1(nc.psum_tensor("ps_g1", [128, 512], F32))
                ps_u0 = sb1(nc.psum_tensor("ps_u0", [128, 512], F32))
                ps_u1 = sb1(nc.psum_tensor("ps_u1", [128, 512], F32))


                P.wait("vector", sm["cs_se"], v_scev)
                P.wait("scalar", sm["cs_dve"], v_bc)

                wgT_made = {}
                gup_done = {}
                wd_T_done = {}
                wdtw_v = {}
                hw_v = {}
                sg_rd = 0
                deq_v = {}
                wd_deq_v = {}

                def emit_wd_deq(idx):
                    if idx in wd_deq_v:
                        return
                    m, _g = wd_seq[idx]
                    bw = idx % 3
                    P.wait("scalar", smp["d_wdl"][bw], wdload_v[idx])
                    wd_deq_v[idx] = P.group("scalar", lambda e, m=m, bw=bw: e.activation(
                        out=wdb[bw][:], in_=wdb[bw][:], func=AF.Identity,
                        scale=scd[:, m:m + 1], bias=bcd[:, m:m + 1]),
                        sm["w_se"])
                    wd_free[idx] = wd_deq_v[idx]

                def emit_deq(cc):
                    b = cc % 2
                    P.wait("vector", smp["d_w"][b], wload_v[cc])
                    if cc >= 2:
                        P.wait("vector", sm["w_pe"], wgT_made[cc - 2])
                    v_dg = P.group("vector", lambda e, cc=cc, b=b: e.tensor_scalar(
                        out=wgb[b][:], in0=wgb[b][:], scalar1=-128.0,
                        scalar2=scg[:, cc:cc + 1], op0=ALU.add, op1=ALU.mult),
                        sm["w_dve"])

                    P.wait("scalar", smp["d_w"][b], wload_v[cc])
                    if cc >= 2:
                        P.wait("scalar", sm["w_pe"], wgT_made[cc - 2])
                    v_du = P.group("scalar", lambda e, cc=cc, b=b: e.activation(
                        out=wub[b][:], in_=wub[b][:], func=AF.Identity,
                        scale=scu[:, cc:cc + 1], bias=bcu[:, cc:cc + 1]),
                        sm["w_se"])
                    wfree[cc] = (v_dg, v_du)
                    deq_v[cc] = (v_dg, v_du)

                emit_deq(0)

                wT_ev_last = {}
                wTv = {}

                def emit_cc_prep(cc):
                    b = cc % 2
                    v_dg, v_du = deq_v[cc]

                    P.wait("tensor", sm["w_dve"], v_dg)
                    P.wait("tensor", sm["w_se"], v_du)
                    for w_idx, (wsrc, wdst) in enumerate(
                            ((wgb[b], wgT[b]), (wub[b], wuT[b]))):
                        for kg in range(4):
                            gi = w_idx * 4 + kg
                            ps = ps_w[gi % 4]
                            g = P.psum_guard.get(id(ps))
                            if g is not None:
                                P.wait("tensor", g[0], g[1])

                            def t_w(eng, wsrc=wsrc, kg=kg, ps=ps):
                                last = None
                                for k2 in range(4):
                                    kk = kg * 4 + k2
                                    last = eng.matmul(ps[:, k2 * 128:(k2 + 1) * 128],
                                                      wsrc[:, kk * 128:(kk + 1) * 128],
                                                      idb[:], start=(k2 == 0),
                                                      stop=(k2 == 3),
                                                      skip_group_check=True)
                                return last
                            v_pe = P.group("tensor", t_w, sm["w_pe"])
                            ev_eng, ev_sem = (("vector", sm["w_dve"]) if gi % 2 == 0
                                              else ("scalar", sm["w_se"]))
                            P.wait(ev_eng, sm["w_pe"], v_pe)
                            if cc >= 2:
                                P.wait(ev_eng, sm["w_pe"], gup_done.get(cc - 2, 0))

                            def ev_w(eng, wdst=wdst, kg=kg, ps=ps):
                                o = wdst[:].rearrange("p k q -> p (k q)")[
                                    :, kg * 512:(kg + 1) * 512]
                                if hasattr(eng, "tensor_copy"):
                                    return eng.tensor_copy(out=o, in_=ps[:])
                                return eng.copy(out=o, in_=ps[:])
                            v_ev = P.group(ev_eng, ev_w, ev_sem)
                            P.psum_freed(ps, ev_sem, v_ev)
                            wT_ev_last[ev_eng] = v_ev
                    wgT_made[cc] = sm["w_pe"].n

                    for rep in range(2):
                        idx = cc * 2 + rep
                        m, g_ = wd_seq[idx]
                        bw = idx % 3
                        if idx == 0:
                            emit_wd_deq(0)
                            emit_wd_deq(1)
                        if idx + 2 < 128:
                            emit_wd_deq(idx + 2)
                        v_dd = wd_deq_v[idx]

                        P.wait("tensor", sm["w_se"], v_dd)
                        for kg in range(2):
                            ps = ps_w[2 * rep + kg]
                            g2 = P.psum_guard.get(id(ps))
                            if g2 is not None:
                                P.wait("tensor", g2[0], g2[1])

                            def t_d(eng, kg=kg, ps=ps, bw=bw):
                                last = None
                                for k2 in range(4):
                                    kc = kg * 4 + k2
                                    last = eng.matmul(ps[:, k2 * 128:(k2 + 1) * 128],
                                                      wdb[bw][:, kc * 128:(kc + 1) * 128],
                                                      idb[:], start=(k2 == 0),
                                                      stop=(k2 == 3),
                                                      skip_group_check=True)
                                return last
                            v_pe = P.group("tensor", t_d, sm["w_pe"])
                            P.wait("vector", sm["w_pe"], v_pe)
                            if idx >= 1:
                                P.wait("vector", sm["d_wdtw"], wdtw_v[idx - 1])

                            def ev_d(eng, kg=kg, ps=ps):
                                last = None
                                for k2 in range(4):
                                    kc = kg * 4 + k2
                                    last = eng.tensor_copy(out=wdstg[:, kc, :],
                                                           in_=ps[:, k2 * 128:(k2 + 1) * 128])
                                return last
                            v_ev = P.group("vector", ev_d, sm["w_dve"])
                            P.psum_freed(ps, sm["w_dve"], v_ev)
                        wd_T_done[idx] = sm["w_pe"].n
                        v_evd = sm["w_dve"].n

                        P.wait("gpsimd", sm["w_dve"], v_evd)
                        wdtw_v[idx] = P.group("gpsimd", lambda e, m=m, g_=g_: e.dma_start(
                            out=wdt_dram[m, g_, :, :, :], in_=wdstg[:]),
                            sm["d_wdtw"], 16)
                        if idx + 3 < 128:
                            emit_wdload(idx + 3)

                    if cc + 1 < KI:
                        emit_deq(cc + 1)

                    wTv[cc] = (wT_ev_last.get("vector", 0),
                               wT_ev_last.get("scalar", 0))

                sg_rd_box = [0]

                def emit_gu_nb(cc, nb):
                    b = cc % 2
                    if nb == 0:
                        P.wait("tensor", sm["w_dve"], wTv[cc][0])
                        P.wait("tensor", sm["w_se"], wTv[cc][1])
                    if cc == 0:
                        jv = pfx_evac_at[min(4 * nb + 3, NPCH - 1)]
                        P.wait("tensor", sm["p_dve"], jv[0])
                        P.wait("tensor", sm["p_se"], jv[1])
                    psg = (ps_g0, ps_g1)[nb % 2]
                    psu = (ps_u0, ps_u1)[nb % 2]
                    for pp in (psg, psu):
                        g3 = P.psum_guard.get(id(pp))
                        if g3 is not None:
                            P.wait("tensor", g3[0], g3[1])

                    def mm_gu(eng, b=b, nb=nb, psg=psg, psu=psu):
                        for kk in range(KD):
                            eng.matmul(psg[:, 0:CH], wgT[b][:, kk, :],
                                       ffn_inT[:, kk, nb * CH:(nb + 1) * CH],
                                       start=(kk == 0), stop=(kk == KD - 1))
                        last = None
                        for kk in range(KD):
                            last = eng.matmul(psu[:, 0:CH], wuT[b][:, kk, :],
                                              ffn_inT[:, kk, nb * CH:(nb + 1) * CH],
                                              start=(kk == 0), stop=(kk == KD - 1))
                        return last
                    v_gu = P.group("tensor", mm_gu, sm["w_pe"])

                    P.wait("scalar", sm["w_pe"], v_gu)
                    if sg_rd_box[0]:
                        P.wait("scalar", sm["f_dve"], sg_rd_box[0])
                    v_sg = P.group("scalar", lambda e, psg=psg: e.activation(
                        out=sg_t[:], in_=psg[:, 0:CH], func=AF.Silu),
                        sm["f_se"])
                    P.psum_freed(psg, sm["f_se"], v_sg)

                    hidx = cc * NCH + nb
                    hb = hidx % 2
                    P.wait("vector", sm["f_se"], v_sg)
                    if hidx >= 2:
                        P.wait("vector", smp["d_hw"][hidx % 2], hw_v[hidx - 2])
                    v_h = P.group("vector", lambda e, psu=psu, hb=hb: e.tensor_tensor(
                        out=h_sb[hb][:], in0=sg_t[:], in1=psu[:, 0:CH],
                        op=ALU.mult), sm["f_dve"])
                    P.psum_freed(psu, sm["f_dve"], v_h)
                    sg_rd_box[0] = v_h

                    P.wait("gpsimd", sm["f_dve"], v_h)
                    hw_v[hidx] = P.group("gpsimd", lambda e, cc=cc, nb=nb, hb=hb: e.dma_start(
                        out=h_dram[cc, nb, :, :], in_=h_sb[hb][:]),
                        smp["d_hw"][hidx % 2], 16)

                def emit_cc_finish(cc):
                    gup_done[cc] = sm["w_pe"].n
                    if cc + 2 < KI:
                        emit_wload(cc + 2)

                with ExitStack() as pfx_scope:
                    pfx = pfx_scope.enter_context
                    pxp = [pfx(nc.sbuf_tensor(f"pxp{i}", [128, D], BF16))
                           for i in range(4)]
                    pfin = [pfx(nc.sbuf_tensor(f"pfin{i}", [128, D], BF16))
                            for i in range(2)]
                    pst = [pfx(nc.sbuf_tensor(f"pst{i}", [128, 4, 6], F32))
                           for i in range(2)]
                    pmv = [pfx(nc.sbuf_tensor(f"pmv{i}", [128, 2], F32))
                           for i in range(2)]
                    psq = [pfx(nc.sbuf_tensor(f"psq{i}", [128, 1], F32))
                           for i in range(2)]
                    prs = [pfx(nc.sbuf_tensor(f"prs{i}", [128, 1], F32))
                           for i in range(2)]
                    pnm = [pfx(nc.sbuf_tensor(f"pnm{i}", [128, 1], F32))
                           for i in range(2)]

                    NPCH = 32
                    pxl_v = {}
                    pcons = {}
                    pfin_rd = {}
                    a_v = {}
                    b_v = {}
                    c_v = {}
                    d_v = {}

                    def p_load(j):
                        bx = j % 4
                        if j < 4:
                            P.wait("sync", sm["ph"], v_attn_done)
                        else:
                            P.wait("sync", sm["p_se"], pcons[j - 4])
                        s0, s1 = xsw_snap[j]
                        P.wait("sync", smp["d_xsw"][0], s0)
                        P.wait("sync", smp["d_xsw"][1], s1)
                        pxl_v[j] = P.group("sync", lambda e, j=j, bx=bx: e.dma_start(
                            out=pxp[bx][:],
                            in_=xs_dram[j * 128:(j + 1) * 128, :]), smp["d_xsr4"][bx], 16)

                    def p_A(j):
                        b = j % 2
                        bx = j % 4
                        P.wait("vector", smp["d_xsr4"][bx], pxl_v[j])
                        if j == 0:
                            P.wait("vector", sm["ph"], v_attn_done)

                        def stats(eng, b=b, bx=bx):
                            last = None
                            for a in range(4):
                                last = eng.bn_stats(pst[b][:, a, :],
                                                    pxp[bx][:, a * 512:(a + 1) * 512])
                            return last
                        vv = P.group("vector", stats, sm["p_dve"])
                        P.wait("vector", sm["p_dve"], vv)
                        a_v[j] = P.group("vector", lambda e, b=b: e.bn_aggr(
                            pmv[b][:], pst[b][:].rearrange("p a b -> p (a b)")),
                            sm["p_dve"])

                    def p_B(j):
                        b = j % 2
                        P.wait("scalar", sm["p_dve"], a_v[j])
                        if j == 0:
                            P.wait("scalar", sm["ph"], v_attn_done)
                        b_v[j] = P.group("scalar", lambda e, b=b: e.activation(
                            out=psq[b][:], in_=pmv[b][:, 1:2], func=AF.Sqrt,
                            bias=eps_t[:]), sm["p_se"])

                    def p_C(j):
                        b = j % 2
                        P.wait("vector", sm["p_se"], b_v[j])
                        if j > 1:
                            P.wait("vector", sm["p_se"], d_v[j - 2])
                        vv = P.group("vector", lambda e, b=b: e.reciprocal(
                            prs[b][:], psq[b][:]), sm["p_dve"])
                        P.wait("vector", sm["p_dve"], vv)
                        c_v[j] = P.group("vector", lambda e, b=b: e.scalar_tensor_tensor(
                            out=pnm[b][:], in0=pmv[b][:, 0:1], scalar=-1.0,
                            in1=prs[b][:], op0=ALU.mult, op1=ALU.mult),
                            sm["p_dve"])

                    def p_D(j):
                        b = j % 2
                        bx = j % 4
                        P.wait("scalar", sm["p_dve"], c_v[j])
                        if j > 1:
                            P.wait("scalar", sm["p_pe"], pfin_rd[j - 2])
                        d_v[j] = P.group("scalar", lambda e, b=b, bx=bx: e.activation(
                            out=pfin[b][:], in_=pxp[bx][:], func=AF.Identity,
                            scale=prs[b][:], bias=pnm[b][:]), sm["p_se"])
                        pcons[j] = d_v[j]

                    def p_E(j):
                        b = j % 2
                        P.wait("tensor", sm["p_se"], d_v[j])
                        if j == 0:
                            P.wait("tensor", sm["ph"], v_attn_done)
                        for kq in range(4):
                            ps = ps_w[kq % 4]
                            g = P.psum_guard.get(id(ps))
                            if g is not None:
                                P.wait("tensor", g[0], g[1])

                            def t_fin(eng, kq=kq, ps=ps, b=b):
                                last = None
                                for k2 in range(4):
                                    kk = kq * 4 + k2
                                    last = eng.matmul(ps[:, k2 * 128:(k2 + 1) * 128],
                                                      pfin[b][:, kk * 128:(kk + 1) * 128],
                                                      idb[:], start=(k2 == 0),
                                                      stop=(k2 == 3),
                                                      skip_group_check=True)
                                return last
                            v_pe = P.group("tensor", t_fin, sm["p_pe"])
                            ev_eng, ev_sem = (("vector", sm["p_dve"]) if kq % 2 == 0
                                              else ("scalar", sm["p_se"]))
                            P.wait(ev_eng, sm["p_pe"], v_pe)

                            def ev_fin(eng, kq=kq, ps=ps, j=j):
                                o = ffn_inT[:, kq * 4:(kq + 1) * 4,
                                            j * 128:(j + 1) * 128]
                                i_ = ps[:, 0:512].rearrange("p (k q) -> p k q", k=4)
                                if hasattr(eng, "tensor_copy"):
                                    return eng.tensor_copy(out=o, in_=i_)
                                return eng.copy(out=o, in_=i_)
                            v_ev = P.group(ev_eng, ev_fin, ev_sem)
                            P.psum_freed(ps, ev_sem, v_ev)
                        pfin_rd[j] = sm["p_pe"].n

                    pfx_evac_at = {}
                    for jj in range(4):
                        p_load(jj)
                    for j in range(NPCH):
                        p_A(j)
                        p_B(j)
                        if j >= 1:
                            p_C(j - 1)
                            p_D(j - 1)
                            if j + 3 < NPCH:
                                p_load(j + 3)
                        if j >= 2:
                            p_E(j - 2)
                            pfx_evac_at[j - 2] = (sm["p_dve"].n, sm["p_se"].n)
                        if j == 1:
                            emit_cc_prep(0)
                        if j == 3:
                            emit_wload(2)
                            emit_cc_prep(1)
                        if j >= 5 and (j - 5) % 4 == 0 and (j - 5) // 4 <= 6:
                            emit_gu_nb(0, (j - 5) // 4)
                    p_E(NPCH - 2)
                    pfx_evac_at[NPCH - 2] = (sm["p_dve"].n, sm["p_se"].n)
                    p_C(NPCH - 1)
                    p_D(NPCH - 1)
                    p_E(NPCH - 1)
                    pfx_evac_at[NPCH - 1] = (sm["p_dve"].n, sm["p_se"].n)
                    v_pfx_dve = sm["p_dve"].n
                    v_pfx_se = sm["p_se"].n
                    v_pfx_pe = sm["p_pe"].n

                emit_gu_nb(0, 7)
                emit_cc_finish(0)
                for cc in range(1, KI):
                    if cc >= 2:
                        emit_cc_prep(cc)
                    for nb in range(NCH):
                        emit_gu_nb(cc, nb)
                    emit_cc_finish(cc)

                v_s1_hw = (smp["d_hw"][0].n, smp["d_hw"][1].n)
                v_s1_wdtw = sm["d_wdtw"].n
                P.group("tensor", lambda e: e.nop(), sm["ph"])
                P.group("vector", lambda e: e.engine_nop(), sm["ph"])
                v_s1_done = P.group("scalar", lambda e: e.copy(out=dummy1[:], in_=eps_t[:]), sm["ph"])

            with ExitStack() as s2_scope:
                sb2 = s2_scope.enter_context
                h_ts = [sb2(nc.sbuf_tensor(f"h_ts{i}", [128, KI, CH], BF16))
                        for i in range(2)]
                wdt_st = [sb2(nc.sbuf_tensor(f"wdt_st{i}", [128, 4, 8, 128], BF16))
                          for i in range(3)]
                xr_t = [sb2(nc.sbuf_tensor(f"xr_t{i}", [128, D], BF16))
                        for i in range(2)]
                fo_f = [sb2(nc.sbuf_tensor(f"fo_f{i}", [128, CH], BF16))
                        for i in range(2)]
                fo_tok = sb2(nc.sbuf_tensor("fo_tok", [128, NSUB, D], BF16))
                ln4_t = [sb2(nc.sbuf_tensor(f"ln4_t{i}", [128, D], BF16))
                         for i in range(2)]
                o_t = [sb2(nc.sbuf_tensor(f"o_t{i}", [128, D], F32)) for i in range(2)]
                st2 = sb2(nc.sbuf_tensor("st2", [128, NSUB, 4, 6], F32))
                mv2 = sb2(nc.sbuf_tensor("mv2", [128, NSUB, 2], F32))
                sq2 = sb2(nc.sbuf_tensor("sq2", [128, NSUB], F32))
                rs2 = sb2(nc.sbuf_tensor("rs2", [128, NSUB], F32))
                nm2 = sb2(nc.sbuf_tensor("nm2", [128, NSUB], F32))

                ps_d0 = sb2(nc.psum_tensor("ps_d0", [128, 512], F32))
                ps_d1 = sb2(nc.psum_tensor("ps_d1", [128, 512], F32))
                ps_T0 = sb2(nc.psum_tensor("ps_T0", [128, 512], F32))
                ps_T1 = sb2(nc.psum_tensor("ps_T1", [128, 512], F32))

                hload_v = {}
                hts_cons = {}

                def emit_hload(ts):
                    bt = ts % 2
                    if ts < 2:
                        P.wait("gpsimd", smp["d_hw"][0], v_s1_hw[0])
                        P.wait("gpsimd", smp["d_hw"][1], v_s1_hw[1])
                        P.wait("gpsimd", sm["ph"], v_s1_done)
                    else:
                        P.wait("gpsimd", sm["f_pe"], hts_cons[ts - 2])

                    def f(eng, ts=ts, bt=bt):
                        last = None
                        for gg in range(8):
                            last = eng.dma_start(
                                out=h_ts[bt][:, gg * 8:(gg + 1) * 8, :],
                                in_=h_dram[gg * 8:(gg + 1) * 8, ts, :, :].rearrange(
                                    "c p t -> p c t"))
                            last.then_inc(smp["d_hr"][bt].h, 16)
                    P.on("gpsimd", f)
                    hload_v[ts] = smp["d_hr"][bt].inc(16 * 8)

                wdt_v = {}
                wdt_cons = {}

                def emit_wdtload(widx):
                    bm = widx % 3
                    qeng = "sync"
                    ts, rem = divmod(widx, KD * 2)
                    m, half = divmod(rem, 2)
                    if widx < 3:
                        P.wait(qeng, sm["d_wdtw"], v_s1_wdtw)
                        P.wait(qeng, sm["ph"], v_s1_done)
                    else:
                        P.wait(qeng, sm["f_pe"], wdt_cons[widx - 3])
                    wdt_v[widx] = P.group(qeng, lambda e, m=m, half=half, bm=bm: e.dma_start(
                        out=wdt_st[bm][:],
                        in_=wdt_dram[m, half * 4:(half + 1) * 4, :, :, :].rearrange(
                            "g p k h -> p g k h")), smp["d_wdtr3"][bm], 16)

                xr_v = {}
                xr_cons = {}

                def emit_xr(ts, s):
                    oi = ts * NSUB + s
                    if oi == 0:
                        P.wait("scalar", sm["ph"], v_s1_done)
                    if oi >= 2:
                        P.wait("scalar", sm["f_dve"], xr_cons[oi - 2])
                    xr_v[oi] = P.group("scalar", lambda e, ts=ts, s=s: e.dma_start(
                        out=xr_t[s % 2][:],
                        in_=xs_dram[ts * CH + s * 128: ts * CH + (s + 1) * 128, :]),
                        smp["d_xsr"][s % 2], 16)

                emit_hload(0)
                emit_hload(1)
                emit_wdtload(0)
                emit_wdtload(1)
                emit_wdtload(2)

                out_v = {}
                oadd_v = {}
                fo_rd = {}
                fotok_rd = {}
                pending_tdn = [None]

                def emit_tdn(ts, m, psd):
                    idxm = ts * KD + m
                    bf = m % 2
                    P.wait("scalar", sm["f_pe"], fo_rd[idxm])
                    if idxm > 1:
                        P.wait("scalar", sm["f_pe"], tdn_pe_done[idxm - 2])
                    v_fo = P.group("scalar", lambda e, psd=psd, bf=bf: e.copy(
                        out=fo_f[bf][:], in_=psd[:, 0:CH]), sm["f_se"])
                    P.psum_freed(psd, sm["f_se"], v_fo)

                    P.wait("tensor", sm["f_se"], v_fo)
                    for sp in range(2):
                        ps = (ps_T0, ps_T1)[sp]
                        g5 = P.psum_guard.get(id(ps))
                        if g5 is not None:
                            P.wait("tensor", g5[0], g5[1])

                        def t_dn(eng, sp=sp, ps=ps, bf=bf):
                            last = None
                            for s2_ in range(2):
                                s = sp * 2 + s2_
                                last = eng.matmul(ps[:, s2_ * 128:(s2_ + 1) * 128],
                                                  fo_f[bf][:, s * 128:(s + 1) * 128],
                                                  idb[:], start=(s2_ == 0),
                                                  stop=(s2_ == 1),
                                                  skip_group_check=True)
                            return last
                        v_pe = P.group("tensor", t_dn, sm["f_pe"])
                        P.wait("vector", sm["f_pe"], v_pe)
                        if ts > 0:
                            P.wait("vector", sm["f_se"], fotok_rd[ts - 1])

                        def ev_tdn(eng, sp=sp, ps=ps, m=m):
                            last = None
                            for s2_ in range(2):
                                s = sp * 2 + s2_
                                last = eng.tensor_copy(
                                    out=fo_tok[:, s, m * 128:(m + 1) * 128],
                                    in_=ps[:, s2_ * 128:(s2_ + 1) * 128])
                            return last
                        v_ev = P.group("vector", ev_tdn, sm["f_dve"])
                        P.psum_freed(ps, sm["f_dve"], v_ev)
                    tdn_pe_done[idxm] = sm["f_pe"].n

                tdn_pe_done = {}

                for ts in range(NCH):
                    bt = ts % 2
                    P.wait("tensor", smp["d_hr"][ts % 2], hload_v[ts])
                    for m in range(KD):
                        psd = (ps_d0, ps_d1)[m % 2]
                        g4 = P.psum_guard.get(id(psd))
                        if g4 is not None:
                            P.wait("tensor", g4[0], g4[1])
                        for half in range(2):
                            widx = ts * KD * 2 + m * 2 + half
                            bm = widx % 3
                            P.wait("tensor", smp["d_wdtr3"][bm], wdt_v[widx])

                            def mm_dn(eng, bt=bt, bm=bm, psd=psd, half=half):
                                last = None
                                for kcl in range(32):
                                    kc = half * 32 + kcl
                                    last = eng.matmul(psd[:, 0:CH],
                                                      wdt_st[bm][:, kcl // 8, kcl % 8, :],
                                                      h_ts[bt][:, kc, :],
                                                      start=(kc == 0), stop=(kc == KI - 1))
                                return last
                            v_dn = P.group("tensor", mm_dn, sm["f_pe"])
                            wdt_cons[widx] = v_dn
                            if widx + 3 < NCH * KD * 2:
                                emit_wdtload(widx + 3)
                        if m == KD - 1:
                            hts_cons[ts] = v_dn
                        fo_rd[ts * KD + m] = v_dn

                        if pending_tdn[0] is not None:
                            emit_tdn(*pending_tdn[0])
                        pending_tdn[0] = (ts, m, psd)
                    emit_tdn(*pending_tdn[0])
                    pending_tdn[0] = None
                    v_fotok = sm["f_dve"].n

                    fo_subs = [fo_tok[:, s, :] for s in range(NSUB)]
                    P.wait("vector", sm["f_dve"], v_fotok)
                    v_l4 = emit_ln_generic(P, fo_subs, st2, mv2, sq2, rs2, nm2,
                                           sm["f_dve"], sm["f_se"], eps_t)

                    P.wait("scalar", sm["f_dve"], v_l4)
                    for s in range(NSUB):
                        oi = ts * NSUB + s
                        emit_xr(ts, s)
                        if oi >= 2:
                            P.wait("scalar", sm["f_dve"], oadd_v[oi - 2])
                        v_a4 = P.group("scalar", lambda e, s=s: e.activation(
                            out=ln4_t[s % 2][:], in_=fo_tok[:, s, :], func=AF.Identity,
                            scale=rs2[:, s:s + 1], bias=nm2[:, s:s + 1]),
                            sm["f_se"])

                        P.wait("vector", sm["f_se"], v_a4)
                        P.wait("vector", smp["d_xsr"][s % 2], xr_v[oi])
                        if oi >= 2:
                            P.wait("vector", smp["d_out"][oi % 2], out_v[oi - 2])
                        v_oa = P.group("vector", lambda e, s=s: e.tensor_tensor(
                            out=o_t[s % 2][:], in0=xr_t[s % 2][:], in1=ln4_t[s % 2][:],
                            op=ALU.add), sm["f_dve"])
                        oadd_v[oi] = v_oa
                        xr_cons[oi] = v_oa

                        P.wait("gpsimd", sm["f_dve"], v_oa)
                        out_v[oi] = P.group("gpsimd", lambda e, ts=ts, s=s: e.dma_start(
                            out=out_t[ts * CH + s * 128: ts * CH + (s + 1) * 128, :],
                            in_=o_t[s % 2][:]), smp["d_out"][oi % 2], 16)
                    fotok_rd[ts] = sm["f_se"].n
                    if ts + 2 < NCH:
                        emit_hload(ts + 2)

                P.wait("gpsimd", smp["d_out"][0], smp["d_out"][0].n)
                P.wait("gpsimd", smp["d_out"][1], smp["d_out"][1].n)

        with nc.Block() as block:
            @block.sync
            def _(sync):
                for f in P.streams["sync"]:
                    f(sync)

            @block.tensor
            def _(tensor):
                for f in P.streams["tensor"]:
                    f(tensor)

            @block.vector
            def _(vector):
                for f in P.streams["vector"]:
                    f(vector)

            @block.scalar
            def _(scalar):
                for f in P.streams["scalar"]:
                    f(scalar)

            @block.gpsimd
            def _(gpsimd):
                for f in P.streams["gpsimd"]:
                    f(gpsimd)

    return nc


def emit_ln_generic(P, in_subs, st, mv, sq, rs, nm, sd, ss, eps_t):
    nsub = len(in_subs)

    def stats(eng):
        last = None
        for s in range(nsub):
            for a in range(4):
                last = eng.bn_stats(st[:, s, a, :],
                                    in_subs[s][:, a * 512:(a + 1) * 512])
        return last
    vv = P.group("vector", stats, sd)
    P.wait("vector", sd, vv)

    def aggr(eng):
        last = None
        for s in range(nsub):
            last = eng.bn_aggr(mv[:, s, :],
                               st[:, s, :, :].rearrange("p a b -> p (a b)"))
        return last
    v_ag = P.group("vector", aggr, sd)

    P.wait("scalar", sd, v_ag)
    v_sq = P.group("scalar", lambda e: e.activation(
        out=sq[:, 0:nsub], in_=mv[:, :, 1], func=AF.Sqrt, bias=eps_t[:]), ss)

    P.wait("vector", ss, v_sq)
    vv = P.group("vector", lambda e: e.reciprocal(rs[:, 0:nsub], sq[:, 0:nsub]), sd)
    P.wait("vector", sd, vv)
    v_nm = P.group("vector", lambda e: e.scalar_tensor_tensor(
        out=nm[:, 0:nsub], in0=mv[:, :, 0], scalar=-1.0, in1=rs[:, 0:nsub],
        op0=ALU.mult, op1=ALU.mult), sd)
    return v_nm


def kernel(L0_attn_u, L0_attn_A_factors, L0_attn_W_b_rank, L0_attn_W_b_state,
           L0_attn_W_c_rank, L0_attn_W_c_model, L0_attn_D_vec,
           L0_ffn_gate_w, L0_ffn_gate_scale, L0_ffn_up_w, L0_ffn_up_scale,
           L0_ffn_down_w, L0_ffn_down_scale):
    x = np.ascontiguousarray(np.asarray(L0_attn_u), dtype=np.float32).reshape(NCORES * T, D)
    wbr = np.ascontiguousarray(
        np.asarray(L0_attn_W_b_rank, dtype=np.float32).reshape(KD, 128, 64)
        .transpose(1, 0, 2).reshape(128, KD * 64))
    wbs = np.ascontiguousarray(np.asarray(L0_attn_W_b_state, dtype=np.float32))
    wcr = np.ascontiguousarray(
        np.asarray(L0_attn_W_c_rank, dtype=np.float32).reshape(2, 128, 64)
        .transpose(1, 0, 2).reshape(128, 2 * 64))
    wcm = np.ascontiguousarray(np.asarray(L0_attn_W_c_model, dtype=np.float32))
    dvec = np.ascontiguousarray(np.asarray(L0_attn_D_vec, dtype=np.float32).reshape(KD, 128))
    afac = np.ascontiguousarray(np.asarray(L0_attn_A_factors, dtype=np.float32).reshape(1, 32))
    gw = np.ascontiguousarray(np.asarray(L0_ffn_gate_w, dtype=np.int32))
    uw = np.ascontiguousarray(np.asarray(L0_ffn_up_w, dtype=np.int32))
    dw = np.ascontiguousarray(np.asarray(L0_ffn_down_w, dtype=np.int32))
    gsc = np.ascontiguousarray(np.asarray(L0_ffn_gate_scale, dtype=np.float32).reshape(KI, 128))
    usc = np.ascontiguousarray(np.asarray(L0_ffn_up_scale, dtype=np.float32).reshape(KI, 128))
    dsc = np.ascontiguousarray(np.asarray(L0_ffn_down_scale, dtype=np.float32).reshape(KD, 128))
    idb = np.eye(128, dtype=ml_dtypes.bfloat16)
    idf = np.eye(128, dtype=np.float32)

    nc = build_nc()
    in_maps = []
    for i in range(NCORES):
        in_maps.append(dict(
            x=np.ascontiguousarray(x[i * T:(i + 1) * T]),
            wbr=wbr, wbs=wbs, wcr=wcr, wcm=wcm, dvec=dvec, afac=afac,
            gw=gw, uw=uw, dw=dw, gsc=gsc, usc=usc, dsc=dsc,
            idb=idb, idf=idf,
        ))
    res = run_bass_kernel_spmd(nc, in_maps, core_ids=list(range(NCORES)))
    out = np.concatenate([np.asarray(r["out"]) for r in res.results], axis=0)
    return out.reshape(2, 16384, D).astype(np.float32)
